# version 11
# baseline (speedup 1.0000x reference)
import math
import numpy as np
import ml_dtypes
import concourse.bass as bass
import concourse.mybir as mybir
from concourse.bass_utils import run_bass_kernel_spmd

F32 = mybir.dt.float32
BF16 = mybir.dt.bfloat16
AF = mybir.ActivationFunctionType
ALU = mybir.AluOpType
AX = mybir.AxisListType

BN, SEQ, D, H, HD = 2, 8192, 1024, 16, 64
TOK = 2048
NC = 8
NHP = 8
QC = 512
LN_EPS = 1e-5
FFN_LN_EPS = 1e-5
ABSMEAN_EPS = 1e-6
BIG = 1e9
RG = [[0, 1, 2, 3], [4, 5, 6, 7]]

ENGINES = ("sync", "gpsimd", "tensor", "vector", "scalar")
DMA_QUEUES = ("sync", "gpsimd")


class Buf:
    __slots__ = ("name", "last_write", "readers")

    def __init__(self, name=""):
        self.name = name
        self.last_write = None
        self.readers = []


class Prog:

    def __init__(self, nc, trust_engines=("tensor",)):
        self.nc = nc
        self.steps = {e: [] for e in ENGINES}
        self._consts = {}
        self._ctx = None
        self.trust_engines = set(trust_engines)
        self._phase_deps = None
        self._phase_seen = set()

    def buf(self, name=""):
        return Buf(name)

    def constf(self, value):
        key = float(value)
        if key not in self._consts:
            t = self._ctx.enter_context(
                self.nc.sbuf_tensor(f"c_f{len(self._consts)}", [1, 1], F32))
            b = Buf(f"constf_{key}")
            self.add("vector", lambda e: e.memset(t[:], key), writes=[b])
            self._consts[key] = (t, b)
        return self._consts[key]

    def begin_phase(self):
        self._phase_deps = [(e, len(self.steps[e]) - 1)
                            for e in ENGINES if self.steps[e]]
        self._phase_seen = set()

    def add(self, eng, fn, reads=(), writes=(), deps=(), dma=False):
        import traceback
        tag = traceback.extract_stack(limit=3)[0].lineno
        tok = (eng, len(self.steps[eng]))
        alldeps = list(deps)
        if self._phase_deps is not None and eng not in self._phase_seen:
            alldeps.extend(t for t in self._phase_deps if t[0] != eng or t[1] >= 0)
            self._phase_seen.add(eng)
            if len(self._phase_seen) == len(ENGINES):
                self._phase_deps = None
        for b in reads:
            if b.last_write is not None:
                alldeps.append(b.last_write)
        for b in writes:
            if b.last_write is not None:
                alldeps.append(b.last_write)
            alldeps.extend(b.readers)
        self.steps[eng].append(dict(fn=fn, deps=alldeps, dma=dma, inc=0, tag=tag))
        for b in reads:
            b.readers.append(tok)
        for b in writes:
            b.last_write = tok
            b.readers = []
        return tok

    def dma(self, q, out, in_, reads=(), writes=(), deps=()):
        return self.add(q, lambda e: e.dma_start(out=out, in_=in_),
                        reads=reads, writes=writes, deps=deps, dma=True)

    def emit(self, ctx):
        nc = self.nc
        sems = {e: ctx.enter_context(nc.semaphore("sem_" + e)) for e in ENGINES}
        for e in ENGINES:
            for s in self.steps[e]:
                if s["dma"]:
                    s["inc"] = 16
        for e in ENGINES:
            for i, s in enumerate(self.steps[e]):
                for (de, di) in s["deps"]:
                    if de == e and de in self.trust_engines:
                        continue
                    if de == e and di >= i:
                        raise RuntimeError(f"bad same-queue dep {de}:{di} >= {i}")
                    tgt = self.steps[de][di]
                    if tgt["inc"] == 0:
                        tgt["inc"] = 1
        cum = {}
        for e in ENGINES:
            c = 0
            cc = []
            for s in self.steps[e]:
                c += s["inc"]
                cc.append(c)
            cum[e] = cc
        blk = ctx.enter_context(nc.Block())

        no_dedup = True

        def emitter_for(ename):
            def emitter(eng):
                waited = {}
                for i, s in enumerate(self.steps[ename]):
                    waitv = {}
                    for (de, di) in s["deps"]:
                        if de == ename and de in self.trust_engines:
                            continue
                        if de == ename and di >= i:
                            continue
                        v = cum[de][di]
                        if v > waitv.get(de, 0):
                            waitv[de] = v
                    for de, v in sorted(waitv.items()):
                        if no_dedup or v > waited.get(de, 0):
                            eng.wait_ge(sems[de], v)
                            waited[de] = max(v, waited.get(de, 0))
                    try:
                        inst = s["fn"](eng)
                    except Exception:
                        import sys
                        print(f"EMIT FAIL {ename} step {i} tag={s.get('tag')}",
                              file=sys.stderr)
                        raise
                    if s["inc"]:
                        inst.then_inc(sems[ename], s["inc"])
                if ename in DMA_QUEUES and cum[ename] and cum[ename][-1] > 0:
                    eng.wait_ge(sems[ename], cum[ename][-1])
            return emitter

        blk.sync(emitter_for("sync"))
        blk.tensor(emitter_for("tensor"))
        blk.vector(emitter_for("vector"))
        blk.scalar(emitter_for("scalar"))
        blk.gpsimd(emitter_for("gpsimd"))



def build_consts(nc, p, ctx):
    C = {}

    def sb(name, shape, dtype):
        h = ctx.enter_context(nc.sbuf_tensor(name, shape, dtype))
        b = Buf(name)
        return h, b

    ones_col, b_ones_col = sb("c_ones_col", [128, 1], BF16)
    ones_row, b_ones_row = sb("c_ones_row", [1, QC], BF16)
    onesf_col, b_onesf_col = sb("c_onesf_col", [1, 128], F32)
    onesf_colb, b_onesf_colb = sb("c_onesf_colb", [1, 128], BF16)
    onesf_c128, b_onesf_c128 = sb("c_onesf_c128", [128, 8], F32)
    p.add("vector", lambda e: e.memset(ones_col[:], 1.0), writes=[b_ones_col])
    p.add("vector", lambda e: e.memset(ones_row[:], 1.0), writes=[b_ones_row])
    p.add("vector", lambda e: e.memset(onesf_col[:], 1.0), writes=[b_onesf_col])
    p.add("vector", lambda e: e.memset(onesf_colb[:], 1.0), writes=[b_onesf_colb])
    p.add("vector", lambda e: e.memset(onesf_c128[:], 1.0), writes=[b_onesf_c128])
    C["onesf_c128"] = (onesf_c128, b_onesf_c128)
    C["ones_col"] = (ones_col, b_ones_col)
    C["ones_row"] = (ones_row, b_ones_row)
    C["onesf_col"] = (onesf_col, b_onesf_col)
    C["onesf_colb"] = (onesf_colb, b_onesf_colb)
    return C


def _scr(nc, p, ctx, n):
    out = []
    for i in range(n):
        h = ctx.enter_context(nc.sbuf_tensor(f"scr{i}_{len(p.steps['vector'])}",
                                             [128, 128], F32))
        out.append((h, Buf(f"scr{i}")))
    return out


def add_rope_consts(nc, p, ctx, C):
    (u1, b_u1), (u2, b_u2), (u3, b_u3) = _scr(nc, p, ctx, 3)
    R = ctx.enter_context(nc.sbuf_tensor("c_R", [128, 128], BF16))
    b_R = Buf("c_R")
    p.add("gpsimd", lambda e: e.iota(u1[:], [[-1, 128]], base=0, channel_multiplier=1,
                                     allow_small_or_imprecise_dtypes=True), writes=[b_u1])
    p.add("gpsimd", lambda e: e.iota(u2[:].rearrange("p (a b) -> p a b", b=64),
                                     [[0, 2], [1, 64]], base=0, channel_multiplier=0,
                                     allow_small_or_imprecise_dtypes=True), writes=[b_u2])
    p.add("vector", lambda e: e.tensor_scalar(u3[:], u1[:], 32.0, None, ALU.is_equal),
          reads=[b_u1], writes=[b_u3])
    p.add("vector", lambda e: e.tensor_scalar(u2[:], u2[:], 32.0, None, ALU.is_lt),
          reads=[b_u2], writes=[b_u2])
    p.add("vector", lambda e: e.tensor_mul(u3[:], u3[:], u2[:]),
          reads=[b_u2, b_u3], writes=[b_u3])
    p.add("vector", lambda e: e.tensor_scalar(u1[:], u1[:], -32.0, None, ALU.is_equal),
          reads=[b_u1], writes=[b_u1])
    p.add("vector", lambda e: e.tensor_scalar(u2[:], u2[:], -1.0, 1.0, ALU.mult, ALU.add),
          reads=[b_u2], writes=[b_u2])
    p.add("vector", lambda e: e.tensor_mul(u1[:], u1[:], u2[:]),
          reads=[b_u1, b_u2], writes=[b_u1])
    p.add("vector", lambda e: e.tensor_sub(R[:], u1[:], u3[:]),
          reads=[b_u1, b_u3], writes=[b_R])
    C["R"] = (R, b_R)
    Dm = ctx.enter_context(nc.sbuf_tensor("c_D", [32, 128], F32))
    b_D = Buf("c_D")
    p.add("gpsimd", lambda e: e.iota(Dm[:].rearrange("p (a b) -> p a b", b=32),
                                     [[0, 4], [1, 32]], base=0, channel_multiplier=-1,
                                     allow_small_or_imprecise_dtypes=True), writes=[b_D])
    p.add("vector", lambda e: e.tensor_scalar(Dm[:], Dm[:], 0.0, None, ALU.is_equal),
          reads=[b_D], writes=[b_D])
    C["D"] = (Dm, b_D)


def add_E1_const(nc, p, ctx, C):
    (v1, b_v1), (v2, b_v2), (v3, b_v3), (v4, b_v4), (v5, b_v5) = _scr(nc, p, ctx, 5)
    E1 = ctx.enter_context(nc.sbuf_tensor("c_E1", [128, 128], F32))
    b_E1 = Buf("c_E1")
    p.add("gpsimd", lambda e: e.iota(v1[:], [[0, 128]], base=0, channel_multiplier=1,
                                     allow_small_or_imprecise_dtypes=True), writes=[b_v1])
    p.add("gpsimd", lambda e: e.iota(v2[:], [[1, 128]], base=0, channel_multiplier=0,
                                     allow_small_or_imprecise_dtypes=True), writes=[b_v2])
    p.add("vector", lambda e: e.tensor_scalar(v3[:], v1[:], 0.0, None, ALU.is_equal),
          reads=[b_v1], writes=[b_v3])
    p.add("vector", lambda e: e.tensor_scalar(v4[:], v2[:], 64.0, None, ALU.is_lt),
          reads=[b_v2], writes=[b_v4])
    p.add("vector", lambda e: e.tensor_mul(v3[:], v3[:], v4[:]),
          reads=[b_v4, b_v3], writes=[b_v3])
    p.add("vector", lambda e: e.tensor_scalar(v5[:], v1[:], 64.0, None, ALU.is_equal),
          reads=[b_v1], writes=[b_v5])
    p.add("vector", lambda e: e.tensor_scalar(v2[:], v4[:], -1.0, 1.0, ALU.mult, ALU.add),
          reads=[b_v4], writes=[b_v2])
    p.add("vector", lambda e: e.tensor_mul(v5[:], v5[:], v2[:]),
          reads=[b_v5, b_v2], writes=[b_v5])
    p.add("vector", lambda e: e.tensor_add(E1[:], v5[:], v3[:]),
          reads=[b_v5, b_v3], writes=[b_E1])
    C["E1"] = (E1, b_E1)


def add_moe_consts(nc, p, ctx, C):
    idf32 = ctx.enter_context(nc.sbuf_tensor("c_idf32", [128, 128], F32))
    b_id = Buf("c_idf32")
    p.add("gpsimd", lambda e: e.iota(idf32[:], [[-1, 128]], base=0, channel_multiplier=1,
                                     allow_small_or_imprecise_dtypes=True), writes=[b_id])
    p.add("vector", lambda e: e.tensor_scalar(idf32[:], idf32[:], 0.0, None, ALU.is_equal),
          reads=[b_id], writes=[b_id])
    C["idf32"] = (idf32, b_id)
    S8 = ctx.enter_context(nc.sbuf_tensor("c_sel8", [8, 8 * 128], BF16))
    b_S8 = Buf("c_sel8")
    p.add("gpsimd", lambda e: e.iota(S8[:].rearrange("p (a b) -> p a b", b=128),
                                     [[1, 8], [0, 128]], base=0, channel_multiplier=-1,
                                     allow_small_or_imprecise_dtypes=True), writes=[b_S8])
    p.add("vector", lambda e: e.tensor_scalar(S8[:], S8[:], 0.0, None, ALU.is_equal),
          reads=[b_S8], writes=[b_S8])
    C["sel8"] = (S8, b_S8)


def ln_stats_normalize(nc, p, ctx, C, sx, b_sx, lnx, b_lnx, tag, pstat, b_pstat,
                       pbc, b_pbc, eps=LN_EPS):
    onesf_col = C["onesf_col"][0]
    b_onesf = C["onesf_col"][1]
    ones_colb, b_ones_colb = C["ones_col"]

    xb = ctx.enter_context(nc.sbuf_tensor(f"{tag}_xb", [128, 2 * QC], BF16))
    xs = ctx.enter_context(nc.sbuf_tensor(f"{tag}_xs", [128, 2 * QC], BF16))
    b_xb, b_xs = p.buf(), p.buf()
    row = ctx.enter_context(nc.sbuf_tensor(f"{tag}_row", [1, QC], F32))
    row2 = ctx.enter_context(nc.sbuf_tensor(f"{tag}_row2", [1, QC], F32))
    rowm = ctx.enter_context(nc.sbuf_tensor(f"{tag}_rowm", [1, QC], F32))
    b_row, b_row2, b_rowm = p.buf(), p.buf(), p.buf()
    tmp = ctx.enter_context(nc.sbuf_tensor(f"{tag}_tmp", [128, QC], F32))
    b_tmp = p.buf()

    for qc in range(TOK // QC):
        for ic in range(8):
            ssl = slice(ic * TOK + qc * QC, ic * TOK + (qc + 1) * QC)
            dsl = slice((ic % 2) * QC, (ic % 2) * QC + QC)
            st = ic == 0
            sp = ic == 7
            p.add("vector", lambda e, ssl=ssl, dsl=dsl: e.tensor_copy(xb[:, dsl], sx[:, ssl]),
                  reads=[b_sx], writes=[b_xb])
            p.add("scalar", lambda e, ssl=ssl, dsl=dsl: e.activation(
                xs[:, dsl], sx[:, ssl], AF.Square),
                reads=[b_sx], writes=[b_xs])
            p.add("tensor", lambda e, dsl=dsl, st=st, sp=sp: e.matmul(
                pstat[0:1, 0:QC], ones_colb[:], xb[:, dsl], start=st, stop=sp,
                tile_position=(0, 0)),
                reads=[b_xb, b_ones_colb], writes=([b_pstat] if sp else []))
            p.add("tensor", lambda e, dsl=dsl, st=st, sp=sp: e.matmul(
                pstat[32:33, QC:2 * QC], ones_colb[:], xs[:, dsl], start=st, stop=sp,
                tile_position=(0, 32)),
                reads=[b_xs, b_ones_colb], writes=([b_pstat] if sp else []))
        p.add("vector", lambda e: e.tensor_scalar(
            rowm[:], pstat[0:1, 0:QC], 1.0 / D, None, ALU.mult),
            reads=[b_pstat], writes=[b_rowm])
        p.add("vector", lambda e: e.tensor_scalar(
            row[:], pstat[32:33, QC:2 * QC], 1.0 / D, None, ALU.mult),
            reads=[b_pstat], writes=[b_row])
        p.add("vector", lambda e: e.scalar_tensor_tensor(
            row2[:], rowm[:], -1.0, rowm[:], ALU.mult, ALU.mult),
            reads=[b_rowm], writes=[b_row2])
        p.add("vector", lambda e: e.tensor_add(
            row[:], row[:], row2[:]),
            reads=[b_row, b_row2], writes=[b_row])
        ceps, b_ceps = p.constf(eps)
        p.add("scalar", lambda e: e.activation(
            row[:], row[:], AF.Ln, bias=ceps[:]),
            reads=[b_row, b_ceps], writes=[b_row])
        p.add("scalar", lambda e: e.activation(
            row[:], row[:], AF.Exp, scale=-0.5),
            reads=[b_row], writes=[b_row])
        p.add("vector", lambda e: e.scalar_tensor_tensor(
            row2[:], rowm[:], -1.0, row[:], ALU.mult, ALU.mult),
            reads=[b_rowm, b_row], writes=[b_row2])
        p.add("tensor", lambda e: e.matmul(
            pbc[:, 0:QC], onesf_col[:], row[:], start=True, stop=True),
            reads=[b_row, b_onesf], writes=[b_pbc])
        p.add("tensor", lambda e: e.matmul(
            pbc[:, QC:2 * QC], onesf_col[:], row2[:], start=True, stop=True),
            reads=[b_row2, b_onesf], writes=[b_pbc])
        for ic in range(8):
            csl = slice(ic * TOK + qc * QC, ic * TOK + (qc + 1) * QC)
            p.add("vector", lambda e, csl=csl: e.tensor_mul(
                tmp[:], sx[:, csl], pbc[:, 0:QC]),
                reads=[b_sx, b_pbc], writes=[b_tmp])
            tok_v = p.add("vector", lambda e, csl=csl: e.tensor_add(
                lnx[:, csl], tmp[:], pbc[:, QC:2 * QC]),
                reads=[b_tmp, b_pbc], writes=[b_lnx])
        tok_s = p.add("scalar", lambda e: e.activation(row[:], row[:], AF.Identity),
                      reads=[b_row], writes=[b_row])
        tok_t = p.add("tensor", lambda e: e.matmul(
            pbc[:, 0:QC], onesf_col[:], row[:], start=True, stop=True),
            reads=[b_row, b_onesf], writes=[b_pbc])
    return [tok_v, tok_s, tok_t]


QCH = 512


def make_quant_shared(nc, p, ctx, tag):
    sw = [ctx.enter_context(nc.sbuf_tensor(f"{tag}_sw{i}", [128, QCH], F32))
          for i in range(2)]
    swq = [ctx.enter_context(nc.sbuf_tensor(f"{tag}_swq{i}", [128, QCH], BF16))
           for i in range(2)]
    ta = ctx.enter_context(nc.sbuf_tensor(f"{tag}_ta", [128, QCH], F32))
    return dict(sw=sw, b_sw=[p.buf(), p.buf()], swq=swq,
                b_swq=[p.buf(), p.buf()], ta=ta, b_ta=p.buf())


def make_quant(nc, p, ctx, C, sh_, src_dram, n_rows, n_cols, scratch, b_scratch,
               tag, pg, b_pg):
    onesf_col, b_onesf = C["onesf_col"]
    onesf_c128, b_onesf128 = C["onesf_c128"]
    sw, b_sw = sh_["sw"], sh_["b_sw"]
    swq, b_swq = sh_["swq"], sh_["b_swq"]
    ta, b_ta = sh_["ta"], sh_["b_ta"]
    nch = n_rows // 128
    nhc = n_cols // QCH
    chunks = [(ic, hc) for ic in range(nch) for hc in range(nhc)]
    sabs = ctx.enter_context(nc.sbuf_tensor(f"{tag}_sabs", [128, max(8, len(chunks))], F32))
    b_sabs = p.buf()
    srow = ctx.enter_context(nc.sbuf_tensor(f"{tag}_srow", [128, 24], F32))
    sat, sthr, snthr = srow[:, 0:8], srow[:, 8:16], srow[:, 16:24]
    b_sat, b_sthr, b_snthr = p.buf(), p.buf(), p.buf()
    sg8 = ctx.enter_context(nc.sbuf_tensor(f"{tag}_sg8", [1, 8], F32))
    b_sg8 = p.buf()

    def pass1():
        for i, (ic, hc) in enumerate(chunks):
            par = i % 2
            p.dma("gpsimd", sw[par][:],
                  src_dram[ic * 128:(ic + 1) * 128, hc * QCH:(hc + 1) * QCH],
                  writes=[b_sw[par]])
            p.add("vector", lambda e, i=i, par=par: e.reduce_sum(
                sabs[:, i:i + 1], sw[par][:], axis=AX.X, apply_absolute_value=True),
                reads=[b_sw[par]], writes=[b_sabs])

    def mid():
        p.add("vector", lambda e: e.reduce_sum(sat[:, 0:1], sabs[:, 0:len(chunks)],
                                               axis=AX.X),
              reads=[b_sabs], writes=[b_sat])
        p.add("tensor", lambda e: e.matmul(pg[0:1, 0:8], sat[:, 0:1], onesf_c128[:, 0:8],
                                           start=True, stop=True),
              reads=[b_sat, b_onesf128], writes=[b_pg])
        ntot = float(n_rows * n_cols)
        p.add("vector", lambda e: e.tensor_scalar(sg8[:], pg[0:1, 0:8], 0.5 / ntot,
                                                  0.5e-6, ALU.mult, ALU.add),
              reads=[b_pg], writes=[b_sg8])
        p.add("tensor", lambda e: e.matmul(pg[:, 0:8], onesf_col[:], sg8[:],
                                           start=True, stop=True),
              reads=[b_sg8, b_onesf], writes=[b_pg])
        p.add("vector", lambda e: e.tensor_copy(sthr[:], pg[:, 0:8]),
              reads=[b_pg], writes=[b_sthr])
        p.add("vector", lambda e: e.tensor_scalar(snthr[:], sthr[:], -1.0, None, ALU.mult),
              reads=[b_sthr], writes=[b_snthr])

    def pass2():
        for i, (ic, hc) in enumerate(chunks):
            par = i % 2
            p.dma("gpsimd", sw[par][:],
                  src_dram[ic * 128:(ic + 1) * 128, hc * QCH:(hc + 1) * QCH],
                  writes=[b_sw[par]])
            p.add("vector", lambda e, par=par: e.tensor_scalar(
                ta[:], sw[par][:], snthr[:, 0:1], None, ALU.is_le),
                reads=[b_sw[par], b_snthr], writes=[b_ta])
            p.add("vector", lambda e, par=par: e.scalar_tensor_tensor(
                swq[par][:], sw[par][:], sthr[:, 0:1], ta[:], ALU.is_ge, ALU.subtract),
                reads=[b_sw[par], b_sthr, b_ta], writes=[b_swq[par]])
            p.dma("sync", scratch[ic * 128:(ic + 1) * 128, hc * QCH:(hc + 1) * QCH],
                  swq[par][:], reads=[b_swq[par]], writes=[b_scratch])

    return pass1, mid, pass2



def build_fused(debug=False):
    nc = bass.Bass("TRN2", target_bir_lowering=False)
    xin = nc.dram_tensor("xin", [D, TOK], F32, kind="ExternalInput").ap()
    pos0 = nc.dram_tensor("pos0", [1, 1], F32, kind="ExternalInput").ap()
    wq0 = nc.dram_tensor("wq0", [D, D], F32, kind="ExternalInput").ap()
    wk0 = nc.dram_tensor("wk0", [D, D], F32, kind="ExternalInput").ap()
    wv0 = nc.dram_tensor("wv0", [D, D], F32, kind="ExternalInput").ap()
    bqkv = nc.dram_tensor("bqkv", [1, 3 * D], F32, kind="ExternalInput").ap()
    wfcT = nc.dram_tensor("wfcT", [D, 4096], F32, kind="ExternalInput").ap()
    wprojT = nc.dram_tensor("wprojT", [4096, D], F32, kind="ExternalInput").ap()
    lnw = nc.dram_tensor("lnw", [1, 4096], F32, kind="ExternalInput").ap()
    lnb = nc.dram_tensor("lnb", [1, 4096], F32, kind="ExternalInput").ap()
    wq1 = nc.dram_tensor("wq1", [D, D], F32, kind="ExternalInput").ap()
    wk1 = nc.dram_tensor("wk1", [D, D], F32, kind="ExternalInput").ap()
    wv1 = nc.dram_tensor("wv1", [D, D], F32, kind="ExternalInput").ap()
    wo1 = nc.dram_tensor("wo1", [D, D], F32, kind="ExternalInput").ap()
    b1 = nc.dram_tensor("b1", [1, 4 * D], F32, kind="ExternalInput").ap()
    rwT = nc.dram_tensor("rwT", [D, 8], F32, kind="ExternalInput").ap()
    wgT = nc.dram_tensor("wgT", [8 * D, 2048], F32, kind="ExternalInput").ap()
    wuT = nc.dram_tensor("wuT", [8 * D, 2048], F32, kind="ExternalInput").ap()
    wdT = nc.dram_tensor("wdT", [8 * 2048, D], F32, kind="ExternalInput").ap()
    oX4 = nc.dram_tensor("oX4", [D, TOK], F32, kind="ExternalOutput").ap()
    if debug:
        dX1 = nc.dram_tensor("dX1", [D, TOK], F32, kind="ExternalOutput").ap()
        dX2 = nc.dram_tensor("dX2", [D, TOK], F32, kind="ExternalOutput").ap()
        dX3 = nc.dram_tensor("dX3", [D, TOK], F32, kind="ExternalOutput").ap()
        dX0 = nc.dram_tensor("dX0", [D, TOK], F32, kind="ExternalOutput").ap()
        dSM = nc.dram_tensor("dSM", [1, 104], F32, kind="ExternalOutput").ap()
        dBC = nc.dram_tensor("dBC", [128, 1], F32, kind="ExternalOutput").ap()
        dL = nc.dram_tensor("dL", [128, QC], F32, kind="ExternalOutput").ap()
        dQ = nc.dram_tensor("dQ", [128, QC], BF16, kind="ExternalOutput").ap()
        dK = nc.dram_tensor("dK", [128, QC], BF16, kind="ExternalOutput").ap()
        dV = nc.dram_tensor("dV", [128, QC], BF16, kind="ExternalOutput").ap()
        dDW = nc.dram_tensor("dDW", [8, TOK], F32, kind="ExternalOutput").ap()
        dLN = nc.dram_tensor("dLN", [128, TOK], BF16, kind="ExternalOutput").ap()
        dH = nc.dram_tensor("dH", [128, 1024], BF16, kind="ExternalOutput").ap()
        dSB = nc.dram_tensor("dSB", [128, 256], F32, kind="ExternalOutput").ap()
        dSC = nc.dram_tensor("dSC", [128, 1024], F32, kind="ExternalOutput").ap()
    q0 = nc.dram_tensor("q0", [D, TOK], BF16, kind="Internal").ap()
    k0in = nc.dram_tensor("k0in", [D, TOK], BF16, kind="Internal").ap()
    v0in = nc.dram_tensor("v0in", [NHP, TOK, 128], BF16, kind="Internal").ap()
    s0in = nc.dram_tensor("s0in", [1, 16], BF16, kind="Internal").ap()
    k0out = nc.dram_tensor("k0out", [4, D, TOK], BF16, kind="Internal").ap()
    v0out = nc.dram_tensor("v0out", [4, NHP, TOK, 128], BF16, kind="Internal").ap()
    s0out = nc.dram_tensor("s0out", [4, 1, 16], BF16, kind="Internal").ap()
    q1 = nc.dram_tensor("q1", [D, TOK], BF16, kind="Internal").ap()
    k1in = nc.dram_tensor("k1in", [D, TOK], BF16, kind="Internal").ap()
    v1in = nc.dram_tensor("v1in", [NHP, TOK, 128], BF16, kind="Internal").ap()
    k1out = nc.dram_tensor("k1out", [4, D, TOK], BF16, kind="Internal").ap()
    v1out = nc.dram_tensor("v1out", [4, NHP, TOK, 128], BF16, kind="Internal").ap()
    wfcq = nc.dram_tensor("wfcq", [D, 4096], BF16, kind="Internal").ap()
    wprojq = nc.dram_tensor("wprojq", [4096, D], BF16, kind="Internal").ap()

    NKB = SEQ // 128
    from contextlib import ExitStack
    with ExitStack() as ctx:
        p = Prog(nc)
        p._ctx = ctx
        C = build_consts(nc, p, ctx)
        p.constf(LN_EPS)
        p.constf(FFN_LN_EPS)
        ones_row, b_ones_row = C["ones_row"]
        onesf_col, b_onesf = C["onesf_col"]
        onesf_colb, b_onesf_colb = C["onesf_colb"]
        ones_colb, b_ones_colb = C["ones_col"]

        sx = ctx.enter_context(nc.sbuf_tensor("sx", [128, 8 * TOK], F32))
        b_sx = p.buf("sx")
        nacc = ctx.enter_context(nc.sbuf_tensor("nacc", [1, 32], F32))
        b_nacc = p.buf("nacc")
        b_q0, b_k0in, b_v0in, b_s0in = p.buf(), p.buf(), p.buf(), p.buf()
        b_k0out, b_v0out, b_s0out = p.buf(), p.buf(), p.buf()
        b_q1, b_k1in, b_v1in = p.buf(), p.buf(), p.buf()
        b_k1out, b_v1out = p.buf(), p.buf()
        b_wfcq, b_wprojq = p.buf("wfcq"), p.buf("wprojq")

        def phase_A():
          with ExitStack() as actx:
            for ic in range(8):
                p.dma("sync", sx[:, ic * TOK:(ic + 1) * TOK],
                      xin[ic * 128:(ic + 1) * 128, :], writes=[b_sx])
            swq = actx.enter_context(nc.sbuf_tensor("a_swq", [128, 8 * D], BF16))
            swk = actx.enter_context(nc.sbuf_tensor("a_swk", [128, 8 * D], BF16))
            swv = actx.enter_context(nc.sbuf_tensor("a_swv", [128, 8 * D], BF16))
            b_swq, b_swk, b_swv = p.buf(), p.buf(), p.buf()
            sb_b = actx.enter_context(nc.sbuf_tensor("a_sbb", [1, 3 * D], BF16))
            b_sbb = p.buf()
            lnx = actx.enter_context(nc.sbuf_tensor("a_lnx", [128, 8 * TOK], BF16))
            b_lnx = p.buf()
            for ic in range(8):
                p.dma("gpsimd", swq[:, ic * D:(ic + 1) * D],
                      wq0[ic * 128:(ic + 1) * 128, :], writes=[b_swq])
                p.dma("gpsimd", swk[:, ic * D:(ic + 1) * D],
                      wk0[ic * 128:(ic + 1) * 128, :], writes=[b_swk])
                p.dma("gpsimd", swv[:, ic * D:(ic + 1) * D],
                      wv0[ic * 128:(ic + 1) * 128, :], writes=[b_swv])
            p.dma("gpsimd", sb_b[:], bqkv, writes=[b_sbb])

            add_rope_consts(nc, p, actx, C)
            C2 = actx.enter_context(nc.sbuf_tensor("a_C2", [128, TOK], F32))
            S2 = actx.enter_context(nc.sbuf_tensor("a_S2", [128, TOK], F32))
            b_C2, b_S2 = p.buf(), p.buf()
            tctx = ExitStack()
            inv = tctx.enter_context(nc.sbuf_tensor("a_inv", [32, 1], F32))
            b_inv = p.buf()
            ramp = tctx.enter_context(nc.sbuf_tensor("a_ramp", [1, QC], F32))
            b_ramp = p.buf()
            spos = tctx.enter_context(nc.sbuf_tensor("a_spos", [1, 1], F32))
            b_spos = p.buf()
            ang = tctx.enter_context(nc.sbuf_tensor("a_ang", [32, QC], F32))
            wk1_ = tctx.enter_context(nc.sbuf_tensor("a_wk1", [32, QC], F32))
            wk2_ = tctx.enter_context(nc.sbuf_tensor("a_wk2", [32, QC], F32))
            b_ang, b_wk1, b_wk2 = p.buf(), p.buf(), p.buf()
            pA = actx.enter_context(nc.psum_tensor("a_pA", [128, 2 * QC], F32))
            pB = actx.enter_context(nc.psum_tensor("a_pB", [128, 2 * QC], F32))
            pC = actx.enter_context(nc.psum_tensor("a_pC", [128, 2 * QC], F32))
            pD = actx.enter_context(nc.psum_tensor("a_pD", [128, 2 * QC], F32))
            b_pA, b_pB, b_pC, b_pD = p.buf("pA"), p.buf("pB"), p.buf("pC"), p.buf("pD")
            R, b_R = C["R"]
            Dm, b_D = C["D"]
            prt, b_prt = pA, b_pA

            p.dma("sync", spos[:], pos0, writes=[b_spos])
            p.add("gpsimd", lambda e: e.iota(inv[:], [[0, 1]], base=0, channel_multiplier=1,
                                             allow_small_or_imprecise_dtypes=True),
                  writes=[b_inv])
            p.add("scalar", lambda e: e.activation(inv[:], inv[:], AF.Exp,
                                                   scale=-math.log(10000.0) / 32.0),
                  reads=[b_inv], writes=[b_inv])

            TWO_PI = 2.0 * math.pi
            CW_HI = 6.28125
            CW_LO = TWO_PI - CW_HI
            MAGIC = 1.5 * 2.0 ** 23

            for qc in range(TOK // QC):
                qsl = slice(qc * QC, (qc + 1) * QC)
                p.add("gpsimd", lambda e, qc=qc: e.iota(ramp[:], [[1, QC]], base=qc * QC,
                                                        channel_multiplier=0,
                                                        allow_small_or_imprecise_dtypes=True),
                      writes=[b_ramp])
                p.add("vector", lambda e: e.tensor_scalar(ramp[:], ramp[:], spos[:], None, ALU.add),
                      reads=[b_ramp, b_spos], writes=[b_ramp])
                p.add("tensor", lambda e: e.matmul(
                    prt[0:32, 0:QC], onesf_col[:, 0:32], ramp[:], start=True, stop=True),
                    reads=[b_ramp, b_onesf], writes=[b_prt])
                p.add("vector", lambda e: e.tensor_scalar(ang[:], prt[0:32, 0:QC], inv[:], None, ALU.mult),
                      reads=[b_prt, b_inv], writes=[b_ang])
                p.add("vector", lambda e: e.tensor_scalar(wk1_[:], ang[:], 1.0 / TWO_PI, MAGIC,
                                                          ALU.mult, ALU.add),
                      reads=[b_ang], writes=[b_wk1])
                p.add("vector", lambda e: e.tensor_scalar(wk1_[:], wk1_[:], MAGIC, None, ALU.subtract),
                      reads=[b_wk1], writes=[b_wk1])
                p.add("vector", lambda e: e.scalar_tensor_tensor(wk2_[:], wk1_[:], -CW_HI, ang[:],
                                                                 ALU.mult, ALU.add),
                      reads=[b_wk1, b_ang], writes=[b_wk2])
                p.add("vector", lambda e: e.scalar_tensor_tensor(wk2_[:], wk1_[:], -CW_LO, wk2_[:],
                                                                 ALU.mult, ALU.add),
                      reads=[b_wk1, b_wk2], writes=[b_wk2])
                p.add("scalar", lambda e: e.activation(ang[:], wk2_[:], AF.Sin),
                      reads=[b_wk2], writes=[b_ang])
                p.add("tensor", lambda e: e.matmul(prt[:, 0:QC], Dm[:], ang[:], start=True, stop=True),
                      reads=[b_ang, b_D], writes=[b_prt])
                p.add("vector", lambda e, qsl=qsl: e.tensor_copy(S2[:, qsl], prt[:, 0:QC]),
                      reads=[b_prt], writes=[b_S2])
                p.add("vector", lambda e: e.tensor_scalar(wk1_[:], wk2_[:], math.pi / 2.0, None, ALU.is_gt),
                      reads=[b_wk2], writes=[b_wk1])
                p.add("vector", lambda e: e.scalar_tensor_tensor(wk2_[:], wk1_[:], -TWO_PI, wk2_[:],
                                                                 ALU.mult, ALU.add),
                      reads=[b_wk1, b_wk2], writes=[b_wk2])
                p.add("vector", lambda e: e.tensor_scalar(wk2_[:], wk2_[:], math.pi / 2.0, None, ALU.add),
                      reads=[b_wk2], writes=[b_wk2])
                p.add("scalar", lambda e: e.activation(ang[:], wk2_[:], AF.Sin),
                      reads=[b_wk2], writes=[b_ang])
                p.add("tensor", lambda e: e.matmul(prt[:, 0:QC], Dm[:], ang[:], start=True, stop=True),
                      reads=[b_ang, b_D], writes=[b_prt])
                p.add("vector", lambda e, qsl=qsl: e.tensor_copy(C2[:, qsl], prt[:, 0:QC]),
                      reads=[b_prt], writes=[b_C2])

            ln_stats_normalize(nc, p, tctx, C, sx, b_sx, lnx, b_lnx, "a_ln",
                               pB, b_pB, pC, b_pC)
            tctx.close()
            p.begin_phase()

            stg = actx.enter_context(nc.sbuf_tensor("a_stg", [128, TOK], BF16))
            b_stg = p.buf()
            qb = actx.enter_context(nc.sbuf_tensor("a_qb", [128, QC], BF16))
            b_qb = p.buf()
            tq = actx.enter_context(nc.sbuf_tensor("a_tq", [128, QC], F32))
            tr = actx.enter_context(nc.sbuf_tensor("a_tr", [128, QC], F32))
            b_tq, b_tr = p.buf(), p.buf()
            sq2 = actx.enter_context(nc.sbuf_tensor("a_sq2", [128, QC], BF16))
            b_sq2 = p.buf()
            nstrip = actx.enter_context(nc.sbuf_tensor("a_nstrip", [1, 128], F32))
            b_nstrip = p.buf()
            sk16 = actx.enter_context(nc.sbuf_tensor("a_sk16", [1, 16], BF16))
            b_sk16 = p.buf()

            pP, b_pP = pA, b_pA
            pR, b_pR = pB, b_pB
            pN, b_pN = pC, b_pC

            k_toks = []
            for kind in range(2):
                sw = [swq, swk][kind]
                b_sw = [b_swq, b_swk][kind]
                brow = kind
                out_t = [q0, k0in][kind]
                b_out = [b_q0, b_k0in][kind]
                for hp in range(NHP):
                    for qc in range(TOK // QC):
                        psl = slice((qc % 2) * QC, (qc % 2) * QC + QC)
                        qsl = slice(qc * QC, (qc + 1) * QC)
                        for ic in range(8):
                            st = ic == 0
                            p.add("tensor", lambda e, ic=ic, hp=hp, qsl=qsl, psl=psl, st=st, sw=sw: e.matmul(
                                pP[:, psl], sw[:, ic * D + hp * 128: ic * D + (hp + 1) * 128],
                                lnx[:, ic * TOK:][:, qsl], start=st, stop=False),
                                reads=[b_sw, b_lnx], writes=([b_pP] if st else []))
                        p.add("tensor", lambda e, hp=hp, psl=psl, brow=brow: e.matmul(
                            pP[:, psl], sb_b[0:1, brow * D + hp * 128: brow * D + (hp + 1) * 128],
                            ones_row[:], start=False, stop=True),
                            reads=[b_sbb, b_ones_row], writes=[b_pP])
                        p.add("scalar", lambda e, psl=psl: e.activation(qb[:], pP[:, psl], AF.Copy),
                              reads=[b_pP], writes=[b_qb])
                        p.add("tensor", lambda e, psl=psl: e.matmul(
                            pR[:, psl], R[:], qb[:], start=True, stop=True),
                            reads=[b_qb, b_R], writes=[b_pR])
                        p.add("vector", lambda e, qsl=qsl: e.tensor_mul(tq[:], qb[:], C2[:, qsl]),
                              reads=[b_qb, b_C2], writes=[b_tq])
                        p.add("vector", lambda e, qsl=qsl, psl=psl: e.tensor_mul(
                            tr[:], pR[:, psl], S2[:, qsl]),
                              reads=[b_pR, b_S2], writes=[b_tr])
                        p.add("vector", lambda e, qsl=qsl: e.tensor_add(stg[:, qsl], tq[:], tr[:]),
                              reads=[b_tq, b_tr], writes=[b_stg])
                        p.add("scalar", lambda e, qsl=qsl: e.activation(sq2[:], stg[:, qsl], AF.Square),
                              reads=[b_stg], writes=[b_sq2])
                        p.add("tensor", lambda e: e.matmul(
                            pN[0:1, 0:QC], ones_colb[0:64, :], sq2[0:64, :],
                            start=True, stop=True, tile_position=(0, 0)),
                            reads=[b_sq2, b_ones_colb], writes=[b_pN])
                        p.add("tensor", lambda e: e.matmul(
                            pN[0:1, QC:2 * QC], ones_colb[64:128, :], sq2[64:128, :],
                            start=True, stop=True, tile_position=(64, 0)),
                            reads=[b_sq2, b_ones_colb], writes=[b_pN])
                        colA = ((kind * 8 + hp) * 2 + 0) * 4 + qc
                        colB = ((kind * 8 + hp) * 2 + 1) * 4 + qc
                        p.add("vector", lambda e, colA=colA: e.reduce_max(
                            nstrip[:, colA:colA + 1], pN[0:1, 0:QC], axis=AX.X),
                            reads=[b_pN], writes=[b_nstrip])
                        p.add("vector", lambda e, colB=colB: e.reduce_max(
                            nstrip[:, colB:colB + 1], pN[0:1, QC:2 * QC], axis=AX.X),
                            reads=[b_pN], writes=[b_nstrip])
                    tk = p.dma("sync", out_t[hp * 128:(hp + 1) * 128, :], stg[:],
                               reads=[b_stg], writes=[b_out])
                    if kind == 1:
                        k_toks.append(tk)

            p.add("vector", lambda e: e.reduce_max(
                nacc[:], nstrip[:].rearrange("p (h q) -> p h q", q=4), axis=AX.X),
                reads=[b_nstrip], writes=[b_nacc])
            p.add("vector", lambda e: e.tensor_scalar(
                sk16[:], nacc[:, 16:32], 1.01, None, ALU.mult),
                reads=[b_nacc], writes=[b_sk16])
            s_tok = p.dma("sync", s0in, sk16[:], reads=[b_sk16], writes=[b_s0in])

            pV, b_pV = pD, b_pD
            sV0 = actx.enter_context(nc.sbuf_tensor("a_sV", [128, D], BF16))
            b_sV0 = p.buf()
            v_toks = []
            for tt in range(TOK // 128):
                tsl = slice(tt * 128, (tt + 1) * 128)
                for half in range(2):
                    hsl = slice(half * QC, (half + 1) * QC)
                    for ic in range(8):
                        st = ic == 0
                        sp = ic == 7
                        p.add("tensor", lambda e, ic=ic, tsl=tsl, hsl=hsl, st=st, sp=sp: e.matmul(
                            pV[:, hsl], lnx[:, ic * TOK:][:, tsl],
                            swv[:, ic * D:][:, hsl], start=st, stop=sp),
                            reads=[b_lnx, b_swv], writes=([b_pV] if st else []))
                    p.add("tensor", lambda e, hsl=hsl: e.matmul(
                        pV[:, hsl], onesf_colb[:],
                        sb_b[0:1, 2 * D + hsl.start: 2 * D + hsl.stop],
                        start=False, stop=True),
                        reads=[b_sbb, b_onesf_colb], writes=[b_pV])
                p.add("scalar", lambda e: e.activation(sV0[:], pV[:, 0:D], AF.Copy),
                      reads=[b_pV], writes=[b_sV0])
                for hp in range(NHP):
                    v_toks.append(p.dma(
                        "sync", v0in[hp, tsl, :], sV0[:, hp * 128:(hp + 1) * 128],
                        reads=[b_sV0], writes=[b_v0in]))
            return k_toks, v_toks, s_tok

        k_toks, v_toks, s_tok = phase_A()

        p.add("gpsimd", lambda e: e.collective_compute(
            "AllGather", ALU.bypass, ins=[s0in], outs=[s0out], replica_groups=RG),
            reads=[b_s0in], writes=[b_s0out], deps=[s_tok])
        p.add("gpsimd", lambda e: e.collective_compute(
            "AllGather", ALU.bypass, ins=[k0in], outs=[k0out], replica_groups=RG),
            reads=[b_k0in], writes=[b_k0out], deps=k_toks)
        p.add("gpsimd", lambda e: e.collective_compute(
            "AllGather", ALU.bypass, ins=[v0in], outs=[v0out], replica_groups=RG),
            reads=[b_v0in], writes=[b_v0out], deps=v_toks)

        p.begin_phase()
        def phase_B():
          with ExitStack() as bctx:
            add_E1_const(nc, p, bctx, C)
            E1, b_E1 = C["E1"]
            sQ = bctx.enter_context(nc.sbuf_tensor("b_sQ", [128, NHP * TOK], BF16))
            b_sQ = p.buf()
            sK = [bctx.enter_context(nc.sbuf_tensor(f"b_sK{i}", [128, SEQ], BF16)) for i in range(2)]
            sVt = [bctx.enter_context(nc.sbuf_tensor(f"b_sV{i}", [128, SEQ], BF16)) for i in range(2)]
            b_sK = [p.buf(), p.buf()]
            b_sV = [p.buf(), p.buf()]
            sP = bctx.enter_context(nc.sbuf_tensor("b_sP", [128, 3072], BF16))
            b_sP = [p.buf(), p.buf(), p.buf()]
            sl = bctx.enter_context(nc.sbuf_tensor("b_sl", [128, QC], F32))
            srl = bctx.enter_context(nc.sbuf_tensor("b_srl", [128, QC], F32))
            sO = bctx.enter_context(nc.sbuf_tensor("b_sO", [128, QC], F32))
            sC_ = bctx.enter_context(nc.sbuf_tensor("b_sC", [128, QC], F32))
            b_sl, b_srl, b_sO, b_sC = p.buf(), p.buf(), p.buf(), p.buf()
            smq = bctx.enter_context(nc.sbuf_tensor("b_smq", [1, 16], F32))
            smkb = bctx.enter_context(nc.sbuf_tensor("b_smkb", [1, 64], BF16))
            smk = bctx.enter_context(nc.sbuf_tensor("b_smk", [1, 64], F32))
            sm1 = bctx.enter_context(nc.sbuf_tensor("b_sm1", [1, 8], F32))
            sm2 = bctx.enter_context(nc.sbuf_tensor("b_sm2", [1, 8], F32))
            sm3 = bctx.enter_context(nc.sbuf_tensor("b_sm3", [1, 8], F32))
            sbias_c = bctx.enter_context(nc.sbuf_tensor("b_sbias", [128, 1], F32))
            b_smq, b_smkb, b_smk, b_sm1, b_sbias = (p.buf(), p.buf(), p.buf(),
                                                    p.buf(), p.buf())

            pS = bctx.enter_context(nc.psum_tensor("b_pS", [128, 3072], F32))
            b_pSslot = [p.buf(), p.buf(), p.buf()]
            pO = bctx.enter_context(nc.psum_tensor("b_pO", [128, QC], F32))
            b_pO = p.buf()
            pL = bctx.enter_context(nc.psum_tensor("b_pL", [128, QC], F32))
            b_pL = p.buf()
            sacc = bctx.enter_context(nc.sbuf_tensor("b_sacc", [128, 1024], F32))
            sacc2 = bctx.enter_context(nc.sbuf_tensor("b_sacc2", [128, 1024], F32))
            b_sacc = p.buf()
            b_sacc2 = p.buf()
            onesf1 = bctx.enter_context(nc.sbuf_tensor("b_onesf1", [128, 1], F32))
            b_onesf1 = p.buf()
            p.add("vector", lambda e: e.memset(onesf1[:], 1.0), writes=[b_onesf1])

            p.dma("sync", smkb[:], s0out.rearrange("r a h -> a (r h)"),
                  reads=[b_s0out], writes=[b_smkb])
            p.add("vector", lambda e: e.tensor_copy(smq[:], nacc[:, 0:16]),
                  reads=[b_nacc], writes=[b_smq])
            p.add("vector", lambda e: e.tensor_copy(smk[:], smkb[:]),
                  reads=[b_smkb], writes=[b_smk])
            p.add("vector", lambda e: e.memset(sm1[:], 0.0), writes=[b_sm1])
            p.add("vector", lambda e: e.memset(sm2[:], 0.0), writes=[b_sm1])
            p.add("vector", lambda e: e.reduce_max(sm1[:, 0:1], smq[:], axis=AX.X),
                  reads=[b_smq], writes=[b_sm1])
            p.add("vector", lambda e: e.reduce_max(sm2[:, 0:1], smk[:], axis=AX.X),
                  reads=[b_smk], writes=[b_sm1])
            p.add("vector", lambda e: e.tensor_mul(sm3[:], sm1[:], sm2[:]),
                  reads=[b_sm1], writes=[b_sm1])
            p.add("scalar", lambda e: e.activation(sm1[:], sm3[:], AF.Sqrt),
                  reads=[b_sm1], writes=[b_sm1])
            p.add("vector", lambda e: e.tensor_scalar(sm2[:], sm1[:],
                                                      -1.02 / 8.0, -0.5, ALU.mult, ALU.add),
                  reads=[b_sm1], writes=[b_sm1])
            p.add("tensor", lambda e: e.matmul(pO[:, 0:1], onesf_col[:], sm2[:, 0:1],
                                               start=True, stop=True),
                  reads=[b_sm1, b_onesf], writes=[b_pO])
            p.add("vector", lambda e: e.tensor_copy(sbias_c[:], pO[:, 0:1]),
                  reads=[b_pO], writes=[b_sbias])
            if debug:
                for ic in range(8):
                    p.dma("sync", dX0[ic * 128:(ic + 1) * 128, :],
                          sx[:, ic * TOK:(ic + 1) * TOK], reads=[b_sx])
                p.dma("sync", dSM[:, 0:64], smk[:], reads=[b_smk])
                p.dma("sync", dSM[:, 64:80], smq[:], reads=[b_smq])
                p.dma("sync", dSM[:, 80:88], sm1[:], reads=[b_sm1])
                p.dma("sync", dSM[:, 88:96], sm2[:], reads=[b_sm1])
                p.dma("sync", dSM[:, 96:104], sm3[:], reads=[b_sm1])
                p.dma("sync", dBC, sbias_c[:], reads=[b_sbias])

            p.dma("sync", sQ[:].rearrange("p (h t) -> p h t", h=NHP),
                  q0.rearrange("(h p) t -> p h t", p=128),
                  reads=[b_q0], writes=[b_sQ])

            def load_hp0(hp, par):
                for r in range(4):
                    p.dma("sync", sK[par][:, r * TOK:(r + 1) * TOK],
                          k0out[r, hp * 128:(hp + 1) * 128, :],
                          reads=[b_k0out], writes=[b_sK[par]])
                    p.dma("gpsimd",
                          sVt[par][:, r * TOK:(r + 1) * TOK]
                          .rearrange("p (kb d) -> p kb d", d=128),
                          v0out[r, hp].rearrange("(kb p) d -> p kb d", p=128),
                          reads=[b_v0out], writes=[b_sV[par]])

            load_hp0(0, 0)
            for hp in range(NHP):
                par = hp % 2
                if hp + 1 < NHP:
                    load_hp0(hp + 1, (hp + 1) % 2)
                for qc in range(TOK // QC):
                    qoff = hp * TOK + qc * QC

                    def emit_S(g, qoff=qoff, par=par):
                        slot = g % 3
                        scol = slot * 1024
                        p.add("tensor", lambda e, g=g, scol=scol, qoff=qoff, par=par: e.matmul(
                            pS[:, scol:scol + QC], sK[par][0:64, g * 128:(g + 1) * 128],
                            sQ[0:64, qoff:qoff + QC], start=True, stop=True,
                            tile_position=(0, 0)),
                            reads=[b_sK[par], b_sQ], writes=[b_pSslot[slot]])
                        p.add("tensor", lambda e, g=g, scol=scol, qoff=qoff, par=par: e.matmul(
                            pS[:, scol + QC:scol + 1024], sK[par][64:128, g * 128:(g + 1) * 128],
                            sQ[64:128, qoff:qoff + QC], start=True, stop=True,
                            tile_position=(64, 0)),
                            reads=[b_sK[par], b_sQ], writes=[b_pSslot[slot]])
                        p.add("scalar", lambda e, scol=scol: e.activation(
                            sP[:, scol:scol + 1024], pS[:, scol:scol + 1024], AF.Exp,
                            bias=sbias_c[:], scale=0.125),
                            reads=[b_pSslot[slot], b_sbias], writes=[b_sP[slot]])

                    def emit_PV(g, par=par):
                        slot = g % 3
                        scol = slot * 1024
                        st, sp_ = g == 0, g == NKB - 1
                        p.add("tensor", lambda e, g=g, scol=scol, par=par, st=st, sp_=sp_: e.matmul(
                            pO[0:64, :], sVt[par][:, g * 128:g * 128 + 64],
                            sP[:, scol:scol + QC], start=st, stop=sp_,
                            tile_position=(0, 0)),
                            reads=[b_sV[par], b_sP[slot]],
                            writes=([b_pO] if st or sp_ else []))
                        p.add("tensor", lambda e, g=g, scol=scol, par=par, st=st, sp_=sp_: e.matmul(
                            pO[64:128, :], sVt[par][:, g * 128 + 64:g * 128 + 128],
                            sP[:, scol + QC:scol + 1024], start=st, stop=sp_,
                            tile_position=(0, 64)),
                            reads=[b_sV[par], b_sP[slot]],
                            writes=([b_pO] if st or sp_ else []))
                        if g % 4 == 3:
                            p.add("gpsimd", lambda e, scol=scol, st=(g == 3): e.tensor_tensor(
                                sacc2[:], sacc2[:], sP[:, scol:scol + 1024], ALU.add)
                                if not st else e.tensor_copy(sacc2[:], sP[:, scol:scol + 1024]),
                                reads=[b_sP[slot]], writes=[b_sacc2])
                        else:
                            p.add("vector", lambda e, scol=scol, st=(g == 0): e.tensor_tensor(
                                sacc[:], sacc[:], sP[:, scol:scol + 1024], ALU.add)
                                if not st else e.tensor_copy(sacc[:], sP[:, scol:scol + 1024]),
                                reads=[b_sP[slot]], writes=[b_sacc])

                    for g in range(NKB + 2):
                        if g < NKB:
                            emit_S(g)
                        if g >= 2:
                            emit_PV(g - 2)
                    p.add("vector", lambda e: e.memset(pL[:], 1.0), writes=[b_pL])
                    p.add("tensor", lambda e: e.matmul(
                        pL[0:1, :], onesf1[:], sacc[:, 0:QC], start=True, stop=False),
                        reads=[b_sacc, b_onesf1], writes=[b_pL])
                    p.add("tensor", lambda e: e.matmul(
                        pL[0:1, :], onesf1[:], sacc2[:, 0:QC], start=False, stop=True),
                        reads=[b_sacc2, b_onesf1], writes=[b_pL])
                    p.add("tensor", lambda e: e.matmul(
                        pL[64:65, :], onesf1[:], sacc[:, QC:2 * QC], start=True, stop=False),
                        reads=[b_sacc, b_onesf1], writes=[b_pL])
                    p.add("tensor", lambda e: e.matmul(
                        pL[64:65, :], onesf1[:], sacc2[:, QC:2 * QC], start=False, stop=True),
                        reads=[b_sacc2, b_onesf1], writes=[b_pL])
                    p.add("vector", lambda e: e.tensor_copy(sl[:], pL[:]),
                          reads=[b_pL], writes=[b_sl])
                    p.add("vector", lambda e: e.reciprocal(srl[:], sl[:]),
                          reads=[b_sl], writes=[b_srl])
                    p.add("tensor", lambda e: e.matmul(pS[:, 0:QC], E1[:], srl[:],
                                                       start=True, stop=True),
                          reads=[b_srl, b_E1], writes=[b_pSslot[0]])
                    p.add("vector", lambda e: e.tensor_copy(sO[:], pO[:]),
                          reads=[b_pO], writes=[b_sO])
                    p.add("vector", lambda e: e.tensor_mul(sC_[:], sO[:], pS[:, 0:QC]),
                          reads=[b_sO, b_pSslot[0]], writes=[b_sC])
                    p.add("vector", lambda e, qc=qc, hp=hp: e.tensor_add(
                        sx[:, hp * TOK + qc * QC: hp * TOK + (qc + 1) * QC],
                        sx[:, hp * TOK + qc * QC: hp * TOK + (qc + 1) * QC], sC_[:]),
                        reads=[b_sC], writes=[b_sx])
                    if debug and hp == 0 and qc == 0:
                        p.dma("sync", dL, sl[:], reads=[b_sl])
                        p.dma("sync", dQ, sQ[:, 0:QC], reads=[b_sQ])
                        p.dma("sync", dK, sK[0][:, 0:QC], reads=[b_sK[0]])
                        p.dma("sync", dV, sVt[0][:, 0:QC], reads=[b_sV[0]])
            if debug:
                for ic in range(8):
                    p.dma("sync", dX1[ic * 128:(ic + 1) * 128, :],
                          sx[:, ic * TOK:(ic + 1) * TOK], reads=[b_sx])

        phase_B()

        p.begin_phase()
        def phase_C():
          with ExitStack() as cctx:
            pH = cctx.enter_context(nc.psum_tensor("c_pH", [128, 4 * QC], F32))
            b_pH = p.buf()
            pstat = cctx.enter_context(nc.psum_tensor("c_pstat", [128, 2 * QC], F32))
            b_pstat = p.buf()
            pbc = cctx.enter_context(nc.psum_tensor("c_pbc", [128, 2 * QC], F32))
            b_pbc = p.buf()

            lnx = cctx.enter_context(nc.sbuf_tensor("c_lnx", [128, 8 * TOK], BF16))
            b_lnx = p.buf()
            qsh = make_quant_shared(nc, p, cctx, "c_q")
            qf1, qfm, qf2 = make_quant(nc, p, cctx, C, qsh, wfcT, D, 4096,
                                       wfcq, b_wfcq, "c_qf", pH, b_pH)
            qp1, qpm, qp2 = make_quant(nc, p, cctx, C, qsh, wprojT, 4096, D,
                                       wprojq, b_wprojq, "c_qp", pH, b_pH)
            qf1(); qfm(); qf2()
            ln_stats_normalize(nc, p, cctx, C, sx, b_sx, lnx, b_lnx, "c_ln",
                               pstat, b_pstat, pbc, b_pbc)
            qp1(); qpm(); qp2()

            slnw = cctx.enter_context(nc.sbuf_tensor("c_slnw", [128, 32], F32))
            slnb = cctx.enter_context(nc.sbuf_tensor("c_slnb", [128, 32], F32))
            b_slnw, b_slnb = p.buf(), p.buf()
            p.add("sync", lambda e: e.dma_start(
                out=slnw[:], in_=lnw.rearrange("a (c p) -> (a p) c", p=128),
                allow_slow_non_contiguous=True), writes=[b_slnw], dma=True)
            p.add("sync", lambda e: e.dma_start(
                out=slnb[:], in_=lnb.rearrange("a (c p) -> (a p) c", p=128),
                allow_slow_non_contiguous=True), writes=[b_slnb], dma=True)

            swf = [cctx.enter_context(nc.sbuf_tensor(f"c_swf{i}", [128, 8 * QC], BF16))
                   for i in range(2)]
            b_swf = [p.buf(), p.buf()]
            swp = [cctx.enter_context(nc.sbuf_tensor(f"c_swp{i}", [128, D], BF16))
                   for i in range(2)]
            b_swp = [p.buf(), p.buf()]
            sh = cctx.enter_context(nc.sbuf_tensor("c_sh", [128, 32 * QC], BF16))
            b_sh = p.buf()
            shsq = cctx.enter_context(nc.sbuf_tensor("c_shsq", [128, QC], BF16))
            b_shsq = p.buf()
            hrow = cctx.enter_context(nc.sbuf_tensor("c_hrow", [1, QC], F32))
            hrow2 = cctx.enter_context(nc.sbuf_tensor("c_hrow2", [1, QC], F32))
            hrowm = cctx.enter_context(nc.sbuf_tensor("c_hrowm", [1, QC], F32))
            b_hrow, b_hrow2, b_hrowm = p.buf(), p.buf(), p.buf()
            t1 = cctx.enter_context(nc.sbuf_tensor("c_t1", [128, QC], F32))
            t2 = cctx.enter_context(nc.sbuf_tensor("c_t2", [128, QC], F32))
            b_t1, b_t2 = p.buf(), p.buf()
            ceps, b_ceps = p.constf(FFN_LN_EPS)

            for tc in range(TOK // QC):
                tsl = slice(tc * QC, (tc + 1) * QC)
                for fb in range(8):
                    par = fb % 2
                    for ic in range(8):
                        p.dma("gpsimd", swf[par][:, ic * QC:(ic + 1) * QC],
                              wfcq[ic * 128:(ic + 1) * 128, fb * QC:(fb + 1) * QC],
                              reads=[b_wfcq], writes=[b_swf[par]])
                    for ft in range(4):
                        for ic in range(8):
                            st = ic == 0
                            sp_ = ic == 7
                            p.add("tensor", lambda e, ft=ft, ic=ic, par=par, tsl=tsl, st=st, sp_=sp_: e.matmul(
                                pH[:, ft * QC:(ft + 1) * QC],
                                swf[par][:, ic * QC + ft * 128: ic * QC + (ft + 1) * 128],
                                lnx[:, ic * TOK:][:, tsl], start=st, stop=sp_),
                                reads=[b_swf[par], b_lnx],
                                writes=([b_pH] if st or sp_ else []))
                    p.add("scalar", lambda e, fb=fb: e.activation(
                        sh[:, fb * 4 * QC:(fb + 1) * 4 * QC], pH[:], AF.Gelu),
                        reads=[b_pH], writes=[b_sh])
                    for ft in range(4):
                        fc = fb * 4 + ft
                        st = fc == 0
                        sp_ = fc == 31
                        p.add("scalar", lambda e, fc=fc: e.activation(
                            shsq[:], sh[:, fc * QC:(fc + 1) * QC], AF.Square),
                            reads=[b_sh], writes=[b_shsq])
                        p.add("tensor", lambda e, fc=fc, st=st, sp_=sp_: e.matmul(
                            pstat[0:1, 0:QC], ones_colb[:], sh[:, fc * QC:(fc + 1) * QC],
                            start=st, stop=sp_, tile_position=(0, 0)),
                            reads=[b_sh, b_ones_colb], writes=([b_pstat] if sp_ else []))
                        p.add("tensor", lambda e, st=st, sp_=sp_: e.matmul(
                            pstat[32:33, QC:2 * QC], ones_colb[:], shsq[:],
                            start=st, stop=sp_, tile_position=(0, 32)),
                            reads=[b_shsq, b_ones_colb], writes=([b_pstat] if sp_ else []))
                p.add("vector", lambda e: e.tensor_scalar(
                    hrowm[:], pstat[0:1, 0:QC], 1.0 / 4096.0, None, ALU.mult),
                    reads=[b_pstat], writes=[b_hrowm])
                p.add("vector", lambda e: e.tensor_scalar(
                    hrow[:], pstat[32:33, QC:2 * QC], 1.0 / 4096.0, None, ALU.mult),
                    reads=[b_pstat], writes=[b_hrow])
                p.add("vector", lambda e: e.scalar_tensor_tensor(
                    hrow2[:], hrowm[:], -1.0, hrowm[:], ALU.mult, ALU.mult),
                    reads=[b_hrowm], writes=[b_hrow2])
                p.add("vector", lambda e: e.tensor_add(hrow[:], hrow[:], hrow2[:]),
                      reads=[b_hrow, b_hrow2], writes=[b_hrow])
                p.add("scalar", lambda e: e.activation(hrow[:], hrow[:], AF.Ln, bias=ceps[:]),
                      reads=[b_hrow, b_ceps], writes=[b_hrow])
                p.add("scalar", lambda e: e.activation(hrow[:], hrow[:], AF.Exp, scale=-0.5),
                      reads=[b_hrow], writes=[b_hrow])
                p.add("vector", lambda e: e.scalar_tensor_tensor(
                    hrow2[:], hrowm[:], -1.0, hrow[:], ALU.mult, ALU.mult),
                    reads=[b_hrowm, b_hrow], writes=[b_hrow2])
                p.add("tensor", lambda e: e.matmul(
                    pbc[:, 0:QC], onesf_col[:], hrow[:], start=True, stop=True),
                    reads=[b_hrow, b_onesf], writes=[b_pbc])
                p.add("tensor", lambda e: e.matmul(
                    pbc[:, QC:2 * QC], onesf_col[:], hrow2[:], start=True, stop=True),
                    reads=[b_hrow2, b_onesf], writes=[b_pbc])
                for fc in range(32):
                    fsl = slice(fc * QC, (fc + 1) * QC)
                    p.add("vector", lambda e, fsl=fsl: e.tensor_mul(
                        t1[:], sh[:, fsl], pbc[:, 0:QC]),
                        reads=[b_sh, b_pbc], writes=[b_t1])
                    p.add("vector", lambda e, fsl=fsl: e.tensor_add(
                        t2[:], t1[:], pbc[:, QC:2 * QC]),
                        reads=[b_t1, b_pbc], writes=[b_t2])
                    p.add("vector", lambda e, fc=fc, fsl=fsl: e.tensor_scalar(
                        sh[:, fsl], t2[:], slnw[:, fc:fc + 1], slnb[:, fc:fc + 1],
                        ALU.mult, ALU.add),
                        reads=[b_t2, b_slnw, b_slnb], writes=[b_sh])
                ybanks = [(pH, b_pH, 0), (pH, b_pH, 1), (pH, b_pH, 2), (pH, b_pH, 3),
                          (pstat, b_pstat, 0), (pstat, b_pstat, 1),
                          (pbc, b_pbc, 0), (pbc, b_pbc, 1)]
                for fc in range(32):
                    par = fc % 2
                    p.dma("gpsimd", swp[par][:],
                          wprojq[fc * 128:(fc + 1) * 128, :],
                          reads=[b_wprojq], writes=[b_swp[par]])
                    for dt_ in range(8):
                        yt, b_yt, bi = ybanks[dt_]
                        st = fc == 0
                        sp_ = fc == 31
                        p.add("tensor", lambda e, fc=fc, dt_=dt_, par=par, yt=yt, bi=bi, st=st, sp_=sp_: e.matmul(
                            yt[:, bi * QC:(bi + 1) * QC],
                            swp[par][:, dt_ * 128:(dt_ + 1) * 128],
                            sh[:, fc * QC:(fc + 1) * QC], start=st, stop=sp_),
                            reads=[b_swp[par], b_sh],
                            writes=([b_yt] if st or sp_ else []))
                for dt_ in range(8):
                    yt, b_yt, bi = ybanks[dt_]
                    p.add("vector", lambda e, dt_=dt_, yt=yt, bi=bi, tsl=tsl: e.tensor_add(
                        sx[:, dt_ * TOK:][:, tsl], sx[:, dt_ * TOK:][:, tsl],
                        yt[:, bi * QC:(bi + 1) * QC]),
                        reads=[b_yt], writes=[b_sx])
            if debug:
                for ic in range(8):
                    p.dma("sync", dX2[ic * 128:(ic + 1) * 128, :],
                          sx[:, ic * TOK:(ic + 1) * TOK], reads=[b_sx])

        phase_C()

        p.begin_phase()
        def phase_D():
          with ExitStack() as dctx:
            swq = dctx.enter_context(nc.sbuf_tensor("d_swq", [128, 8 * D], BF16))
            swk = dctx.enter_context(nc.sbuf_tensor("d_swk", [128, 8 * D], BF16))
            swv = dctx.enter_context(nc.sbuf_tensor("d_swv", [128, 8 * D], BF16))
            b_swq, b_swk, b_swv = p.buf(), p.buf(), p.buf()
            sb_b = dctx.enter_context(nc.sbuf_tensor("d_sbb", [1, 4 * D], BF16))
            b_sbb = p.buf()
            lnx = dctx.enter_context(nc.sbuf_tensor("d_lnx", [128, 8 * TOK], BF16))
            b_lnx = p.buf()
            stg = dctx.enter_context(nc.sbuf_tensor("d_stg", [128, TOK], BF16))
            b_stg = p.buf()
            sV0 = dctx.enter_context(nc.sbuf_tensor("d_sV", [128, D], BF16))
            b_sV0 = p.buf()

            pP = dctx.enter_context(nc.psum_tensor("d_pP", [128, 2 * QC], F32))
            b_pP = p.buf()
            pstat = dctx.enter_context(nc.psum_tensor("d_pstat", [128, 2 * QC], F32))
            b_pstat = p.buf()
            pbc = dctx.enter_context(nc.psum_tensor("d_pbc", [128, 2 * QC], F32))
            b_pbc = p.buf()
            pV = dctx.enter_context(nc.psum_tensor("d_pV", [128, 2 * QC], F32))
            b_pV = p.buf()

            for ic in range(8):
                p.dma("gpsimd", swq[:, ic * D:(ic + 1) * D],
                      wq1[ic * 128:(ic + 1) * 128, :], writes=[b_swq])
                p.dma("gpsimd", swk[:, ic * D:(ic + 1) * D],
                      wk1[ic * 128:(ic + 1) * 128, :], writes=[b_swk])
                p.dma("gpsimd", swv[:, ic * D:(ic + 1) * D],
                      wv1[ic * 128:(ic + 1) * 128, :], writes=[b_swv])
            p.dma("gpsimd", sb_b[:], b1, writes=[b_sbb])

            ln_stats_normalize(nc, p, dctx, C, sx, b_sx, lnx, b_lnx, "d_ln",
                               pstat, b_pstat, pbc, b_pbc)

            k_toks1 = []
            q_toks1 = []
            for kind in range(2):
                sw = [swq, swk][kind]
                b_sw = [b_swq, b_swk][kind]
                out_t = [q1, k1in][kind]
                b_out = [b_q1, b_k1in][kind]
                for hp in range(NHP):
                    for qc in range(TOK // QC):
                        psl = slice((qc % 2) * QC, (qc % 2) * QC + QC)
                        qsl = slice(qc * QC, (qc + 1) * QC)
                        for ic in range(8):
                            st = ic == 0
                            p.add("tensor", lambda e, ic=ic, hp=hp, qsl=qsl, psl=psl, st=st, sw=sw: e.matmul(
                                pP[:, psl], sw[:, ic * D + hp * 128: ic * D + (hp + 1) * 128],
                                lnx[:, ic * TOK:][:, qsl], start=st, stop=False),
                                reads=[b_sw, b_lnx], writes=([b_pP] if st else []))
                        p.add("tensor", lambda e, hp=hp, psl=psl, kind=kind: e.matmul(
                            pP[:, psl], sb_b[0:1, kind * D + hp * 128: kind * D + (hp + 1) * 128],
                            ones_row[:], start=False, stop=True),
                            reads=[b_sbb, b_ones_row], writes=[b_pP])
                        p.add("vector", lambda e, qsl=qsl, psl=psl: e.tensor_scalar(
                            stg[:, qsl], pP[:, psl], 1.0, None, ALU.is_ge),
                            reads=[b_pP], writes=[b_stg])
                    tk = p.dma("sync", out_t[hp * 128:(hp + 1) * 128, :], stg[:],
                               reads=[b_stg], writes=[b_out])
                    (q_toks1 if kind == 0 else k_toks1).append(tk)

            v_toks1 = []
            for tt in range(TOK // 128):
                tsl = slice(tt * 128, (tt + 1) * 128)
                for half in range(2):
                    hsl = slice(half * QC, (half + 1) * QC)
                    for ic in range(8):
                        st = ic == 0
                        p.add("tensor", lambda e, ic=ic, tsl=tsl, hsl=hsl, st=st: e.matmul(
                            pV[:, hsl], lnx[:, ic * TOK:][:, tsl],
                            swv[:, ic * D:][:, hsl], start=st, stop=False),
                            reads=[b_lnx, b_swv], writes=([b_pV] if st else []))
                    p.add("tensor", lambda e, hsl=hsl: e.matmul(
                        pV[:, hsl], onesf_colb[:],
                        sb_b[0:1, 2 * D + hsl.start: 2 * D + hsl.stop],
                        start=False, stop=True),
                        reads=[b_sbb, b_onesf_colb], writes=[b_pV])
                p.add("scalar", lambda e: e.activation(sV0[:], pV[:, 0:D], AF.Copy),
                      reads=[b_pV], writes=[b_sV0])
                for hp in range(NHP):
                    v_toks1.append(p.dma(
                        "sync", v1in[hp, tsl, :], sV0[:, hp * 128:(hp + 1) * 128],
                        reads=[b_sV0], writes=[b_v1in]))
            return k_toks1, v_toks1

        k_toks1, v_toks1 = phase_D()

        p.add("gpsimd", lambda e: e.collective_compute(
            "AllGather", ALU.bypass, ins=[k1in], outs=[k1out], replica_groups=RG),
            reads=[b_k1in], writes=[b_k1out], deps=k_toks1)
        p.add("gpsimd", lambda e: e.collective_compute(
            "AllGather", ALU.bypass, ins=[v1in], outs=[v1out], replica_groups=RG),
            reads=[b_v1in], writes=[b_v1out], deps=v_toks1)

        p.begin_phase()
        def phase_E():
          with ExitStack() as ectx:
            sctx_t = ectx.enter_context(nc.sbuf_tensor("e_sctx", [128, NHP * TOK], BF16))
            b_sctx = p.buf()
            pS = ectx.enter_context(nc.psum_tensor("e_pS", [128, 3072], F32))
            b_pSslot = [p.buf(), p.buf(), p.buf()]
            pO = ectx.enter_context(nc.psum_tensor("e_pO", [128, QC], F32))
            b_pO = p.buf()
            with ExitStack() as e1ctx:
                sQ = e1ctx.enter_context(nc.sbuf_tensor("e_sQ", [128, NHP * TOK], BF16))
                b_sQ = p.buf()
                sK = [e1ctx.enter_context(nc.sbuf_tensor(f"e_sK{i}", [128, SEQ], BF16))
                      for i in range(2)]
                sVt = [e1ctx.enter_context(nc.sbuf_tensor(f"e_sV{i}", [128, SEQ], BF16))
                       for i in range(2)]
                b_sK = [p.buf(), p.buf()]
                b_sV = [p.buf(), p.buf()]
                sP = e1ctx.enter_context(nc.sbuf_tensor("e_sP", [128, 3072], BF16))
                b_sP = [p.buf(), p.buf(), p.buf()]

                p.dma("sync", sQ[:].rearrange("p (h t) -> p h t", h=NHP),
                      q1.rearrange("(h p) t -> p h t", p=128),
                      reads=[b_q1], writes=[b_sQ])

                def load_hp1(hp, par):
                    for r in range(4):
                        p.dma("sync", sK[par][:, r * TOK:(r + 1) * TOK],
                              k1out[r, hp * 128:(hp + 1) * 128, :],
                              reads=[b_k1out], writes=[b_sK[par]])
                        p.dma("gpsimd",
                              sVt[par][:, r * TOK:(r + 1) * TOK]
                              .rearrange("p (kb d) -> p kb d", d=128),
                              v1out[r, hp].rearrange("(kb p) d -> p kb d", p=128),
                              reads=[b_v1out], writes=[b_sV[par]])

                load_hp1(0, 0)
                for hp in range(NHP):
                    par = hp % 2
                    if hp + 1 < NHP:
                        load_hp1(hp + 1, (hp + 1) % 2)
                    for qc in range(TOK // QC):
                        qoff = hp * TOK + qc * QC

                        def emit_S1(g, qoff=qoff, par=par):
                            slot = g % 3
                            scol = slot * 1024
                            p.add("tensor", lambda e, g=g, scol=scol, qoff=qoff, par=par: e.matmul(
                                pS[:, scol:scol + QC], sK[par][0:64, g * 128:(g + 1) * 128],
                                sQ[0:64, qoff:qoff + QC], start=True, stop=True,
                                tile_position=(0, 0)),
                                reads=[b_sK[par], b_sQ], writes=[b_pSslot[slot]])
                            p.add("tensor", lambda e, g=g, scol=scol, qoff=qoff, par=par: e.matmul(
                                pS[:, scol + QC:scol + 1024], sK[par][64:128, g * 128:(g + 1) * 128],
                                sQ[64:128, qoff:qoff + QC], start=True, stop=True,
                                tile_position=(64, 0)),
                                reads=[b_sK[par], b_sQ], writes=[b_pSslot[slot]])
                            p.add("scalar", lambda e, scol=scol: e.activation(
                                sP[:, scol:scol + 1024], pS[:, scol:scol + 1024], AF.Sigmoid,
                                scale=0.125),
                                reads=[b_pSslot[slot]], writes=[b_sP[slot]])

                        def emit_PV1(g, par=par):
                            slot = g % 3
                            scol = slot * 1024
                            st, sp_ = g == 0, g == NKB - 1
                            p.add("tensor", lambda e, g=g, scol=scol, par=par, st=st, sp_=sp_: e.matmul(
                                pO[0:64, :], sVt[par][:, g * 128:g * 128 + 64],
                                sP[:, scol:scol + QC], start=st, stop=sp_,
                                tile_position=(0, 0)),
                                reads=[b_sV[par], b_sP[slot]],
                                writes=([b_pO] if st or sp_ else []))
                            p.add("tensor", lambda e, g=g, scol=scol, par=par, st=st, sp_=sp_: e.matmul(
                                pO[64:128, :], sVt[par][:, g * 128 + 64:g * 128 + 128],
                                sP[:, scol + QC:scol + 1024], start=st, stop=sp_,
                                tile_position=(0, 64)),
                                reads=[b_sV[par], b_sP[slot]],
                                writes=([b_pO] if st or sp_ else []))

                        for g in range(NKB + 2):
                            if g < NKB:
                                emit_S1(g)
                            if g >= 2:
                                emit_PV1(g - 2)
                        p.add("vector", lambda e, hp=hp, qc=qc: e.tensor_copy(
                            sctx_t[:, hp * TOK + qc * QC: hp * TOK + (qc + 1) * QC],
                            pO[:]),
                            reads=[b_pO], writes=[b_sctx])

            p.begin_phase()
            swo = ectx.enter_context(nc.sbuf_tensor("e_swo", [128, 8 * D], BF16))
            b_swo = p.buf()
            sb_b = ectx.enter_context(nc.sbuf_tensor("e_sbb", [1, 4 * D], BF16))
            b_sbb = p.buf()
            ssp = ectx.enter_context(nc.sbuf_tensor("e_ssp", [128, QC], F32))
            b_ssp = p.buf()
            for ic in range(8):
                p.dma("gpsimd", swo[:, ic * D:(ic + 1) * D],
                      wo1[ic * 128:(ic + 1) * 128, :], writes=[b_swo])
            p.dma("gpsimd", sb_b[:], b1, writes=[b_sbb])

            for hp in range(NHP):
                for qc in range(TOK // QC):
                    psl = slice((qc % 2) * QC, (qc % 2) * QC + QC)
                    for ic in range(8):
                        st = ic == 0
                        p.add("tensor", lambda e, ic=ic, hp=hp, qc=qc, psl=psl, st=st: e.matmul(
                            pS[:, psl], swo[:, ic * D + hp * 128: ic * D + (hp + 1) * 128],
                            sctx_t[:, ic * TOK + qc * QC: ic * TOK + (qc + 1) * QC],
                            start=st, stop=False),
                            reads=[b_swo, b_sctx], writes=([b_pSslot[0]] if st else []))
                    p.add("tensor", lambda e, hp=hp, psl=psl: e.matmul(
                        pS[:, psl], sb_b[0:1, 3 * D + hp * 128: 3 * D + (hp + 1) * 128],
                        ones_row[:], start=False, stop=True),
                        reads=[b_sbb, b_ones_row], writes=[b_pSslot[0]])
                    p.add("vector", lambda e, psl=psl: e.tensor_scalar(
                        ssp[:], pS[:, psl], 1.0, None, ALU.is_ge),
                        reads=[b_pSslot[0]], writes=[b_ssp])
                    p.add("vector", lambda e, qc=qc, hp=hp: e.tensor_add(
                        sx[:, hp * TOK + qc * QC: hp * TOK + (qc + 1) * QC],
                        sx[:, hp * TOK + qc * QC: hp * TOK + (qc + 1) * QC], ssp[:]),
                        reads=[b_ssp], writes=[b_sx])
            if debug:
                for ic in range(8):
                    p.dma("sync", dX3[ic * 128:(ic + 1) * 128, :],
                          sx[:, ic * TOK:(ic + 1) * TOK], reads=[b_sx])

        phase_E()

        p.begin_phase()
        def phase_F():
          with ExitStack() as fctx:
            add_moe_consts(nc, p, fctx, C)
            sel8, b_sel8 = C["sel8"]
            idf32, b_idf32 = C["idf32"]

            lnx = fctx.enter_context(nc.sbuf_tensor("f_lnx", [128, 8 * TOK], BF16))
            b_lnx = p.buf()
            sh = fctx.enter_context(nc.sbuf_tensor("f_sh", [128, 16 * 1024], BF16))
            b_sh = p.buf()
            srw = fctx.enter_context(nc.sbuf_tensor("f_srw", [128, 8 * 8], BF16))
            b_srw = p.buf()
            sdwT = fctx.enter_context(nc.sbuf_tensor("f_sdwT", [8, TOK], BF16))
            b_sdwT = p.buf()

            pG = fctx.enter_context(nc.psum_tensor("f_pG", [128, 4 * QC], F32))
            b_pG = p.buf()
            pU = fctx.enter_context(nc.psum_tensor("f_pU", [128, 2 * QC], F32))
            b_pU = p.buf()
            pBC = fctx.enter_context(nc.psum_tensor("f_pBC", [128, 2 * QC], F32))
            b_pBC = p.buf()

            for ic in range(8):
                p.dma("gpsimd", srw[:, ic * 8:(ic + 1) * 8],
                      rwT[ic * 128:(ic + 1) * 128, :], writes=[b_srw])

            p.constf(LN_EPS)
            with ExitStack() as lctx:
                lndone = ln_stats_normalize(nc, p, lctx, C, sx, b_sx, lnx, b_lnx, "f_ln",
                                            pU, b_pU, pBC, b_pBC)

            wall = fctx.enter_context(nc.sbuf_tensor("f_wall", [128, 88], F32))
            w8 = [wall[:, i * 8:(i + 1) * 8] for i in range(8)]
            b_w8 = [p.buf() for _ in range(8)]
            wm = [wall[:, 64 + i:65 + i] for i in range(2)]
            b_wm = [p.buf(), p.buf()]
            ssum = wall[:, 66:74]
            b_ssum = p.buf()
            srs = wall[:, 74:82]
            b_srs = p.buf()

            for tt in range(TOK // 128):
                tsl = slice(tt * 128, (tt + 1) * 128)
                for ic in range(8):
                    st = ic == 0
                    sp_ = ic == 7
                    p.add("tensor", lambda e, ic=ic, tsl=tsl, st=st, sp_=sp_: e.matmul(
                        pG[:, 0:8], lnx[:, ic * TOK:][:, tsl], srw[:, ic * 8:(ic + 1) * 8],
                        start=st, stop=sp_),
                        reads=[b_lnx, b_srw], writes=([b_pG] if st or sp_ else []))
                slg, as_, f1, d1, k1, g1, msk, oh = w8
                bslg, bas, bf1, bd1, bk1, bg1, bmsk, boh = b_w8
                m1, m2 = wm
                bm1, bm2 = b_wm
                A = lambda eng, fn, rd, wr: p.add(eng, fn, reads=rd, writes=wr)
                p.add("vector", lambda e: e.tensor_copy(slg[:], pG[:, 0:8]),
                      reads=[b_pG], writes=[bslg],
                      deps=(lndone if tt == 0 else ()))
                A("vector", lambda e: e.memset(ssum[:], 1.0), [], [b_ssum])
                from concourse.bass import broadcast_tensor_aps as _bc2
                def _maxb(out, a, mcol):
                    aa, mm = _bc2(a, mcol)
                    return lambda e: e.tensor_tensor(out, aa, mm, ALU.max)
                A("vector", lambda e: e.reduce_max(m1[:], slg[:], axis=AX.X), [bslg], [bm1])
                A("vector", lambda e: e.tensor_scalar(as_[:], slg[:], -1.0, None, ALU.mult), [bslg], [bas])
                A("vector", lambda e: e.tensor_max(as_[:], as_[:], slg[:]), [bslg, bas], [bas])
                A("vector", _maxb(f1[:], as_[:], m1[:]), [bas, bm1], [bf1])
                A("vector", lambda e: e.tensor_scalar(d1[:], slg[:], m1[:], None, ALU.subtract), [bslg, bm1], [bd1])
                A("vector", lambda e: e.scalar_tensor_tensor(k1[:], f1[:], 0.02, d1[:], ALU.mult, ALU.add), [bf1, bd1], [bk1])
                A("vector", lambda e: e.tensor_scalar(k1[:], k1[:], 0.0, None, ALU.is_ge), [bk1], [bk1])
                A("scalar", lambda e: e.activation(g1[:], d1[:], AF.Exp), [bd1], [bg1])
                A("vector", lambda e: e.tensor_mul(g1[:], g1[:], k1[:]), [bg1, bk1], [bg1])
                A("vector", lambda e: e.reduce_sum(ssum[:, 0:1], g1[:], axis=AX.X), [bg1, b_ssum], [b_ssum])
                A("vector", lambda e: e.tensor_scalar(oh[:], slg[:], m1[:], None, ALU.is_ge), [bslg, bm1], [boh])
                A("vector", lambda e: e.scalar_tensor_tensor(msk[:], oh[:], -BIG, slg[:], ALU.mult, ALU.add), [boh, bslg], [bmsk])
                A("vector", lambda e: e.reduce_max(m2[:], msk[:], axis=AX.X), [bmsk], [bm2])
                A("vector", _maxb(f1[:], as_[:], m2[:]), [bas, bm2], [bf1])
                A("vector", lambda e: e.tensor_scalar(d1[:], slg[:], m2[:], None, ALU.subtract), [bslg, bm2], [bd1])
                A("vector", lambda e: e.scalar_tensor_tensor(k1[:], f1[:], 0.02, d1[:], ALU.mult, ALU.add), [bf1, bd1], [bk1])
                A("vector", lambda e: e.tensor_scalar(k1[:], k1[:], 0.0, None, ALU.is_ge), [bk1], [bk1])
                A("vector", lambda e: e.tensor_scalar(d1[:], msk[:], m2[:], None, ALU.subtract), [bmsk, bm2], [bd1])
                A("scalar", lambda e: e.activation(g1[:], d1[:], AF.Exp), [bd1], [bg1])
                A("vector", lambda e: e.tensor_mul(g1[:], g1[:], k1[:]), [bg1, bk1], [bg1])
                A("vector", lambda e: e.reduce_sum(ssum[:, 1:2], g1[:], axis=AX.X), [bg1, b_ssum], [b_ssum])
                A("vector", lambda e: e.tensor_scalar(msk[:], msk[:], m2[:], None, ALU.is_ge), [bmsk, bm2], [bmsk])
                A("vector", lambda e: e.reciprocal(srs[:], ssum[:]), [b_ssum], [b_srs])
                A("vector", lambda e: e.tensor_scalar(oh[:], oh[:], srs[:, 0:1], None, ALU.mult), [boh, b_srs], [boh])
                A("vector", lambda e: e.scalar_tensor_tensor(slg[:], msk[:], srs[:, 1:2], oh[:], ALU.mult, ALU.add), [bmsk, b_srs, boh], [bslg])
                p.add("tensor", lambda e: e.transpose(pU[0:8, 0:128], slg[:], idf32[:]),
                      reads=[bslg, b_idf32], writes=[b_pU])
                p.add("vector", lambda e, tsl=tsl: e.tensor_copy(sdwT[:, tsl], pU[0:8, 0:128]),
                      reads=[b_pU], writes=[b_sdwT])
            if debug:
                p.dma("gpsimd", dDW, sdwT[:], reads=[b_sdwT])
                p.dma("sync", dLN, lnx[:, 0:TOK], reads=[b_lnx])

            from concourse.bass import broadcast_tensor_aps as _bcast
            swg = [fctx.enter_context(nc.sbuf_tensor(f"f_swg{i}", [128, 8 * QC], BF16))
                   for i in range(2)]
            swu = [fctx.enter_context(nc.sbuf_tensor(f"f_swu{i}", [128, 8 * QC], BF16))
                   for i in range(2)]
            b_swg = [p.buf(), p.buf()]
            b_swu = [p.buf(), p.buf()]
            swd = fctx.enter_context(nc.sbuf_tensor("f_swd", [128, 16 * 1024], BF16))
            b_swd = p.buf()
            ssg = fctx.enter_context(nc.sbuf_tensor("f_ssg", [128, 2 * QC], BF16))
            b_ssg = p.buf()
            sdwB = fctx.enter_context(nc.sbuf_tensor("f_sdwB", [128, 1024], BF16))
            b_sdwB = p.buf()
            scc = fctx.enter_context(nc.sbuf_tensor("f_scc", [128, 2 * QC], BF16))
            b_scc = p.buf()
            sxv = sx[:].rearrange("p (c t) -> p c t", c=8)
            shv = sh[:].rearrange("p (f t) -> p f t", f=16)

            for e_i in range(8):
                for th in range(2):
                    for fb in range(4):
                        par = (e_i * 8 + th * 4 + fb) % 2
                        for ic in range(8):
                            p.dma("gpsimd", swg[par][:, ic * QC:(ic + 1) * QC],
                                  wgT[e_i * D + ic * 128: e_i * D + (ic + 1) * 128,
                                      fb * QC:(fb + 1) * QC], writes=[b_swg[par]],
                                  deps=(lndone if e_i == 0 and th == 0 and fb == 0
                                        and ic == 0 else ()))
                            p.dma("gpsimd", swu[par][:, ic * QC:(ic + 1) * QC],
                                  wuT[e_i * D + ic * 128: e_i * D + (ic + 1) * 128,
                                      fb * QC:(fb + 1) * QC], writes=[b_swu[par]])
                        if th == 0 and fb == 0:
                            for fc in range(16):
                                p.dma("gpsimd", swd[:, fc * 1024:(fc + 1) * 1024],
                                      wdT[e_i * 2048 + fc * 128: e_i * 2048 + (fc + 1) * 128, :],
                                      writes=[b_swd])
                        for tc in range(2):
                            tsl = slice(th * 1024 + tc * QC, th * 1024 + (tc + 1) * QC)
                            for ft in range(4):
                                ut, but, fo = (pU, b_pU, ft) if ft < 2 else (pBC, b_pBC, ft - 2)
                                for ic in range(8):
                                    st = ic == 0
                                    sp_ = ic == 7
                                    p.add("tensor", lambda e, ft=ft, ic=ic, tsl=tsl, st=st, sp_=sp_, par=par: e.matmul(
                                        pG[:, ft * QC:(ft + 1) * QC],
                                        swg[par][:, ic * QC + ft * 128: ic * QC + (ft + 1) * 128],
                                        lnx[:, ic * TOK:][:, tsl], start=st, stop=sp_),
                                        reads=[b_swg[par], b_lnx],
                                        writes=([b_pG] if st or sp_ else []))
                                    p.add("tensor", lambda e, fo=fo, ic=ic, ft=ft, tsl=tsl, st=st, sp_=sp_, ut=ut, but=but, par=par: e.matmul(
                                        ut[:, fo * QC:(fo + 1) * QC],
                                        swu[par][:, ic * QC + ft * 128: ic * QC + (ft + 1) * 128],
                                        lnx[:, ic * TOK:][:, tsl], start=st, stop=sp_),
                                        reads=[b_swu[par], b_lnx],
                                        writes=([but] if st or sp_ else []))
                            for half in range(2):
                                ut, but = (pU, b_pU) if half == 0 else (pBC, b_pBC)
                                p.add("scalar", lambda e, half=half: e.activation(
                                    ssg[:], pG[:, half * 2 * QC:(half + 1) * 2 * QC],
                                    AF.Silu),
                                    reads=[b_pG], writes=[b_ssg])
                                p.add("vector", lambda e, fb=fb, tc=tc, half=half, ut=ut: e.tensor_mul(
                                    shv[:, fb * 4 + half * 2: fb * 4 + half * 2 + 2,
                                        tc * QC:(tc + 1) * QC],
                                    ssg[:].rearrange("p (f t) -> p f t", f=2),
                                    ut[:, 0:2 * QC].rearrange("p (f t) -> p f t", f=2)),
                                    reads=[b_ssg, but], writes=[b_sh])
                    if debug and e_i == 0 and th == 0:
                        p.dma("sync", dH, sh[:, 0:1024], reads=[b_sh])
                    for j in range(2):
                        p.add("tensor", lambda e, e_i=e_i, th=th, j=j: e.matmul(
                            pU[:, 0:QC], sel8[:, e_i * 128:(e_i + 1) * 128],
                            sdwT[0:8, th * 1024 + j * QC: th * 1024 + (j + 1) * QC],
                            start=True, stop=True),
                            reads=[b_sdwT, b_sel8], writes=[b_pU])
                        p.add("vector", lambda e, j=j: e.tensor_copy(
                            sdwB[:, j * QC:(j + 1) * QC], pU[:, 0:QC]),
                            reads=[b_pU], writes=[b_sdwB])
                    dbanks = [(pG, b_pG, 0), (pG, b_pG, 1), (pG, b_pG, 2), (pG, b_pG, 3),
                              (pU, b_pU, 0), (pU, b_pU, 1), (pBC, b_pBC, 0), (pBC, b_pBC, 1)]
                    for t8 in range(2):
                        t0 = t8 * QC
                        tg = th * 1024 + t0
                        for fc in range(16):
                            st = fc == 0
                            sp_ = fc == 15
                            for dt_ in range(8):
                                yt, b_yt, bi = dbanks[dt_]
                                p.add("tensor", lambda e, fc=fc, dt_=dt_, t0=t0, st=st, sp_=sp_, yt=yt, bi=bi: e.matmul(
                                    yt[:, bi * QC:(bi + 1) * QC],
                                    swd[:, fc * 1024 + dt_ * 128: fc * 1024 + (dt_ + 1) * 128],
                                    shv[:, fc, t0:t0 + QC], start=st, stop=sp_),
                                    reads=[b_swd, b_sh],
                                    writes=([b_yt] if st or sp_ else []))
                        a2w = sdwB[:, t0:t0 + QC].rearrange("p (a t) -> p a t", a=1)
                        a0 = scc[:].rearrange("p (c t) -> p c t", c=2)

                        def scale_add(src2, b_src, cbase, tg=tg, a0=a0, a2w=a2w):
                            _, a2 = _bcast(src2, a2w)
                            p.add("vector", lambda e, a0=a0, src2=src2, a2=a2: e.tensor_tensor(
                                a0, src2, a2, ALU.mult),
                                reads=[b_src, b_sdwB], writes=[b_scc])
                            p.add("vector", lambda e, a0=a0, cbase=cbase, tg=tg: e.tensor_add(
                                sxv[:, cbase:cbase + 2, tg:tg + QC],
                                sxv[:, cbase:cbase + 2, tg:tg + QC], a0),
                                reads=[b_scc], writes=[b_sx])

                        scale_add(pG[:, 0:2 * QC].rearrange("p (c t) -> p c t", c=2), b_pG, 0)
                        if debug and e_i == 0 and th == 0 and t8 == 0:
                            p.dma("gpsimd", dSB, sdwB[:, 0:256], reads=[b_sdwB])
                            p.dma("gpsimd", dSC, scc[:, 0:1024], reads=[b_scc])
                        scale_add(pG[:, 2 * QC:4 * QC].rearrange("p (c t) -> p c t", c=2), b_pG, 2)
                        scale_add(pU[:, 0:2 * QC].rearrange("p (c t) -> p c t", c=2), b_pU, 4)
                        scale_add(pBC[:, 0:2 * QC].rearrange("p (c t) -> p c t", c=2), b_pBC, 6)

            for ic in range(8):
                p.dma("sync", oX4[ic * 128:(ic + 1) * 128, :],
                      sx[:, ic * TOK:(ic + 1) * TOK], reads=[b_sx])

        phase_F()
        p.emit(ctx)
    return nc



def _perm_deinterleave():
    perm = np.empty(D, np.int64)
    for h in range(H):
        for i in range(32):
            perm[h * 64 + i] = h * 64 + 2 * i
            perm[h * 64 + 32 + i] = h * 64 + 2 * i + 1
    return perm


def host_prep(inp):
    d = {}
    x = np.asarray(inp["L0_attn_hidden_states"])
    d["xfm"] = np.ascontiguousarray(x.transpose(0, 2, 1))
    perm = _perm_deinterleave()
    d["wq0"] = np.ascontiguousarray(np.asarray(inp["L0_attn_Wq"])[:, perm])
    d["wk0"] = np.ascontiguousarray(np.asarray(inp["L0_attn_Wk"])[:, perm])
    d["wv0"] = np.ascontiguousarray(np.asarray(inp["L0_attn_Wv"]))
    d["bqkv0"] = np.ascontiguousarray(np.stack([
        np.asarray(inp["L0_attn_bq"])[perm],
        np.asarray(inp["L0_attn_bk"])[perm],
        np.asarray(inp["L0_attn_bv"])]).reshape(1, 3 * D))
    d["wfcT"] = np.ascontiguousarray(np.asarray(inp["L0_ffn_W_fc"]).T)
    d["wprojT"] = np.ascontiguousarray(np.asarray(inp["L0_ffn_W_proj"]).T)
    d["lnw"] = np.asarray(inp["L0_ffn_ln_w"]).reshape(1, -1)
    d["lnb"] = np.asarray(inp["L0_ffn_ln_b"]).reshape(1, -1)
    d["wq1T"] = np.ascontiguousarray(np.asarray(inp["L1_attn_Wq"]).T)
    d["wk1T"] = np.ascontiguousarray(np.asarray(inp["L1_attn_Wk"]).T)
    d["wv1T"] = np.ascontiguousarray(np.asarray(inp["L1_attn_Wv"]).T)
    d["wo1T"] = np.ascontiguousarray(np.asarray(inp["L1_attn_Wo"]).T)
    d["b1"] = np.ascontiguousarray(np.stack([
        np.asarray(inp["L1_attn_bq"]), np.asarray(inp["L1_attn_bk"]),
        np.asarray(inp["L1_attn_bv"]), np.asarray(inp["L1_attn_bo"])]))
    d["rwT"] = np.ascontiguousarray(np.asarray(inp["L1_ffn_router_w"]).T)
    d["wgT"] = np.ascontiguousarray(np.asarray(inp["L1_ffn_w_gate"]).transpose(0, 2, 1))
    d["wuT"] = np.ascontiguousarray(np.asarray(inp["L1_ffn_w_up"]).transpose(0, 2, 1))
    d["wdT"] = np.ascontiguousarray(np.asarray(inp["L1_ffn_w_down"]).transpose(0, 2, 1))
    return d


_NC_CACHE = {}


def kernel(**inputs):
    debug = bool(inputs.pop("_debug", False))
    d = host_prep(inputs)
    key = ("fused", debug)
    if key not in _NC_CACHE:
        _NC_CACHE[key] = build_fused(debug=debug)
    nc = _NC_CACHE[key]
    in_maps = []
    for c in range(NC):
        b, q = c // 4, c % 4
        in_maps.append(dict(
            xin=np.ascontiguousarray(d["xfm"][b][:, q * TOK:(q + 1) * TOK]),
            pos0=np.array([[q * TOK]], dtype=np.float32),
            wq0=d["wq0"], wk0=d["wk0"], wv0=d["wv0"], bqkv=d["bqkv0"],
            wfcT=d["wfcT"], wprojT=d["wprojT"], lnw=d["lnw"], lnb=d["lnb"],
            wq1=d["wq1T"], wk1=d["wk1T"], wv1=d["wv1T"], wo1=d["wo1T"],
            b1=d["b1"].reshape(1, 4 * D),
            rwT=d["rwT"],
            wgT=d["wgT"].reshape(8 * D, 2048),
            wuT=d["wuT"].reshape(8 * D, 2048),
            wdT=d["wdT"].reshape(8 * 2048, D),
        ))
    res = run_bass_kernel_spmd(nc, in_maps, core_ids=list(range(NC)))
    out = np.empty((BN, SEQ, D), np.float32)
    for c in range(NC):
        b, q = c // 4, c % 4
        out[b, q * TOK:(q + 1) * TOK, :] = np.asarray(res.results[c]["oX4"]).T
    if debug:
        dbg = {k: [np.asarray(res.results[c][k]) for c in range(NC)]
               for k in ("dX0", "dX1", "dX2", "dX3", "dSM", "dBC", "dL",
                         "dQ", "dK", "dV", "dDW", "dLN", "dH", "dSB", "dSC")}
        return out, dbg
    return out


# version 12
# speedup vs baseline: 1.1927x; 1.1927x over previous
import math
import numpy as np
import ml_dtypes
import concourse.bass as bass
import concourse.mybir as mybir
from concourse.bass_utils import run_bass_kernel_spmd

F32 = mybir.dt.float32
BF16 = mybir.dt.bfloat16
AF = mybir.ActivationFunctionType
ALU = mybir.AluOpType
AX = mybir.AxisListType

BN, SEQ, D, H, HD = 2, 8192, 1024, 16, 64
TOK = 2048
NC = 8
NHP = 8
QC = 512
LN_EPS = 1e-5
FFN_LN_EPS = 1e-5
ABSMEAN_EPS = 1e-6
BIG = 1e9
RG = [[0, 1, 2, 3], [4, 5, 6, 7]]

ENGINES = ("sync", "gpsimd", "tensor", "vector", "scalar")
DMA_QUEUES = ("sync", "gpsimd")


class Buf:
    __slots__ = ("name", "last_write", "readers")

    def __init__(self, name=""):
        self.name = name
        self.last_write = None
        self.readers = []


class Prog:

    def __init__(self, nc, trust_engines=("tensor",)):
        self.nc = nc
        self.steps = {e: [] for e in ENGINES}
        self._consts = {}
        self._ctx = None
        self.trust_engines = set(trust_engines)
        self._phase_deps = None
        self._phase_seen = set()

    def buf(self, name=""):
        return Buf(name)

    def constf(self, value):
        key = float(value)
        if key not in self._consts:
            t = self._ctx.enter_context(
                self.nc.sbuf_tensor(f"c_f{len(self._consts)}", [1, 1], F32))
            b = Buf(f"constf_{key}")
            self.add("vector", lambda e: e.memset(t[:], key), writes=[b])
            self._consts[key] = (t, b)
        return self._consts[key]

    def begin_phase(self):
        self._phase_deps = [(e, len(self.steps[e]) - 1)
                            for e in ENGINES if self.steps[e]]
        self._phase_seen = set()

    def add(self, eng, fn, reads=(), writes=(), deps=(), dma=False):
        import traceback
        tag = traceback.extract_stack(limit=3)[0].lineno
        tok = (eng, len(self.steps[eng]))
        alldeps = list(deps)
        if self._phase_deps is not None and eng not in self._phase_seen:
            alldeps.extend(t for t in self._phase_deps if t[0] != eng or t[1] >= 0)
            self._phase_seen.add(eng)
            if len(self._phase_seen) == len(ENGINES):
                self._phase_deps = None
        for b in reads:
            if b.last_write is not None:
                alldeps.append(b.last_write)
        for b in writes:
            if b.last_write is not None:
                alldeps.append(b.last_write)
            alldeps.extend(b.readers)
        self.steps[eng].append(dict(fn=fn, deps=alldeps, dma=dma, inc=0, tag=tag))
        for b in reads:
            b.readers.append(tok)
        for b in writes:
            b.last_write = tok
            b.readers = []
        return tok

    def dma(self, q, out, in_, reads=(), writes=(), deps=()):
        return self.add(q, lambda e: e.dma_start(out=out, in_=in_),
                        reads=reads, writes=writes, deps=deps, dma=True)

    def emit(self, ctx):
        nc = self.nc
        sems = {e: ctx.enter_context(nc.semaphore("sem_" + e)) for e in ENGINES}
        for e in ENGINES:
            for s in self.steps[e]:
                if s["dma"]:
                    s["inc"] = 16
        for e in ENGINES:
            for i, s in enumerate(self.steps[e]):
                for (de, di) in s["deps"]:
                    if de == e and de in self.trust_engines:
                        continue
                    if de == e and di >= i:
                        raise RuntimeError(f"bad same-queue dep {de}:{di} >= {i}")
                    tgt = self.steps[de][di]
                    if tgt["inc"] == 0:
                        tgt["inc"] = 1
        cum = {}
        for e in ENGINES:
            c = 0
            cc = []
            for s in self.steps[e]:
                c += s["inc"]
                cc.append(c)
            cum[e] = cc
        blk = ctx.enter_context(nc.Block())

        no_dedup = True

        def emitter_for(ename):
            def emitter(eng):
                waited = {}
                for i, s in enumerate(self.steps[ename]):
                    waitv = {}
                    for (de, di) in s["deps"]:
                        if de == ename and de in self.trust_engines:
                            continue
                        if de == ename and di >= i:
                            continue
                        v = cum[de][di]
                        if v > waitv.get(de, 0):
                            waitv[de] = v
                    for de, v in sorted(waitv.items()):
                        if no_dedup or v > waited.get(de, 0):
                            eng.wait_ge(sems[de], v)
                            waited[de] = max(v, waited.get(de, 0))
                    try:
                        inst = s["fn"](eng)
                    except Exception:
                        import sys
                        print(f"EMIT FAIL {ename} step {i} tag={s.get('tag')}",
                              file=sys.stderr)
                        raise
                    if s["inc"]:
                        inst.then_inc(sems[ename], s["inc"])
                if ename in DMA_QUEUES and cum[ename] and cum[ename][-1] > 0:
                    eng.wait_ge(sems[ename], cum[ename][-1])
            return emitter

        blk.sync(emitter_for("sync"))
        blk.tensor(emitter_for("tensor"))
        blk.vector(emitter_for("vector"))
        blk.scalar(emitter_for("scalar"))
        blk.gpsimd(emitter_for("gpsimd"))



def build_consts(nc, p, ctx):
    C = {}

    def sb(name, shape, dtype):
        h = ctx.enter_context(nc.sbuf_tensor(name, shape, dtype))
        b = Buf(name)
        return h, b

    ones_col, b_ones_col = sb("c_ones_col", [128, 1], BF16)
    ones_row, b_ones_row = sb("c_ones_row", [1, QC], BF16)
    onesf_col, b_onesf_col = sb("c_onesf_col", [1, 128], F32)
    onesf_colb, b_onesf_colb = sb("c_onesf_colb", [1, 128], BF16)
    onesf_c128, b_onesf_c128 = sb("c_onesf_c128", [128, 8], F32)
    p.add("vector", lambda e: e.memset(ones_col[:], 1.0), writes=[b_ones_col])
    p.add("vector", lambda e: e.memset(ones_row[:], 1.0), writes=[b_ones_row])
    p.add("vector", lambda e: e.memset(onesf_col[:], 1.0), writes=[b_onesf_col])
    p.add("vector", lambda e: e.memset(onesf_colb[:], 1.0), writes=[b_onesf_colb])
    p.add("vector", lambda e: e.memset(onesf_c128[:], 1.0), writes=[b_onesf_c128])
    C["onesf_c128"] = (onesf_c128, b_onesf_c128)
    C["ones_col"] = (ones_col, b_ones_col)
    C["ones_row"] = (ones_row, b_ones_row)
    C["onesf_col"] = (onesf_col, b_onesf_col)
    C["onesf_colb"] = (onesf_colb, b_onesf_colb)
    return C


def _scr(nc, p, ctx, n):
    out = []
    for i in range(n):
        h = ctx.enter_context(nc.sbuf_tensor(f"scr{i}_{len(p.steps['vector'])}",
                                             [128, 128], F32))
        out.append((h, Buf(f"scr{i}")))
    return out


def add_rope_consts(nc, p, ctx, C):
    (u1, b_u1), (u2, b_u2), (u3, b_u3) = _scr(nc, p, ctx, 3)
    R = ctx.enter_context(nc.sbuf_tensor("c_R", [128, 128], BF16))
    b_R = Buf("c_R")
    p.add("gpsimd", lambda e: e.iota(u1[:], [[-1, 128]], base=0, channel_multiplier=1,
                                     allow_small_or_imprecise_dtypes=True), writes=[b_u1])
    p.add("gpsimd", lambda e: e.iota(u2[:].rearrange("p (a b) -> p a b", b=64),
                                     [[0, 2], [1, 64]], base=0, channel_multiplier=0,
                                     allow_small_or_imprecise_dtypes=True), writes=[b_u2])
    p.add("vector", lambda e: e.tensor_scalar(u3[:], u1[:], 32.0, None, ALU.is_equal),
          reads=[b_u1], writes=[b_u3])
    p.add("vector", lambda e: e.tensor_scalar(u2[:], u2[:], 32.0, None, ALU.is_lt),
          reads=[b_u2], writes=[b_u2])
    p.add("vector", lambda e: e.tensor_mul(u3[:], u3[:], u2[:]),
          reads=[b_u2, b_u3], writes=[b_u3])
    p.add("vector", lambda e: e.tensor_scalar(u1[:], u1[:], -32.0, None, ALU.is_equal),
          reads=[b_u1], writes=[b_u1])
    p.add("vector", lambda e: e.tensor_scalar(u2[:], u2[:], -1.0, 1.0, ALU.mult, ALU.add),
          reads=[b_u2], writes=[b_u2])
    p.add("vector", lambda e: e.tensor_mul(u1[:], u1[:], u2[:]),
          reads=[b_u1, b_u2], writes=[b_u1])
    p.add("vector", lambda e: e.tensor_sub(R[:], u1[:], u3[:]),
          reads=[b_u1, b_u3], writes=[b_R])
    C["R"] = (R, b_R)
    Dm = ctx.enter_context(nc.sbuf_tensor("c_D", [32, 128], F32))
    b_D = Buf("c_D")
    p.add("gpsimd", lambda e: e.iota(Dm[:].rearrange("p (a b) -> p a b", b=32),
                                     [[0, 4], [1, 32]], base=0, channel_multiplier=-1,
                                     allow_small_or_imprecise_dtypes=True), writes=[b_D])
    p.add("vector", lambda e: e.tensor_scalar(Dm[:], Dm[:], 0.0, None, ALU.is_equal),
          reads=[b_D], writes=[b_D])
    C["D"] = (Dm, b_D)


def add_E1_const(nc, p, ctx, C):
    (v1, b_v1), (v2, b_v2), (v3, b_v3), (v4, b_v4), (v5, b_v5) = _scr(nc, p, ctx, 5)
    E1 = ctx.enter_context(nc.sbuf_tensor("c_E1", [128, 128], F32))
    b_E1 = Buf("c_E1")
    p.add("gpsimd", lambda e: e.iota(v1[:], [[0, 128]], base=0, channel_multiplier=1,
                                     allow_small_or_imprecise_dtypes=True), writes=[b_v1])
    p.add("gpsimd", lambda e: e.iota(v2[:], [[1, 128]], base=0, channel_multiplier=0,
                                     allow_small_or_imprecise_dtypes=True), writes=[b_v2])
    p.add("vector", lambda e: e.tensor_scalar(v3[:], v1[:], 0.0, None, ALU.is_equal),
          reads=[b_v1], writes=[b_v3])
    p.add("vector", lambda e: e.tensor_scalar(v4[:], v2[:], 64.0, None, ALU.is_lt),
          reads=[b_v2], writes=[b_v4])
    p.add("vector", lambda e: e.tensor_mul(v3[:], v3[:], v4[:]),
          reads=[b_v4, b_v3], writes=[b_v3])
    p.add("vector", lambda e: e.tensor_scalar(v5[:], v1[:], 64.0, None, ALU.is_equal),
          reads=[b_v1], writes=[b_v5])
    p.add("vector", lambda e: e.tensor_scalar(v2[:], v4[:], -1.0, 1.0, ALU.mult, ALU.add),
          reads=[b_v4], writes=[b_v2])
    p.add("vector", lambda e: e.tensor_mul(v5[:], v5[:], v2[:]),
          reads=[b_v5, b_v2], writes=[b_v5])
    p.add("vector", lambda e: e.tensor_add(E1[:], v5[:], v3[:]),
          reads=[b_v5, b_v3], writes=[b_E1])
    C["E1"] = (E1, b_E1)


def add_moe_consts(nc, p, ctx, C):
    idf32 = ctx.enter_context(nc.sbuf_tensor("c_idf32", [128, 128], F32))
    b_id = Buf("c_idf32")
    p.add("gpsimd", lambda e: e.iota(idf32[:], [[-1, 128]], base=0, channel_multiplier=1,
                                     allow_small_or_imprecise_dtypes=True), writes=[b_id])
    p.add("vector", lambda e: e.tensor_scalar(idf32[:], idf32[:], 0.0, None, ALU.is_equal),
          reads=[b_id], writes=[b_id])
    C["idf32"] = (idf32, b_id)
    S8 = ctx.enter_context(nc.sbuf_tensor("c_sel8", [8, 8 * 128], BF16))
    b_S8 = Buf("c_sel8")
    p.add("gpsimd", lambda e: e.iota(S8[:].rearrange("p (a b) -> p a b", b=128),
                                     [[1, 8], [0, 128]], base=0, channel_multiplier=-1,
                                     allow_small_or_imprecise_dtypes=True), writes=[b_S8])
    p.add("vector", lambda e: e.tensor_scalar(S8[:], S8[:], 0.0, None, ALU.is_equal),
          reads=[b_S8], writes=[b_S8])
    C["sel8"] = (S8, b_S8)


def ln_stats_normalize(nc, p, ctx, C, sx, b_sx, lnx, b_lnx, tag, pstat, b_pstat,
                       pbc, b_pbc, eps=LN_EPS):
    onesf_col = C["onesf_col"][0]
    b_onesf = C["onesf_col"][1]
    ones_colb, b_ones_colb = C["ones_col"]

    xb = ctx.enter_context(nc.sbuf_tensor(f"{tag}_xb", [128, 2 * QC], BF16))
    xs = ctx.enter_context(nc.sbuf_tensor(f"{tag}_xs", [128, 2 * QC], BF16))
    b_xb, b_xs = p.buf(), p.buf()
    row = ctx.enter_context(nc.sbuf_tensor(f"{tag}_row", [1, QC], F32))
    row2 = ctx.enter_context(nc.sbuf_tensor(f"{tag}_row2", [1, QC], F32))
    rowm = ctx.enter_context(nc.sbuf_tensor(f"{tag}_rowm", [1, QC], F32))
    b_row, b_row2, b_rowm = p.buf(), p.buf(), p.buf()
    tmp = ctx.enter_context(nc.sbuf_tensor(f"{tag}_tmp", [128, QC], F32))
    b_tmp = p.buf()

    for qc in range(TOK // QC):
        for ic in range(8):
            ssl = slice(ic * TOK + qc * QC, ic * TOK + (qc + 1) * QC)
            dsl = slice((ic % 2) * QC, (ic % 2) * QC + QC)
            st = ic == 0
            sp = ic == 7
            p.add("vector", lambda e, ssl=ssl, dsl=dsl: e.tensor_copy(xb[:, dsl], sx[:, ssl]),
                  reads=[b_sx], writes=[b_xb])
            p.add("scalar", lambda e, ssl=ssl, dsl=dsl: e.activation(
                xs[:, dsl], sx[:, ssl], AF.Square),
                reads=[b_sx], writes=[b_xs])
            p.add("tensor", lambda e, dsl=dsl, st=st, sp=sp: e.matmul(
                pstat[0:1, 0:QC], ones_colb[:], xb[:, dsl], start=st, stop=sp,
                tile_position=(0, 0)),
                reads=[b_xb, b_ones_colb], writes=([b_pstat] if sp else []))
            p.add("tensor", lambda e, dsl=dsl, st=st, sp=sp: e.matmul(
                pstat[32:33, QC:2 * QC], ones_colb[:], xs[:, dsl], start=st, stop=sp,
                tile_position=(0, 32)),
                reads=[b_xs, b_ones_colb], writes=([b_pstat] if sp else []))
        p.add("vector", lambda e: e.tensor_scalar(
            rowm[:], pstat[0:1, 0:QC], 1.0 / D, None, ALU.mult),
            reads=[b_pstat], writes=[b_rowm])
        p.add("vector", lambda e: e.tensor_scalar(
            row[:], pstat[32:33, QC:2 * QC], 1.0 / D, None, ALU.mult),
            reads=[b_pstat], writes=[b_row])
        p.add("vector", lambda e: e.scalar_tensor_tensor(
            row2[:], rowm[:], -1.0, rowm[:], ALU.mult, ALU.mult),
            reads=[b_rowm], writes=[b_row2])
        p.add("vector", lambda e: e.tensor_add(
            row[:], row[:], row2[:]),
            reads=[b_row, b_row2], writes=[b_row])
        ceps, b_ceps = p.constf(eps)
        p.add("scalar", lambda e: e.activation(
            row[:], row[:], AF.Ln, bias=ceps[:]),
            reads=[b_row, b_ceps], writes=[b_row])
        p.add("scalar", lambda e: e.activation(
            row[:], row[:], AF.Exp, scale=-0.5),
            reads=[b_row], writes=[b_row])
        p.add("vector", lambda e: e.scalar_tensor_tensor(
            row2[:], rowm[:], -1.0, row[:], ALU.mult, ALU.mult),
            reads=[b_rowm, b_row], writes=[b_row2])
        p.add("tensor", lambda e: e.matmul(
            pbc[:, 0:QC], onesf_col[:], row[:], start=True, stop=True),
            reads=[b_row, b_onesf], writes=[b_pbc])
        p.add("tensor", lambda e: e.matmul(
            pbc[:, QC:2 * QC], onesf_col[:], row2[:], start=True, stop=True),
            reads=[b_row2, b_onesf], writes=[b_pbc])
        for ic in range(8):
            csl = slice(ic * TOK + qc * QC, ic * TOK + (qc + 1) * QC)
            p.add("vector", lambda e, csl=csl: e.tensor_mul(
                tmp[:], sx[:, csl], pbc[:, 0:QC]),
                reads=[b_sx, b_pbc], writes=[b_tmp])
            tok_v = p.add("vector", lambda e, csl=csl: e.tensor_add(
                lnx[:, csl], tmp[:], pbc[:, QC:2 * QC]),
                reads=[b_tmp, b_pbc], writes=[b_lnx])
        tok_s = p.add("scalar", lambda e: e.activation(row[:], row[:], AF.Identity),
                      reads=[b_row], writes=[b_row])
        tok_t = p.add("tensor", lambda e: e.matmul(
            pbc[:, 0:QC], onesf_col[:], row[:], start=True, stop=True),
            reads=[b_row, b_onesf], writes=[b_pbc])
    return [tok_v, tok_s, tok_t]


QCH = 512


def make_quant_shared(nc, p, ctx, tag):
    sw = [ctx.enter_context(nc.sbuf_tensor(f"{tag}_sw{i}", [128, QCH], F32))
          for i in range(2)]
    swq = [ctx.enter_context(nc.sbuf_tensor(f"{tag}_swq{i}", [128, QCH], BF16))
           for i in range(2)]
    ta = ctx.enter_context(nc.sbuf_tensor(f"{tag}_ta", [128, QCH], F32))
    return dict(sw=sw, b_sw=[p.buf(), p.buf()], swq=swq,
                b_swq=[p.buf(), p.buf()], ta=ta, b_ta=p.buf())


def make_quant(nc, p, ctx, C, sh_, src_dram, n_rows, n_cols, scratch, b_scratch,
               tag, pg, b_pg):
    onesf_col, b_onesf = C["onesf_col"]
    onesf_c128, b_onesf128 = C["onesf_c128"]
    sw, b_sw = sh_["sw"], sh_["b_sw"]
    swq, b_swq = sh_["swq"], sh_["b_swq"]
    ta, b_ta = sh_["ta"], sh_["b_ta"]
    nch = n_rows // 128
    nhc = n_cols // QCH
    chunks = [(ic, hc) for ic in range(nch) for hc in range(nhc)]
    sabs = ctx.enter_context(nc.sbuf_tensor(f"{tag}_sabs", [128, max(8, len(chunks))], F32))
    b_sabs = p.buf()
    srow = ctx.enter_context(nc.sbuf_tensor(f"{tag}_srow", [128, 24], F32))
    sat, sthr, snthr = srow[:, 0:8], srow[:, 8:16], srow[:, 16:24]
    b_sat, b_sthr, b_snthr = p.buf(), p.buf(), p.buf()
    sg8 = ctx.enter_context(nc.sbuf_tensor(f"{tag}_sg8", [1, 8], F32))
    b_sg8 = p.buf()

    def pass1():
        for i, (ic, hc) in enumerate(chunks):
            par = i % 2
            p.dma("gpsimd", sw[par][:],
                  src_dram[ic * 128:(ic + 1) * 128, hc * QCH:(hc + 1) * QCH],
                  writes=[b_sw[par]])
            p.add("vector", lambda e, i=i, par=par: e.reduce_sum(
                sabs[:, i:i + 1], sw[par][:], axis=AX.X, apply_absolute_value=True),
                reads=[b_sw[par]], writes=[b_sabs])

    def mid():
        p.add("vector", lambda e: e.reduce_sum(sat[:, 0:1], sabs[:, 0:len(chunks)],
                                               axis=AX.X),
              reads=[b_sabs], writes=[b_sat])
        p.add("tensor", lambda e: e.matmul(pg[0:1, 0:8], sat[:, 0:1], onesf_c128[:, 0:8],
                                           start=True, stop=True),
              reads=[b_sat, b_onesf128], writes=[b_pg])
        ntot = float(n_rows * n_cols)
        p.add("vector", lambda e: e.tensor_scalar(sg8[:], pg[0:1, 0:8], 0.5 / ntot,
                                                  0.5e-6, ALU.mult, ALU.add),
              reads=[b_pg], writes=[b_sg8])
        p.add("tensor", lambda e: e.matmul(pg[:, 0:8], onesf_col[:], sg8[:],
                                           start=True, stop=True),
              reads=[b_sg8, b_onesf], writes=[b_pg])
        p.add("vector", lambda e: e.tensor_copy(sthr[:], pg[:, 0:8]),
              reads=[b_pg], writes=[b_sthr])
        p.add("vector", lambda e: e.tensor_scalar(snthr[:], sthr[:], -1.0, None, ALU.mult),
              reads=[b_sthr], writes=[b_snthr])

    def pass2():
        for i, (ic, hc) in enumerate(chunks):
            par = i % 2
            p.dma("gpsimd", sw[par][:],
                  src_dram[ic * 128:(ic + 1) * 128, hc * QCH:(hc + 1) * QCH],
                  writes=[b_sw[par]])
            p.add("vector", lambda e, par=par: e.tensor_scalar(
                ta[:], sw[par][:], snthr[:, 0:1], None, ALU.is_le),
                reads=[b_sw[par], b_snthr], writes=[b_ta])
            p.add("vector", lambda e, par=par: e.scalar_tensor_tensor(
                swq[par][:], sw[par][:], sthr[:, 0:1], ta[:], ALU.is_ge, ALU.subtract),
                reads=[b_sw[par], b_sthr, b_ta], writes=[b_swq[par]])
            p.dma("sync", scratch[ic * 128:(ic + 1) * 128, hc * QCH:(hc + 1) * QCH],
                  swq[par][:], reads=[b_swq[par]], writes=[b_scratch])

    return pass1, mid, pass2



def build_fused(debug=False):
    nc = bass.Bass("TRN2", target_bir_lowering=False)
    xin = nc.dram_tensor("xin", [D, TOK], F32, kind="ExternalInput").ap()
    pos0 = nc.dram_tensor("pos0", [1, 1], F32, kind="ExternalInput").ap()
    wq0 = nc.dram_tensor("wq0", [D, D], F32, kind="ExternalInput").ap()
    wk0 = nc.dram_tensor("wk0", [D, D], F32, kind="ExternalInput").ap()
    wv0 = nc.dram_tensor("wv0", [D, D], F32, kind="ExternalInput").ap()
    bqkv = nc.dram_tensor("bqkv", [1, 3 * D], F32, kind="ExternalInput").ap()
    wfcT = nc.dram_tensor("wfcT", [D, 4096], F32, kind="ExternalInput").ap()
    wprojT = nc.dram_tensor("wprojT", [4096, D], F32, kind="ExternalInput").ap()
    lnw = nc.dram_tensor("lnw", [1, 4096], F32, kind="ExternalInput").ap()
    lnb = nc.dram_tensor("lnb", [1, 4096], F32, kind="ExternalInput").ap()
    wq1 = nc.dram_tensor("wq1", [D, D], F32, kind="ExternalInput").ap()
    wk1 = nc.dram_tensor("wk1", [D, D], F32, kind="ExternalInput").ap()
    wv1 = nc.dram_tensor("wv1", [D, D], F32, kind="ExternalInput").ap()
    wo1 = nc.dram_tensor("wo1", [D, D], F32, kind="ExternalInput").ap()
    b1 = nc.dram_tensor("b1", [1, 4 * D], F32, kind="ExternalInput").ap()
    rwT = nc.dram_tensor("rwT", [D, 8], F32, kind="ExternalInput").ap()
    wgT = nc.dram_tensor("wgT", [8 * D, 2048], F32, kind="ExternalInput").ap()
    wuT = nc.dram_tensor("wuT", [8 * D, 2048], F32, kind="ExternalInput").ap()
    wdT = nc.dram_tensor("wdT", [8 * 2048, D], F32, kind="ExternalInput").ap()
    oX4 = nc.dram_tensor("oX4", [D, TOK], F32, kind="ExternalOutput").ap()
    if debug:
        dX1 = nc.dram_tensor("dX1", [D, TOK], F32, kind="ExternalOutput").ap()
        dX2 = nc.dram_tensor("dX2", [D, TOK], F32, kind="ExternalOutput").ap()
        dX3 = nc.dram_tensor("dX3", [D, TOK], F32, kind="ExternalOutput").ap()
        dX0 = nc.dram_tensor("dX0", [D, TOK], F32, kind="ExternalOutput").ap()
        dSM = nc.dram_tensor("dSM", [1, 104], F32, kind="ExternalOutput").ap()
        dBC = nc.dram_tensor("dBC", [128, 1], F32, kind="ExternalOutput").ap()
        dL = nc.dram_tensor("dL", [128, QC], F32, kind="ExternalOutput").ap()
        dQ = nc.dram_tensor("dQ", [128, QC], BF16, kind="ExternalOutput").ap()
        dK = nc.dram_tensor("dK", [128, QC], BF16, kind="ExternalOutput").ap()
        dV = nc.dram_tensor("dV", [128, QC], BF16, kind="ExternalOutput").ap()
        dDW = nc.dram_tensor("dDW", [8, TOK], F32, kind="ExternalOutput").ap()
        dLN = nc.dram_tensor("dLN", [128, TOK], BF16, kind="ExternalOutput").ap()
        dH = nc.dram_tensor("dH", [128, 1024], BF16, kind="ExternalOutput").ap()
        dSB = nc.dram_tensor("dSB", [128, 256], F32, kind="ExternalOutput").ap()
        dSC = nc.dram_tensor("dSC", [128, 1024], F32, kind="ExternalOutput").ap()
    q0 = nc.dram_tensor("q0", [D, TOK], BF16, kind="Internal").ap()
    k0in = nc.dram_tensor("k0in", [D, TOK], BF16, kind="Internal").ap()
    v0in = nc.dram_tensor("v0in", [NHP, TOK, 128], BF16, kind="Internal").ap()
    s0in = nc.dram_tensor("s0in", [1, 16], BF16, kind="Internal").ap()
    k0out = nc.dram_tensor("k0out", [4, D, TOK], BF16, kind="Internal").ap()
    v0out = nc.dram_tensor("v0out", [4, NHP, TOK, 128], BF16, kind="Internal").ap()
    s0out = nc.dram_tensor("s0out", [4, 1, 16], BF16, kind="Internal").ap()
    q1 = nc.dram_tensor("q1", [D, TOK], BF16, kind="Internal").ap()
    k1in = nc.dram_tensor("k1in", [D, TOK], BF16, kind="Internal").ap()
    v1in = nc.dram_tensor("v1in", [NHP, TOK, 128], BF16, kind="Internal").ap()
    k1out = nc.dram_tensor("k1out", [4, D, TOK], BF16, kind="Internal").ap()
    v1out = nc.dram_tensor("v1out", [4, NHP, TOK, 128], BF16, kind="Internal").ap()
    wfcq = nc.dram_tensor("wfcq", [D, 4096], BF16, kind="Internal").ap()
    wprojq = nc.dram_tensor("wprojq", [4096, D], BF16, kind="Internal").ap()

    NKB = SEQ // 128
    from contextlib import ExitStack
    with ExitStack() as ctx:
        p = Prog(nc)
        p._ctx = ctx
        C = build_consts(nc, p, ctx)
        p.constf(LN_EPS)
        p.constf(FFN_LN_EPS)
        ones_row, b_ones_row = C["ones_row"]
        onesf_col, b_onesf = C["onesf_col"]
        onesf_colb, b_onesf_colb = C["onesf_colb"]
        ones_colb, b_ones_colb = C["ones_col"]

        sx = ctx.enter_context(nc.sbuf_tensor("sx", [128, 8 * TOK], F32))
        b_sx = p.buf("sx")
        nacc = ctx.enter_context(nc.sbuf_tensor("nacc", [1, 32], F32))
        b_nacc = p.buf("nacc")
        b_q0, b_k0in, b_v0in, b_s0in = p.buf(), p.buf(), p.buf(), p.buf()
        b_k0out, b_v0out, b_s0out = p.buf(), p.buf(), p.buf()
        b_q1, b_k1in, b_v1in = p.buf(), p.buf(), p.buf()
        b_k1out, b_v1out = p.buf(), p.buf()
        b_wfcq, b_wprojq = p.buf("wfcq"), p.buf("wprojq")

        def phase_A():
          with ExitStack() as actx:
            for ic in range(8):
                p.dma("sync", sx[:, ic * TOK:(ic + 1) * TOK],
                      xin[ic * 128:(ic + 1) * 128, :], writes=[b_sx])
            swq = actx.enter_context(nc.sbuf_tensor("a_swq", [128, 8 * D], BF16))
            swk = actx.enter_context(nc.sbuf_tensor("a_swk", [128, 8 * D], BF16))
            swv = actx.enter_context(nc.sbuf_tensor("a_swv", [128, 8 * D], BF16))
            b_swq, b_swk, b_swv = p.buf(), p.buf(), p.buf()
            sb_b = actx.enter_context(nc.sbuf_tensor("a_sbb", [1, 3 * D], BF16))
            b_sbb = p.buf()
            lnx = actx.enter_context(nc.sbuf_tensor("a_lnx", [128, 8 * TOK], BF16))
            b_lnx = p.buf()
            for ic in range(8):
                p.dma("gpsimd", swq[:, ic * D:(ic + 1) * D],
                      wq0[ic * 128:(ic + 1) * 128, :], writes=[b_swq])
                p.dma("gpsimd", swk[:, ic * D:(ic + 1) * D],
                      wk0[ic * 128:(ic + 1) * 128, :], writes=[b_swk])
                p.dma("gpsimd", swv[:, ic * D:(ic + 1) * D],
                      wv0[ic * 128:(ic + 1) * 128, :], writes=[b_swv])
            p.dma("gpsimd", sb_b[:], bqkv, writes=[b_sbb])

            add_rope_consts(nc, p, actx, C)
            C2 = actx.enter_context(nc.sbuf_tensor("a_C2", [128, TOK], F32))
            S2 = actx.enter_context(nc.sbuf_tensor("a_S2", [128, TOK], F32))
            b_C2, b_S2 = p.buf(), p.buf()
            tctx = ExitStack()
            inv = tctx.enter_context(nc.sbuf_tensor("a_inv", [32, 1], F32))
            b_inv = p.buf()
            ramp = tctx.enter_context(nc.sbuf_tensor("a_ramp", [1, QC], F32))
            b_ramp = p.buf()
            spos = tctx.enter_context(nc.sbuf_tensor("a_spos", [1, 1], F32))
            b_spos = p.buf()
            ang = tctx.enter_context(nc.sbuf_tensor("a_ang", [32, QC], F32))
            wk1_ = tctx.enter_context(nc.sbuf_tensor("a_wk1", [32, QC], F32))
            wk2_ = tctx.enter_context(nc.sbuf_tensor("a_wk2", [32, QC], F32))
            b_ang, b_wk1, b_wk2 = p.buf(), p.buf(), p.buf()
            pA = actx.enter_context(nc.psum_tensor("a_pA", [128, 2 * QC], F32))
            pB = actx.enter_context(nc.psum_tensor("a_pB", [128, 2 * QC], F32))
            pC = actx.enter_context(nc.psum_tensor("a_pC", [128, 2 * QC], F32))
            pD = actx.enter_context(nc.psum_tensor("a_pD", [128, 2 * QC], F32))
            b_pA, b_pB, b_pC, b_pD = p.buf("pA"), p.buf("pB"), p.buf("pC"), p.buf("pD")
            R, b_R = C["R"]
            Dm, b_D = C["D"]
            prt, b_prt = pA, b_pA

            p.dma("sync", spos[:], pos0, writes=[b_spos])
            p.add("gpsimd", lambda e: e.iota(inv[:], [[0, 1]], base=0, channel_multiplier=1,
                                             allow_small_or_imprecise_dtypes=True),
                  writes=[b_inv])
            p.add("scalar", lambda e: e.activation(inv[:], inv[:], AF.Exp,
                                                   scale=-math.log(10000.0) / 32.0),
                  reads=[b_inv], writes=[b_inv])

            TWO_PI = 2.0 * math.pi
            CW_HI = 6.28125
            CW_LO = TWO_PI - CW_HI
            MAGIC = 1.5 * 2.0 ** 23

            for qc in range(TOK // QC):
                qsl = slice(qc * QC, (qc + 1) * QC)
                p.add("gpsimd", lambda e, qc=qc: e.iota(ramp[:], [[1, QC]], base=qc * QC,
                                                        channel_multiplier=0,
                                                        allow_small_or_imprecise_dtypes=True),
                      writes=[b_ramp])
                p.add("vector", lambda e: e.tensor_scalar(ramp[:], ramp[:], spos[:], None, ALU.add),
                      reads=[b_ramp, b_spos], writes=[b_ramp])
                p.add("tensor", lambda e: e.matmul(
                    prt[0:32, 0:QC], onesf_col[:, 0:32], ramp[:], start=True, stop=True),
                    reads=[b_ramp, b_onesf], writes=[b_prt])
                p.add("vector", lambda e: e.tensor_scalar(ang[:], prt[0:32, 0:QC], inv[:], None, ALU.mult),
                      reads=[b_prt, b_inv], writes=[b_ang])
                p.add("vector", lambda e: e.tensor_scalar(wk1_[:], ang[:], 1.0 / TWO_PI, MAGIC,
                                                          ALU.mult, ALU.add),
                      reads=[b_ang], writes=[b_wk1])
                p.add("vector", lambda e: e.tensor_scalar(wk1_[:], wk1_[:], MAGIC, None, ALU.subtract),
                      reads=[b_wk1], writes=[b_wk1])
                p.add("vector", lambda e: e.scalar_tensor_tensor(wk2_[:], wk1_[:], -CW_HI, ang[:],
                                                                 ALU.mult, ALU.add),
                      reads=[b_wk1, b_ang], writes=[b_wk2])
                p.add("vector", lambda e: e.scalar_tensor_tensor(wk2_[:], wk1_[:], -CW_LO, wk2_[:],
                                                                 ALU.mult, ALU.add),
                      reads=[b_wk1, b_wk2], writes=[b_wk2])
                p.add("scalar", lambda e: e.activation(ang[:], wk2_[:], AF.Sin),
                      reads=[b_wk2], writes=[b_ang])
                p.add("tensor", lambda e: e.matmul(prt[:, 0:QC], Dm[:], ang[:], start=True, stop=True),
                      reads=[b_ang, b_D], writes=[b_prt])
                p.add("vector", lambda e, qsl=qsl: e.tensor_copy(S2[:, qsl], prt[:, 0:QC]),
                      reads=[b_prt], writes=[b_S2])
                p.add("vector", lambda e: e.tensor_scalar(wk1_[:], wk2_[:], math.pi / 2.0, None, ALU.is_gt),
                      reads=[b_wk2], writes=[b_wk1])
                p.add("vector", lambda e: e.scalar_tensor_tensor(wk2_[:], wk1_[:], -TWO_PI, wk2_[:],
                                                                 ALU.mult, ALU.add),
                      reads=[b_wk1, b_wk2], writes=[b_wk2])
                p.add("vector", lambda e: e.tensor_scalar(wk2_[:], wk2_[:], math.pi / 2.0, None, ALU.add),
                      reads=[b_wk2], writes=[b_wk2])
                p.add("scalar", lambda e: e.activation(ang[:], wk2_[:], AF.Sin),
                      reads=[b_wk2], writes=[b_ang])
                p.add("tensor", lambda e: e.matmul(prt[:, 0:QC], Dm[:], ang[:], start=True, stop=True),
                      reads=[b_ang, b_D], writes=[b_prt])
                p.add("vector", lambda e, qsl=qsl: e.tensor_copy(C2[:, qsl], prt[:, 0:QC]),
                      reads=[b_prt], writes=[b_C2])

            ln_stats_normalize(nc, p, tctx, C, sx, b_sx, lnx, b_lnx, "a_ln",
                               pB, b_pB, pC, b_pC)
            tctx.close()
            p.begin_phase()

            stg = actx.enter_context(nc.sbuf_tensor("a_stg", [128, TOK], BF16))
            b_stg = p.buf()
            qb = actx.enter_context(nc.sbuf_tensor("a_qb", [128, QC], BF16))
            b_qb = p.buf()
            tq = actx.enter_context(nc.sbuf_tensor("a_tq", [128, QC], F32))
            tr = actx.enter_context(nc.sbuf_tensor("a_tr", [128, QC], F32))
            b_tq, b_tr = p.buf(), p.buf()
            sq2 = actx.enter_context(nc.sbuf_tensor("a_sq2", [128, QC], BF16))
            b_sq2 = p.buf()
            nstrip = actx.enter_context(nc.sbuf_tensor("a_nstrip", [1, 128], F32))
            b_nstrip = p.buf()
            sk16 = actx.enter_context(nc.sbuf_tensor("a_sk16", [1, 16], BF16))
            b_sk16 = p.buf()

            pP, b_pP = pA, b_pA
            pR, b_pR = pB, b_pB
            pN, b_pN = pC, b_pC

            k_toks = []

            def emit_proj(kind):
                sw = [swq, swk][kind]
                b_sw = [b_swq, b_swk][kind]
                brow = kind
                out_t = [q0, k0in][kind]
                b_out = [b_q0, b_k0in][kind]
                for hp in range(NHP):
                    for qc in range(TOK // QC):
                        psl = slice((qc % 2) * QC, (qc % 2) * QC + QC)
                        qsl = slice(qc * QC, (qc + 1) * QC)
                        for ic in range(8):
                            st = ic == 0
                            p.add("tensor", lambda e, ic=ic, hp=hp, qsl=qsl, psl=psl, st=st, sw=sw: e.matmul(
                                pP[:, psl], sw[:, ic * D + hp * 128: ic * D + (hp + 1) * 128],
                                lnx[:, ic * TOK:][:, qsl], start=st, stop=False),
                                reads=[b_sw, b_lnx], writes=([b_pP] if st else []))
                        p.add("tensor", lambda e, hp=hp, psl=psl, brow=brow: e.matmul(
                            pP[:, psl], sb_b[0:1, brow * D + hp * 128: brow * D + (hp + 1) * 128],
                            ones_row[:], start=False, stop=True),
                            reads=[b_sbb, b_ones_row], writes=[b_pP])
                        p.add("scalar", lambda e, psl=psl: e.activation(qb[:], pP[:, psl], AF.Copy),
                              reads=[b_pP], writes=[b_qb])
                        p.add("tensor", lambda e, psl=psl: e.matmul(
                            pR[:, psl], R[:], qb[:], start=True, stop=True),
                            reads=[b_qb, b_R], writes=[b_pR])
                        p.add("vector", lambda e, qsl=qsl: e.tensor_mul(tq[:], qb[:], C2[:, qsl]),
                              reads=[b_qb, b_C2], writes=[b_tq])
                        p.add("vector", lambda e, qsl=qsl, psl=psl: e.tensor_mul(
                            tr[:], pR[:, psl], S2[:, qsl]),
                              reads=[b_pR, b_S2], writes=[b_tr])
                        p.add("vector", lambda e, qsl=qsl: e.tensor_add(stg[:, qsl], tq[:], tr[:]),
                              reads=[b_tq, b_tr], writes=[b_stg])
                        p.add("scalar", lambda e, qsl=qsl: e.activation(sq2[:], stg[:, qsl], AF.Square),
                              reads=[b_stg], writes=[b_sq2])
                        p.add("tensor", lambda e: e.matmul(
                            pN[0:1, 0:QC], ones_colb[0:64, :], sq2[0:64, :],
                            start=True, stop=True, tile_position=(0, 0)),
                            reads=[b_sq2, b_ones_colb], writes=[b_pN])
                        p.add("tensor", lambda e: e.matmul(
                            pN[0:1, QC:2 * QC], ones_colb[64:128, :], sq2[64:128, :],
                            start=True, stop=True, tile_position=(64, 0)),
                            reads=[b_sq2, b_ones_colb], writes=[b_pN])
                        colA = ((kind * 8 + hp) * 2 + 0) * 4 + qc
                        colB = ((kind * 8 + hp) * 2 + 1) * 4 + qc
                        p.add("vector", lambda e, colA=colA: e.reduce_max(
                            nstrip[:, colA:colA + 1], pN[0:1, 0:QC], axis=AX.X),
                            reads=[b_pN], writes=[b_nstrip])
                        p.add("vector", lambda e, colB=colB: e.reduce_max(
                            nstrip[:, colB:colB + 1], pN[0:1, QC:2 * QC], axis=AX.X),
                            reads=[b_pN], writes=[b_nstrip])
                    tk = p.dma("sync", out_t[hp * 128:(hp + 1) * 128, :], stg[:],
                               reads=[b_stg], writes=[b_out])
                    if kind == 1:
                        k_toks.append(tk)

            emit_proj(1)
            p.add("vector", lambda e: e.reduce_max(
                nacc[:, 16:32], nstrip[:, 64:128].rearrange("p (h q) -> p h q", q=4),
                axis=AX.X),
                reads=[b_nstrip], writes=[b_nacc])
            p.add("vector", lambda e: e.tensor_scalar(
                sk16[:], nacc[:, 16:32], 1.01, None, ALU.mult),
                reads=[b_nacc], writes=[b_sk16])
            s_tok = p.dma("sync", s0in, sk16[:], reads=[b_sk16], writes=[b_s0in])
            p.add("gpsimd", lambda e: e.collective_compute(
                "AllGather", ALU.bypass, ins=[s0in], outs=[s0out], replica_groups=RG),
                reads=[b_s0in], writes=[b_s0out], deps=[s_tok])
            p.add("gpsimd", lambda e: e.collective_compute(
                "AllGather", ALU.bypass, ins=[k0in], outs=[k0out], replica_groups=RG),
                reads=[b_k0in], writes=[b_k0out], deps=k_toks)

            pV, b_pV = pD, b_pD
            sV0 = actx.enter_context(nc.sbuf_tensor("a_sV", [128, D], BF16))
            b_sV0 = p.buf()
            v_toks = []
            for tt in range(TOK // 128):
                tsl = slice(tt * 128, (tt + 1) * 128)
                for half in range(2):
                    hsl = slice(half * QC, (half + 1) * QC)
                    for ic in range(8):
                        st = ic == 0
                        sp = ic == 7
                        p.add("tensor", lambda e, ic=ic, tsl=tsl, hsl=hsl, st=st, sp=sp: e.matmul(
                            pV[:, hsl], lnx[:, ic * TOK:][:, tsl],
                            swv[:, ic * D:][:, hsl], start=st, stop=sp),
                            reads=[b_lnx, b_swv], writes=([b_pV] if st else []))
                    p.add("tensor", lambda e, hsl=hsl: e.matmul(
                        pV[:, hsl], onesf_colb[:],
                        sb_b[0:1, 2 * D + hsl.start: 2 * D + hsl.stop],
                        start=False, stop=True),
                        reads=[b_sbb, b_onesf_colb], writes=[b_pV])
                p.add("scalar", lambda e: e.activation(sV0[:], pV[:, 0:D], AF.Copy),
                      reads=[b_pV], writes=[b_sV0])
                for hp in range(NHP):
                    v_toks.append(p.dma(
                        "sync", v0in[hp, tsl, :], sV0[:, hp * 128:(hp + 1) * 128],
                        reads=[b_sV0], writes=[b_v0in]))
            p.add("gpsimd", lambda e: e.collective_compute(
                "AllGather", ALU.bypass, ins=[v0in], outs=[v0out], replica_groups=RG),
                reads=[b_v0in], writes=[b_v0out], deps=v_toks)

            emit_proj(0)
            p.add("vector", lambda e: e.reduce_max(
                nacc[:, 0:16], nstrip[:, 0:64].rearrange("p (h q) -> p h q", q=4),
                axis=AX.X),
                reads=[b_nstrip], writes=[b_nacc])

        phase_A()

        p.begin_phase()
        def phase_B():
          with ExitStack() as bctx:
            add_E1_const(nc, p, bctx, C)
            E1, b_E1 = C["E1"]
            sQ = bctx.enter_context(nc.sbuf_tensor("b_sQ", [128, NHP * TOK], BF16))
            b_sQ = p.buf()
            sK = [bctx.enter_context(nc.sbuf_tensor(f"b_sK{i}", [128, SEQ], BF16)) for i in range(2)]
            sVt = [bctx.enter_context(nc.sbuf_tensor(f"b_sV{i}", [128, SEQ], BF16)) for i in range(2)]
            b_sK = [p.buf(), p.buf()]
            b_sV = [p.buf(), p.buf()]
            sP = bctx.enter_context(nc.sbuf_tensor("b_sP", [128, 3072], BF16))
            b_sP = [p.buf(), p.buf(), p.buf()]
            sl = bctx.enter_context(nc.sbuf_tensor("b_sl", [128, QC], F32))
            srl = bctx.enter_context(nc.sbuf_tensor("b_srl", [128, QC], F32))
            sO = bctx.enter_context(nc.sbuf_tensor("b_sO", [128, QC], F32))
            sC_ = bctx.enter_context(nc.sbuf_tensor("b_sC", [128, QC], F32))
            b_sl, b_srl, b_sO, b_sC = p.buf(), p.buf(), p.buf(), p.buf()
            smq = bctx.enter_context(nc.sbuf_tensor("b_smq", [1, 16], F32))
            smkb = bctx.enter_context(nc.sbuf_tensor("b_smkb", [1, 64], BF16))
            smk = bctx.enter_context(nc.sbuf_tensor("b_smk", [1, 64], F32))
            sm1 = bctx.enter_context(nc.sbuf_tensor("b_sm1", [1, 8], F32))
            sm2 = bctx.enter_context(nc.sbuf_tensor("b_sm2", [1, 8], F32))
            sm3 = bctx.enter_context(nc.sbuf_tensor("b_sm3", [1, 8], F32))
            sbias_c = bctx.enter_context(nc.sbuf_tensor("b_sbias", [128, 1], F32))
            b_smq, b_smkb, b_smk, b_sm1, b_sbias = (p.buf(), p.buf(), p.buf(),
                                                    p.buf(), p.buf())

            pS = bctx.enter_context(nc.psum_tensor("b_pS", [128, 3072], F32))
            b_pSslot = [p.buf(), p.buf(), p.buf()]
            pO = bctx.enter_context(nc.psum_tensor("b_pO", [128, QC], F32))
            b_pO = p.buf()
            pL = bctx.enter_context(nc.psum_tensor("b_pL", [128, QC], F32))
            b_pL = p.buf()
            sacc = bctx.enter_context(nc.sbuf_tensor("b_sacc", [128, 1024], F32))
            b_sacc = p.buf()
            onesf1 = bctx.enter_context(nc.sbuf_tensor("b_onesf1", [128, 1], F32))
            b_onesf1 = p.buf()
            p.add("vector", lambda e: e.memset(onesf1[:], 1.0), writes=[b_onesf1])

            p.dma("sync", smkb[:], s0out.rearrange("r a h -> a (r h)"),
                  reads=[b_s0out], writes=[b_smkb])
            p.add("vector", lambda e: e.tensor_copy(smq[:], nacc[:, 0:16]),
                  reads=[b_nacc], writes=[b_smq])
            p.add("vector", lambda e: e.tensor_copy(smk[:], smkb[:]),
                  reads=[b_smkb], writes=[b_smk])
            p.add("vector", lambda e: e.memset(sm1[:], 0.0), writes=[b_sm1])
            p.add("vector", lambda e: e.memset(sm2[:], 0.0), writes=[b_sm1])
            p.add("vector", lambda e: e.reduce_max(sm1[:, 0:1], smq[:], axis=AX.X),
                  reads=[b_smq], writes=[b_sm1])
            p.add("vector", lambda e: e.reduce_max(sm2[:, 0:1], smk[:], axis=AX.X),
                  reads=[b_smk], writes=[b_sm1])
            p.add("vector", lambda e: e.tensor_mul(sm3[:], sm1[:], sm2[:]),
                  reads=[b_sm1], writes=[b_sm1])
            p.add("scalar", lambda e: e.activation(sm1[:], sm3[:], AF.Sqrt),
                  reads=[b_sm1], writes=[b_sm1])
            p.add("vector", lambda e: e.tensor_scalar(sm2[:], sm1[:],
                                                      -1.02 / 8.0, -0.5, ALU.mult, ALU.add),
                  reads=[b_sm1], writes=[b_sm1])
            p.add("tensor", lambda e: e.matmul(pO[:, 0:1], onesf_col[:], sm2[:, 0:1],
                                               start=True, stop=True),
                  reads=[b_sm1, b_onesf], writes=[b_pO])
            p.add("vector", lambda e: e.tensor_copy(sbias_c[:], pO[:, 0:1]),
                  reads=[b_pO], writes=[b_sbias])
            if debug:
                for ic in range(8):
                    p.dma("sync", dX0[ic * 128:(ic + 1) * 128, :],
                          sx[:, ic * TOK:(ic + 1) * TOK], reads=[b_sx])
                p.dma("sync", dSM[:, 0:64], smk[:], reads=[b_smk])
                p.dma("sync", dSM[:, 64:80], smq[:], reads=[b_smq])
                p.dma("sync", dSM[:, 80:88], sm1[:], reads=[b_sm1])
                p.dma("sync", dSM[:, 88:96], sm2[:], reads=[b_sm1])
                p.dma("sync", dSM[:, 96:104], sm3[:], reads=[b_sm1])
                p.dma("sync", dBC, sbias_c[:], reads=[b_sbias])

            p.dma("sync", sQ[:].rearrange("p (h t) -> p h t", h=NHP),
                  q0.rearrange("(h p) t -> p h t", p=128),
                  reads=[b_q0], writes=[b_sQ])

            def load_hp0(hp, par):
                for r in range(4):
                    p.dma("sync", sK[par][:, r * TOK:(r + 1) * TOK],
                          k0out[r, hp * 128:(hp + 1) * 128, :],
                          reads=[b_k0out], writes=[b_sK[par]])
                    p.dma("gpsimd",
                          sVt[par][:, r * TOK:(r + 1) * TOK]
                          .rearrange("p (kb d) -> p kb d", d=128),
                          v0out[r, hp].rearrange("(kb p) d -> p kb d", p=128),
                          reads=[b_v0out], writes=[b_sV[par]])

            load_hp0(0, 0)
            for hp in range(NHP):
                par = hp % 2
                if hp + 1 < NHP:
                    load_hp0(hp + 1, (hp + 1) % 2)
                for qc in range(TOK // QC):
                    qoff = hp * TOK + qc * QC

                    def emit_S(g, qoff=qoff, par=par):
                        slot = g % 3
                        scol = slot * 1024
                        p.add("tensor", lambda e, g=g, scol=scol, qoff=qoff, par=par: e.matmul(
                            pS[:, scol:scol + QC], sK[par][0:64, g * 128:(g + 1) * 128],
                            sQ[0:64, qoff:qoff + QC], start=True, stop=True,
                            tile_position=(0, 0)),
                            reads=[b_sK[par], b_sQ], writes=[b_pSslot[slot]])
                        p.add("tensor", lambda e, g=g, scol=scol, qoff=qoff, par=par: e.matmul(
                            pS[:, scol + QC:scol + 1024], sK[par][64:128, g * 128:(g + 1) * 128],
                            sQ[64:128, qoff:qoff + QC], start=True, stop=True,
                            tile_position=(64, 0)),
                            reads=[b_sK[par], b_sQ], writes=[b_pSslot[slot]])
                        p.add("scalar", lambda e, scol=scol: e.activation(
                            sP[:, scol:scol + 1024], pS[:, scol:scol + 1024], AF.Exp,
                            bias=sbias_c[:], scale=0.125),
                            reads=[b_pSslot[slot], b_sbias], writes=[b_sP[slot]])

                    def emit_PV(g, par=par):
                        slot = g % 3
                        scol = slot * 1024
                        st, sp_ = g == 0, g == NKB - 1
                        p.add("tensor", lambda e, g=g, scol=scol, par=par, st=st, sp_=sp_: e.matmul(
                            pO[0:64, :], sVt[par][:, g * 128:g * 128 + 64],
                            sP[:, scol:scol + QC], start=st, stop=sp_,
                            tile_position=(0, 0)),
                            reads=[b_sV[par], b_sP[slot]],
                            writes=([b_pO] if st or sp_ else []))
                        p.add("tensor", lambda e, g=g, scol=scol, par=par, st=st, sp_=sp_: e.matmul(
                            pO[64:128, :], sVt[par][:, g * 128 + 64:g * 128 + 128],
                            sP[:, scol + QC:scol + 1024], start=st, stop=sp_,
                            tile_position=(0, 64)),
                            reads=[b_sV[par], b_sP[slot]],
                            writes=([b_pO] if st or sp_ else []))
                        p.add("vector", lambda e, scol=scol, st=(g == 0): e.tensor_tensor(
                            sacc[:], sacc[:], sP[:, scol:scol + 1024], ALU.add)
                            if not st else e.tensor_copy(sacc[:], sP[:, scol:scol + 1024]),
                            reads=[b_sP[slot]], writes=[b_sacc])

                    for g in range(NKB + 2):
                        if g < NKB:
                            emit_S(g)
                        if g >= 2:
                            emit_PV(g - 2)
                    p.add("vector", lambda e: e.memset(pL[:], 1.0), writes=[b_pL])
                    p.add("tensor", lambda e: e.matmul(
                        pL[0:1, :], onesf1[:], sacc[:, 0:QC], start=True, stop=True),
                        reads=[b_sacc, b_onesf1], writes=[b_pL])
                    p.add("tensor", lambda e: e.matmul(
                        pL[64:65, :], onesf1[:], sacc[:, QC:2 * QC], start=True, stop=True),
                        reads=[b_sacc, b_onesf1], writes=[b_pL])
                    p.add("vector", lambda e: e.tensor_copy(sl[:], pL[:]),
                          reads=[b_pL], writes=[b_sl])
                    p.add("vector", lambda e: e.reciprocal(srl[:], sl[:]),
                          reads=[b_sl], writes=[b_srl])
                    p.add("tensor", lambda e: e.matmul(pS[:, 0:QC], E1[:], srl[:],
                                                       start=True, stop=True),
                          reads=[b_srl, b_E1], writes=[b_pSslot[0]])
                    p.add("vector", lambda e: e.tensor_copy(sO[:], pO[:]),
                          reads=[b_pO], writes=[b_sO])
                    p.add("vector", lambda e: e.tensor_mul(sC_[:], sO[:], pS[:, 0:QC]),
                          reads=[b_sO, b_pSslot[0]], writes=[b_sC])
                    p.add("vector", lambda e, qc=qc, hp=hp: e.tensor_add(
                        sx[:, hp * TOK + qc * QC: hp * TOK + (qc + 1) * QC],
                        sx[:, hp * TOK + qc * QC: hp * TOK + (qc + 1) * QC], sC_[:]),
                        reads=[b_sC], writes=[b_sx])
                    if debug and hp == 0 and qc == 0:
                        p.dma("sync", dL, sl[:], reads=[b_sl])
                        p.dma("sync", dQ, sQ[:, 0:QC], reads=[b_sQ])
                        p.dma("sync", dK, sK[0][:, 0:QC], reads=[b_sK[0]])
                        p.dma("sync", dV, sVt[0][:, 0:QC], reads=[b_sV[0]])
            if debug:
                for ic in range(8):
                    p.dma("sync", dX1[ic * 128:(ic + 1) * 128, :],
                          sx[:, ic * TOK:(ic + 1) * TOK], reads=[b_sx])

        phase_B()

        p.begin_phase()
        def phase_C():
          with ExitStack() as cctx:
            pH = cctx.enter_context(nc.psum_tensor("c_pH", [128, 4 * QC], F32))
            b_pH = p.buf()
            pstat = cctx.enter_context(nc.psum_tensor("c_pstat", [128, 2 * QC], F32))
            b_pstat = p.buf()
            pbc = cctx.enter_context(nc.psum_tensor("c_pbc", [128, 2 * QC], F32))
            b_pbc = p.buf()

            lnx = cctx.enter_context(nc.sbuf_tensor("c_lnx", [128, 8 * TOK], BF16))
            b_lnx = p.buf()
            qsh = make_quant_shared(nc, p, cctx, "c_q")
            qf1, qfm, qf2 = make_quant(nc, p, cctx, C, qsh, wfcT, D, 4096,
                                       wfcq, b_wfcq, "c_qf", pH, b_pH)
            qp1, qpm, qp2 = make_quant(nc, p, cctx, C, qsh, wprojT, 4096, D,
                                       wprojq, b_wprojq, "c_qp", pH, b_pH)
            qf1(); qfm(); qf2()
            ln_stats_normalize(nc, p, cctx, C, sx, b_sx, lnx, b_lnx, "c_ln",
                               pstat, b_pstat, pbc, b_pbc)
            qp1(); qpm(); qp2()

            slnw = cctx.enter_context(nc.sbuf_tensor("c_slnw", [128, 32], F32))
            slnb = cctx.enter_context(nc.sbuf_tensor("c_slnb", [128, 32], F32))
            b_slnw, b_slnb = p.buf(), p.buf()
            p.add("sync", lambda e: e.dma_start(
                out=slnw[:], in_=lnw.rearrange("a (c p) -> (a p) c", p=128),
                allow_slow_non_contiguous=True), writes=[b_slnw], dma=True)
            p.add("sync", lambda e: e.dma_start(
                out=slnb[:], in_=lnb.rearrange("a (c p) -> (a p) c", p=128),
                allow_slow_non_contiguous=True), writes=[b_slnb], dma=True)

            swf = [cctx.enter_context(nc.sbuf_tensor(f"c_swf{i}", [128, 8 * QC], BF16))
                   for i in range(2)]
            b_swf = [p.buf(), p.buf()]
            swp = [cctx.enter_context(nc.sbuf_tensor(f"c_swp{i}", [128, D], BF16))
                   for i in range(2)]
            b_swp = [p.buf(), p.buf()]
            sh = cctx.enter_context(nc.sbuf_tensor("c_sh", [128, 32 * QC], BF16))
            b_sh = p.buf()
            shsq = cctx.enter_context(nc.sbuf_tensor("c_shsq", [128, QC], BF16))
            b_shsq = p.buf()
            hrow = cctx.enter_context(nc.sbuf_tensor("c_hrow", [1, QC], F32))
            hrow2 = cctx.enter_context(nc.sbuf_tensor("c_hrow2", [1, QC], F32))
            hrowm = cctx.enter_context(nc.sbuf_tensor("c_hrowm", [1, QC], F32))
            b_hrow, b_hrow2, b_hrowm = p.buf(), p.buf(), p.buf()
            t1 = cctx.enter_context(nc.sbuf_tensor("c_t1", [128, QC], F32))
            t2 = cctx.enter_context(nc.sbuf_tensor("c_t2", [128, QC], F32))
            b_t1, b_t2 = p.buf(), p.buf()
            ceps, b_ceps = p.constf(FFN_LN_EPS)

            for tc in range(TOK // QC):
                tsl = slice(tc * QC, (tc + 1) * QC)
                for fb in range(8):
                    par = fb % 2
                    for ic in range(8):
                        p.dma("gpsimd", swf[par][:, ic * QC:(ic + 1) * QC],
                              wfcq[ic * 128:(ic + 1) * 128, fb * QC:(fb + 1) * QC],
                              reads=[b_wfcq], writes=[b_swf[par]])
                    for ft in range(4):
                        for ic in range(8):
                            st = ic == 0
                            sp_ = ic == 7
                            p.add("tensor", lambda e, ft=ft, ic=ic, par=par, tsl=tsl, st=st, sp_=sp_: e.matmul(
                                pH[:, ft * QC:(ft + 1) * QC],
                                swf[par][:, ic * QC + ft * 128: ic * QC + (ft + 1) * 128],
                                lnx[:, ic * TOK:][:, tsl], start=st, stop=sp_),
                                reads=[b_swf[par], b_lnx],
                                writes=([b_pH] if st or sp_ else []))
                    p.add("scalar", lambda e, fb=fb: e.activation(
                        sh[:, fb * 4 * QC:(fb + 1) * 4 * QC], pH[:], AF.Gelu),
                        reads=[b_pH], writes=[b_sh])
                    for ft in range(4):
                        fc = fb * 4 + ft
                        st = fc == 0
                        sp_ = fc == 31
                        p.add("scalar", lambda e, fc=fc: e.activation(
                            shsq[:], sh[:, fc * QC:(fc + 1) * QC], AF.Square),
                            reads=[b_sh], writes=[b_shsq])
                        p.add("tensor", lambda e, fc=fc, st=st, sp_=sp_: e.matmul(
                            pstat[0:1, 0:QC], ones_colb[:], sh[:, fc * QC:(fc + 1) * QC],
                            start=st, stop=sp_, tile_position=(0, 0)),
                            reads=[b_sh, b_ones_colb], writes=([b_pstat] if sp_ else []))
                        p.add("tensor", lambda e, st=st, sp_=sp_: e.matmul(
                            pstat[32:33, QC:2 * QC], ones_colb[:], shsq[:],
                            start=st, stop=sp_, tile_position=(0, 32)),
                            reads=[b_shsq, b_ones_colb], writes=([b_pstat] if sp_ else []))
                p.add("vector", lambda e: e.tensor_scalar(
                    hrowm[:], pstat[0:1, 0:QC], 1.0 / 4096.0, None, ALU.mult),
                    reads=[b_pstat], writes=[b_hrowm])
                p.add("vector", lambda e: e.tensor_scalar(
                    hrow[:], pstat[32:33, QC:2 * QC], 1.0 / 4096.0, None, ALU.mult),
                    reads=[b_pstat], writes=[b_hrow])
                p.add("vector", lambda e: e.scalar_tensor_tensor(
                    hrow2[:], hrowm[:], -1.0, hrowm[:], ALU.mult, ALU.mult),
                    reads=[b_hrowm], writes=[b_hrow2])
                p.add("vector", lambda e: e.tensor_add(hrow[:], hrow[:], hrow2[:]),
                      reads=[b_hrow, b_hrow2], writes=[b_hrow])
                p.add("scalar", lambda e: e.activation(hrow[:], hrow[:], AF.Ln, bias=ceps[:]),
                      reads=[b_hrow, b_ceps], writes=[b_hrow])
                p.add("scalar", lambda e: e.activation(hrow[:], hrow[:], AF.Exp, scale=-0.5),
                      reads=[b_hrow], writes=[b_hrow])
                p.add("vector", lambda e: e.scalar_tensor_tensor(
                    hrow2[:], hrowm[:], -1.0, hrow[:], ALU.mult, ALU.mult),
                    reads=[b_hrowm, b_hrow], writes=[b_hrow2])
                p.add("tensor", lambda e: e.matmul(
                    pbc[:, 0:QC], onesf_col[:], hrow[:], start=True, stop=True),
                    reads=[b_hrow, b_onesf], writes=[b_pbc])
                p.add("tensor", lambda e: e.matmul(
                    pbc[:, QC:2 * QC], onesf_col[:], hrow2[:], start=True, stop=True),
                    reads=[b_hrow2, b_onesf], writes=[b_pbc])
                for fc in range(32):
                    fsl = slice(fc * QC, (fc + 1) * QC)
                    p.add("vector", lambda e, fsl=fsl: e.tensor_mul(
                        t1[:], sh[:, fsl], pbc[:, 0:QC]),
                        reads=[b_sh, b_pbc], writes=[b_t1])
                    p.add("vector", lambda e, fsl=fsl: e.tensor_add(
                        t2[:], t1[:], pbc[:, QC:2 * QC]),
                        reads=[b_t1, b_pbc], writes=[b_t2])
                    p.add("vector", lambda e, fc=fc, fsl=fsl: e.tensor_scalar(
                        sh[:, fsl], t2[:], slnw[:, fc:fc + 1], slnb[:, fc:fc + 1],
                        ALU.mult, ALU.add),
                        reads=[b_t2, b_slnw, b_slnb], writes=[b_sh])
                ybanks = [(pH, b_pH, 0), (pH, b_pH, 1), (pH, b_pH, 2), (pH, b_pH, 3),
                          (pstat, b_pstat, 0), (pstat, b_pstat, 1),
                          (pbc, b_pbc, 0), (pbc, b_pbc, 1)]
                for fc in range(32):
                    par = fc % 2
                    p.dma("gpsimd", swp[par][:],
                          wprojq[fc * 128:(fc + 1) * 128, :],
                          reads=[b_wprojq], writes=[b_swp[par]])
                    for dt_ in range(8):
                        yt, b_yt, bi = ybanks[dt_]
                        st = fc == 0
                        sp_ = fc == 31
                        p.add("tensor", lambda e, fc=fc, dt_=dt_, par=par, yt=yt, bi=bi, st=st, sp_=sp_: e.matmul(
                            yt[:, bi * QC:(bi + 1) * QC],
                            swp[par][:, dt_ * 128:(dt_ + 1) * 128],
                            sh[:, fc * QC:(fc + 1) * QC], start=st, stop=sp_),
                            reads=[b_swp[par], b_sh],
                            writes=([b_yt] if st or sp_ else []))
                for dt_ in range(8):
                    yt, b_yt, bi = ybanks[dt_]
                    p.add("vector", lambda e, dt_=dt_, yt=yt, bi=bi, tsl=tsl: e.tensor_add(
                        sx[:, dt_ * TOK:][:, tsl], sx[:, dt_ * TOK:][:, tsl],
                        yt[:, bi * QC:(bi + 1) * QC]),
                        reads=[b_yt], writes=[b_sx])
            if debug:
                for ic in range(8):
                    p.dma("sync", dX2[ic * 128:(ic + 1) * 128, :],
                          sx[:, ic * TOK:(ic + 1) * TOK], reads=[b_sx])

        phase_C()

        p.begin_phase()
        def phase_D():
          with ExitStack() as dctx:
            swq = dctx.enter_context(nc.sbuf_tensor("d_swq", [128, 8 * D], BF16))
            swk = dctx.enter_context(nc.sbuf_tensor("d_swk", [128, 8 * D], BF16))
            swv = dctx.enter_context(nc.sbuf_tensor("d_swv", [128, 8 * D], BF16))
            b_swq, b_swk, b_swv = p.buf(), p.buf(), p.buf()
            sb_b = dctx.enter_context(nc.sbuf_tensor("d_sbb", [1, 4 * D], BF16))
            b_sbb = p.buf()
            lnx = dctx.enter_context(nc.sbuf_tensor("d_lnx", [128, 8 * TOK], BF16))
            b_lnx = p.buf()
            stg = dctx.enter_context(nc.sbuf_tensor("d_stg", [128, TOK], BF16))
            b_stg = p.buf()
            sV0 = dctx.enter_context(nc.sbuf_tensor("d_sV", [128, D], BF16))
            b_sV0 = p.buf()

            pP = dctx.enter_context(nc.psum_tensor("d_pP", [128, 2 * QC], F32))
            b_pP = p.buf()
            pstat = dctx.enter_context(nc.psum_tensor("d_pstat", [128, 2 * QC], F32))
            b_pstat = p.buf()
            pbc = dctx.enter_context(nc.psum_tensor("d_pbc", [128, 2 * QC], F32))
            b_pbc = p.buf()
            pV = dctx.enter_context(nc.psum_tensor("d_pV", [128, 2 * QC], F32))
            b_pV = p.buf()

            for ic in range(8):
                p.dma("gpsimd", swq[:, ic * D:(ic + 1) * D],
                      wq1[ic * 128:(ic + 1) * 128, :], writes=[b_swq])
                p.dma("gpsimd", swk[:, ic * D:(ic + 1) * D],
                      wk1[ic * 128:(ic + 1) * 128, :], writes=[b_swk])
                p.dma("gpsimd", swv[:, ic * D:(ic + 1) * D],
                      wv1[ic * 128:(ic + 1) * 128, :], writes=[b_swv])
            p.dma("gpsimd", sb_b[:], b1, writes=[b_sbb])

            ln_stats_normalize(nc, p, dctx, C, sx, b_sx, lnx, b_lnx, "d_ln",
                               pstat, b_pstat, pbc, b_pbc)

            k_toks1 = []
            q_toks1 = []

            def emit_projD(kind):
                sw = [swq, swk][kind]
                b_sw = [b_swq, b_swk][kind]
                out_t = [q1, k1in][kind]
                b_out = [b_q1, b_k1in][kind]
                for hp in range(NHP):
                    for qc in range(TOK // QC):
                        psl = slice((qc % 2) * QC, (qc % 2) * QC + QC)
                        qsl = slice(qc * QC, (qc + 1) * QC)
                        for ic in range(8):
                            st = ic == 0
                            p.add("tensor", lambda e, ic=ic, hp=hp, qsl=qsl, psl=psl, st=st, sw=sw: e.matmul(
                                pP[:, psl], sw[:, ic * D + hp * 128: ic * D + (hp + 1) * 128],
                                lnx[:, ic * TOK:][:, qsl], start=st, stop=False),
                                reads=[b_sw, b_lnx], writes=([b_pP] if st else []))
                        p.add("tensor", lambda e, hp=hp, psl=psl, kind=kind: e.matmul(
                            pP[:, psl], sb_b[0:1, kind * D + hp * 128: kind * D + (hp + 1) * 128],
                            ones_row[:], start=False, stop=True),
                            reads=[b_sbb, b_ones_row], writes=[b_pP])
                        p.add("vector", lambda e, qsl=qsl, psl=psl: e.tensor_scalar(
                            stg[:, qsl], pP[:, psl], 1.0, None, ALU.is_ge),
                            reads=[b_pP], writes=[b_stg])
                    tk = p.dma("sync", out_t[hp * 128:(hp + 1) * 128, :], stg[:],
                               reads=[b_stg], writes=[b_out])
                    (q_toks1 if kind == 0 else k_toks1).append(tk)

            emit_projD(1)
            p.add("gpsimd", lambda e: e.collective_compute(
                "AllGather", ALU.bypass, ins=[k1in], outs=[k1out], replica_groups=RG),
                reads=[b_k1in], writes=[b_k1out], deps=k_toks1)

            v_toks1 = []
            for tt in range(TOK // 128):
                tsl = slice(tt * 128, (tt + 1) * 128)
                for half in range(2):
                    hsl = slice(half * QC, (half + 1) * QC)
                    for ic in range(8):
                        st = ic == 0
                        p.add("tensor", lambda e, ic=ic, tsl=tsl, hsl=hsl, st=st: e.matmul(
                            pV[:, hsl], lnx[:, ic * TOK:][:, tsl],
                            swv[:, ic * D:][:, hsl], start=st, stop=False),
                            reads=[b_lnx, b_swv], writes=([b_pV] if st else []))
                    p.add("tensor", lambda e, hsl=hsl: e.matmul(
                        pV[:, hsl], onesf_colb[:],
                        sb_b[0:1, 2 * D + hsl.start: 2 * D + hsl.stop],
                        start=False, stop=True),
                        reads=[b_sbb, b_onesf_colb], writes=[b_pV])
                p.add("scalar", lambda e: e.activation(sV0[:], pV[:, 0:D], AF.Copy),
                      reads=[b_pV], writes=[b_sV0])
                for hp in range(NHP):
                    v_toks1.append(p.dma(
                        "sync", v1in[hp, tsl, :], sV0[:, hp * 128:(hp + 1) * 128],
                        reads=[b_sV0], writes=[b_v1in]))
            p.add("gpsimd", lambda e: e.collective_compute(
                "AllGather", ALU.bypass, ins=[v1in], outs=[v1out], replica_groups=RG),
                reads=[b_v1in], writes=[b_v1out], deps=v_toks1)

            emit_projD(0)

        phase_D()

        p.begin_phase()
        def phase_E():
          with ExitStack() as ectx:
            sctx_t = ectx.enter_context(nc.sbuf_tensor("e_sctx", [128, NHP * TOK], BF16))
            b_sctx = p.buf()
            pS = ectx.enter_context(nc.psum_tensor("e_pS", [128, 3072], F32))
            b_pSslot = [p.buf(), p.buf(), p.buf()]
            pO = ectx.enter_context(nc.psum_tensor("e_pO", [128, QC], F32))
            b_pO = p.buf()
            with ExitStack() as e1ctx:
                sQ = e1ctx.enter_context(nc.sbuf_tensor("e_sQ", [128, NHP * TOK], BF16))
                b_sQ = p.buf()
                sK = [e1ctx.enter_context(nc.sbuf_tensor(f"e_sK{i}", [128, SEQ], BF16))
                      for i in range(2)]
                sVt = [e1ctx.enter_context(nc.sbuf_tensor(f"e_sV{i}", [128, SEQ], BF16))
                       for i in range(2)]
                b_sK = [p.buf(), p.buf()]
                b_sV = [p.buf(), p.buf()]
                sP = e1ctx.enter_context(nc.sbuf_tensor("e_sP", [128, 3072], BF16))
                b_sP = [p.buf(), p.buf(), p.buf()]

                p.dma("sync", sQ[:].rearrange("p (h t) -> p h t", h=NHP),
                      q1.rearrange("(h p) t -> p h t", p=128),
                      reads=[b_q1], writes=[b_sQ])

                def load_hp1(hp, par):
                    for r in range(4):
                        p.dma("sync", sK[par][:, r * TOK:(r + 1) * TOK],
                              k1out[r, hp * 128:(hp + 1) * 128, :],
                              reads=[b_k1out], writes=[b_sK[par]])
                        p.dma("gpsimd",
                              sVt[par][:, r * TOK:(r + 1) * TOK]
                              .rearrange("p (kb d) -> p kb d", d=128),
                              v1out[r, hp].rearrange("(kb p) d -> p kb d", p=128),
                              reads=[b_v1out], writes=[b_sV[par]])

                load_hp1(0, 0)
                for hp in range(NHP):
                    par = hp % 2
                    if hp + 1 < NHP:
                        load_hp1(hp + 1, (hp + 1) % 2)
                    for qc in range(TOK // QC):
                        qoff = hp * TOK + qc * QC

                        def emit_S1(g, qoff=qoff, par=par):
                            slot = g % 3
                            scol = slot * 1024
                            p.add("tensor", lambda e, g=g, scol=scol, qoff=qoff, par=par: e.matmul(
                                pS[:, scol:scol + QC], sK[par][0:64, g * 128:(g + 1) * 128],
                                sQ[0:64, qoff:qoff + QC], start=True, stop=True,
                                tile_position=(0, 0)),
                                reads=[b_sK[par], b_sQ], writes=[b_pSslot[slot]])
                            p.add("tensor", lambda e, g=g, scol=scol, qoff=qoff, par=par: e.matmul(
                                pS[:, scol + QC:scol + 1024], sK[par][64:128, g * 128:(g + 1) * 128],
                                sQ[64:128, qoff:qoff + QC], start=True, stop=True,
                                tile_position=(64, 0)),
                                reads=[b_sK[par], b_sQ], writes=[b_pSslot[slot]])
                            p.add("scalar", lambda e, scol=scol: e.activation(
                                sP[:, scol:scol + 1024], pS[:, scol:scol + 1024], AF.Sigmoid,
                                scale=0.125),
                                reads=[b_pSslot[slot]], writes=[b_sP[slot]])

                        def emit_PV1(g, par=par):
                            slot = g % 3
                            scol = slot * 1024
                            st, sp_ = g == 0, g == NKB - 1
                            p.add("tensor", lambda e, g=g, scol=scol, par=par, st=st, sp_=sp_: e.matmul(
                                pO[0:64, :], sVt[par][:, g * 128:g * 128 + 64],
                                sP[:, scol:scol + QC], start=st, stop=sp_,
                                tile_position=(0, 0)),
                                reads=[b_sV[par], b_sP[slot]],
                                writes=([b_pO] if st or sp_ else []))
                            p.add("tensor", lambda e, g=g, scol=scol, par=par, st=st, sp_=sp_: e.matmul(
                                pO[64:128, :], sVt[par][:, g * 128 + 64:g * 128 + 128],
                                sP[:, scol + QC:scol + 1024], start=st, stop=sp_,
                                tile_position=(0, 64)),
                                reads=[b_sV[par], b_sP[slot]],
                                writes=([b_pO] if st or sp_ else []))

                        for g in range(NKB + 2):
                            if g < NKB:
                                emit_S1(g)
                            if g >= 2:
                                emit_PV1(g - 2)
                        p.add("vector", lambda e, hp=hp, qc=qc: e.tensor_copy(
                            sctx_t[:, hp * TOK + qc * QC: hp * TOK + (qc + 1) * QC],
                            pO[:]),
                            reads=[b_pO], writes=[b_sctx])

            p.begin_phase()
            swo = ectx.enter_context(nc.sbuf_tensor("e_swo", [128, 8 * D], BF16))
            b_swo = p.buf()
            sb_b = ectx.enter_context(nc.sbuf_tensor("e_sbb", [1, 4 * D], BF16))
            b_sbb = p.buf()
            ssp = ectx.enter_context(nc.sbuf_tensor("e_ssp", [128, QC], F32))
            b_ssp = p.buf()
            for ic in range(8):
                p.dma("gpsimd", swo[:, ic * D:(ic + 1) * D],
                      wo1[ic * 128:(ic + 1) * 128, :], writes=[b_swo])
            p.dma("gpsimd", sb_b[:], b1, writes=[b_sbb])

            for hp in range(NHP):
                for qc in range(TOK // QC):
                    psl = slice((qc % 2) * QC, (qc % 2) * QC + QC)
                    for ic in range(8):
                        st = ic == 0
                        p.add("tensor", lambda e, ic=ic, hp=hp, qc=qc, psl=psl, st=st: e.matmul(
                            pS[:, psl], swo[:, ic * D + hp * 128: ic * D + (hp + 1) * 128],
                            sctx_t[:, ic * TOK + qc * QC: ic * TOK + (qc + 1) * QC],
                            start=st, stop=False),
                            reads=[b_swo, b_sctx], writes=([b_pSslot[0]] if st else []))
                    p.add("tensor", lambda e, hp=hp, psl=psl: e.matmul(
                        pS[:, psl], sb_b[0:1, 3 * D + hp * 128: 3 * D + (hp + 1) * 128],
                        ones_row[:], start=False, stop=True),
                        reads=[b_sbb, b_ones_row], writes=[b_pSslot[0]])
                    p.add("vector", lambda e, psl=psl: e.tensor_scalar(
                        ssp[:], pS[:, psl], 1.0, None, ALU.is_ge),
                        reads=[b_pSslot[0]], writes=[b_ssp])
                    p.add("vector", lambda e, qc=qc, hp=hp: e.tensor_add(
                        sx[:, hp * TOK + qc * QC: hp * TOK + (qc + 1) * QC],
                        sx[:, hp * TOK + qc * QC: hp * TOK + (qc + 1) * QC], ssp[:]),
                        reads=[b_ssp], writes=[b_sx])
            if debug:
                for ic in range(8):
                    p.dma("sync", dX3[ic * 128:(ic + 1) * 128, :],
                          sx[:, ic * TOK:(ic + 1) * TOK], reads=[b_sx])

        phase_E()

        p.begin_phase()
        def phase_F():
          with ExitStack() as fctx:
            add_moe_consts(nc, p, fctx, C)
            sel8, b_sel8 = C["sel8"]
            idf32, b_idf32 = C["idf32"]

            lnx = fctx.enter_context(nc.sbuf_tensor("f_lnx", [128, 8 * TOK], BF16))
            b_lnx = p.buf()
            sh = fctx.enter_context(nc.sbuf_tensor("f_sh", [128, 16 * 1024], BF16))
            b_sh = p.buf()
            srw = fctx.enter_context(nc.sbuf_tensor("f_srw", [128, 8 * 8], BF16))
            b_srw = p.buf()
            sdwT = fctx.enter_context(nc.sbuf_tensor("f_sdwT", [8, TOK], BF16))
            b_sdwT = p.buf()

            pG = fctx.enter_context(nc.psum_tensor("f_pG", [128, 4 * QC], F32))
            b_pG = p.buf()
            pU = fctx.enter_context(nc.psum_tensor("f_pU", [128, 2 * QC], F32))
            b_pU = p.buf()
            pBC = fctx.enter_context(nc.psum_tensor("f_pBC", [128, 2 * QC], F32))
            b_pBC = p.buf()

            for ic in range(8):
                p.dma("gpsimd", srw[:, ic * 8:(ic + 1) * 8],
                      rwT[ic * 128:(ic + 1) * 128, :], writes=[b_srw])

            p.constf(LN_EPS)
            with ExitStack() as lctx:
                lndone = ln_stats_normalize(nc, p, lctx, C, sx, b_sx, lnx, b_lnx, "f_ln",
                                            pU, b_pU, pBC, b_pBC)

            wall = fctx.enter_context(nc.sbuf_tensor("f_wall", [128, 88], F32))
            w8 = [wall[:, i * 8:(i + 1) * 8] for i in range(8)]
            b_w8 = [p.buf() for _ in range(8)]
            wm = [wall[:, 64 + i:65 + i] for i in range(2)]
            b_wm = [p.buf(), p.buf()]
            ssum = wall[:, 66:74]
            b_ssum = p.buf()
            srs = wall[:, 74:82]
            b_srs = p.buf()

            for tt in range(TOK // 128):
                tsl = slice(tt * 128, (tt + 1) * 128)
                for ic in range(8):
                    st = ic == 0
                    sp_ = ic == 7
                    p.add("tensor", lambda e, ic=ic, tsl=tsl, st=st, sp_=sp_: e.matmul(
                        pG[:, 0:8], lnx[:, ic * TOK:][:, tsl], srw[:, ic * 8:(ic + 1) * 8],
                        start=st, stop=sp_),
                        reads=[b_lnx, b_srw], writes=([b_pG] if st or sp_ else []))
                slg, as_, f1, d1, k1, g1, msk, oh = w8
                bslg, bas, bf1, bd1, bk1, bg1, bmsk, boh = b_w8
                m1, m2 = wm
                bm1, bm2 = b_wm
                A = lambda eng, fn, rd, wr: p.add(eng, fn, reads=rd, writes=wr)
                p.add("vector", lambda e: e.tensor_copy(slg[:], pG[:, 0:8]),
                      reads=[b_pG], writes=[bslg],
                      deps=(lndone if tt == 0 else ()))
                A("vector", lambda e: e.memset(ssum[:], 1.0), [], [b_ssum])
                from concourse.bass import broadcast_tensor_aps as _bc2
                def _maxb(out, a, mcol):
                    aa, mm = _bc2(a, mcol)
                    return lambda e: e.tensor_tensor(out, aa, mm, ALU.max)
                A("vector", lambda e: e.reduce_max(m1[:], slg[:], axis=AX.X), [bslg], [bm1])
                A("vector", lambda e: e.tensor_scalar(as_[:], slg[:], -1.0, None, ALU.mult), [bslg], [bas])
                A("vector", lambda e: e.tensor_max(as_[:], as_[:], slg[:]), [bslg, bas], [bas])
                A("vector", _maxb(f1[:], as_[:], m1[:]), [bas, bm1], [bf1])
                A("vector", lambda e: e.tensor_scalar(d1[:], slg[:], m1[:], None, ALU.subtract), [bslg, bm1], [bd1])
                A("vector", lambda e: e.scalar_tensor_tensor(k1[:], f1[:], 0.02, d1[:], ALU.mult, ALU.add), [bf1, bd1], [bk1])
                A("vector", lambda e: e.tensor_scalar(k1[:], k1[:], 0.0, None, ALU.is_ge), [bk1], [bk1])
                A("scalar", lambda e: e.activation(g1[:], d1[:], AF.Exp), [bd1], [bg1])
                A("vector", lambda e: e.tensor_mul(g1[:], g1[:], k1[:]), [bg1, bk1], [bg1])
                A("vector", lambda e: e.reduce_sum(ssum[:, 0:1], g1[:], axis=AX.X), [bg1, b_ssum], [b_ssum])
                A("vector", lambda e: e.tensor_scalar(oh[:], slg[:], m1[:], None, ALU.is_ge), [bslg, bm1], [boh])
                A("vector", lambda e: e.scalar_tensor_tensor(msk[:], oh[:], -BIG, slg[:], ALU.mult, ALU.add), [boh, bslg], [bmsk])
                A("vector", lambda e: e.reduce_max(m2[:], msk[:], axis=AX.X), [bmsk], [bm2])
                A("vector", _maxb(f1[:], as_[:], m2[:]), [bas, bm2], [bf1])
                A("vector", lambda e: e.tensor_scalar(d1[:], slg[:], m2[:], None, ALU.subtract), [bslg, bm2], [bd1])
                A("vector", lambda e: e.scalar_tensor_tensor(k1[:], f1[:], 0.02, d1[:], ALU.mult, ALU.add), [bf1, bd1], [bk1])
                A("vector", lambda e: e.tensor_scalar(k1[:], k1[:], 0.0, None, ALU.is_ge), [bk1], [bk1])
                A("vector", lambda e: e.tensor_scalar(d1[:], msk[:], m2[:], None, ALU.subtract), [bmsk, bm2], [bd1])
                A("scalar", lambda e: e.activation(g1[:], d1[:], AF.Exp), [bd1], [bg1])
                A("vector", lambda e: e.tensor_mul(g1[:], g1[:], k1[:]), [bg1, bk1], [bg1])
                A("vector", lambda e: e.reduce_sum(ssum[:, 1:2], g1[:], axis=AX.X), [bg1, b_ssum], [b_ssum])
                A("vector", lambda e: e.tensor_scalar(msk[:], msk[:], m2[:], None, ALU.is_ge), [bmsk, bm2], [bmsk])
                A("vector", lambda e: e.reciprocal(srs[:], ssum[:]), [b_ssum], [b_srs])
                A("vector", lambda e: e.tensor_scalar(oh[:], oh[:], srs[:, 0:1], None, ALU.mult), [boh, b_srs], [boh])
                A("vector", lambda e: e.scalar_tensor_tensor(slg[:], msk[:], srs[:, 1:2], oh[:], ALU.mult, ALU.add), [bmsk, b_srs, boh], [bslg])
                p.add("tensor", lambda e: e.transpose(pU[0:8, 0:128], slg[:], idf32[:]),
                      reads=[bslg, b_idf32], writes=[b_pU])
                p.add("vector", lambda e, tsl=tsl: e.tensor_copy(sdwT[:, tsl], pU[0:8, 0:128]),
                      reads=[b_pU], writes=[b_sdwT])
            if debug:
                p.dma("gpsimd", dDW, sdwT[:], reads=[b_sdwT])
                p.dma("sync", dLN, lnx[:, 0:TOK], reads=[b_lnx])

            from concourse.bass import broadcast_tensor_aps as _bcast
            swg = [fctx.enter_context(nc.sbuf_tensor(f"f_swg{i}", [128, 8 * QC], BF16))
                   for i in range(2)]
            swu = [fctx.enter_context(nc.sbuf_tensor(f"f_swu{i}", [128, 8 * QC], BF16))
                   for i in range(2)]
            b_swg = [p.buf(), p.buf()]
            b_swu = [p.buf(), p.buf()]
            swd = fctx.enter_context(nc.sbuf_tensor("f_swd", [128, 16 * 1024], BF16))
            b_swd = p.buf()
            ssg = fctx.enter_context(nc.sbuf_tensor("f_ssg", [128, 2 * QC], BF16))
            b_ssg = p.buf()
            sdwB = fctx.enter_context(nc.sbuf_tensor("f_sdwB", [128, 1024], BF16))
            b_sdwB = p.buf()
            scc = fctx.enter_context(nc.sbuf_tensor("f_scc", [128, 2 * QC], BF16))
            b_scc = p.buf()
            sxv = sx[:].rearrange("p (c t) -> p c t", c=8)
            shv = sh[:].rearrange("p (f t) -> p f t", f=16)

            for e_i in range(8):
                for th in range(2):
                    for fb in range(4):
                        par = (e_i * 8 + th * 4 + fb) % 2
                        for ic in range(8):
                            p.dma("gpsimd", swg[par][:, ic * QC:(ic + 1) * QC],
                                  wgT[e_i * D + ic * 128: e_i * D + (ic + 1) * 128,
                                      fb * QC:(fb + 1) * QC], writes=[b_swg[par]],
                                  deps=(lndone if e_i == 0 and th == 0 and fb == 0
                                        and ic == 0 else ()))
                            p.dma("gpsimd", swu[par][:, ic * QC:(ic + 1) * QC],
                                  wuT[e_i * D + ic * 128: e_i * D + (ic + 1) * 128,
                                      fb * QC:(fb + 1) * QC], writes=[b_swu[par]])
                        if th == 0 and fb == 0:
                            for fc in range(16):
                                p.dma("gpsimd", swd[:, fc * 1024:(fc + 1) * 1024],
                                      wdT[e_i * 2048 + fc * 128: e_i * 2048 + (fc + 1) * 128, :],
                                      writes=[b_swd])
                        for tc in range(2):
                            tsl = slice(th * 1024 + tc * QC, th * 1024 + (tc + 1) * QC)
                            for ft in range(4):
                                ut, but, fo = (pU, b_pU, ft) if ft < 2 else (pBC, b_pBC, ft - 2)
                                for ic in range(8):
                                    st = ic == 0
                                    sp_ = ic == 7
                                    p.add("tensor", lambda e, ft=ft, ic=ic, tsl=tsl, st=st, sp_=sp_, par=par: e.matmul(
                                        pG[:, ft * QC:(ft + 1) * QC],
                                        swg[par][:, ic * QC + ft * 128: ic * QC + (ft + 1) * 128],
                                        lnx[:, ic * TOK:][:, tsl], start=st, stop=sp_),
                                        reads=[b_swg[par], b_lnx],
                                        writes=([b_pG] if st or sp_ else []))
                                    p.add("tensor", lambda e, fo=fo, ic=ic, ft=ft, tsl=tsl, st=st, sp_=sp_, ut=ut, but=but, par=par: e.matmul(
                                        ut[:, fo * QC:(fo + 1) * QC],
                                        swu[par][:, ic * QC + ft * 128: ic * QC + (ft + 1) * 128],
                                        lnx[:, ic * TOK:][:, tsl], start=st, stop=sp_),
                                        reads=[b_swu[par], b_lnx],
                                        writes=([but] if st or sp_ else []))
                            for half in range(2):
                                ut, but = (pU, b_pU) if half == 0 else (pBC, b_pBC)
                                p.add("scalar", lambda e, half=half: e.activation(
                                    ssg[:], pG[:, half * 2 * QC:(half + 1) * 2 * QC],
                                    AF.Silu),
                                    reads=[b_pG], writes=[b_ssg])
                                p.add("vector", lambda e, fb=fb, tc=tc, half=half, ut=ut: e.tensor_mul(
                                    shv[:, fb * 4 + half * 2: fb * 4 + half * 2 + 2,
                                        tc * QC:(tc + 1) * QC],
                                    ssg[:].rearrange("p (f t) -> p f t", f=2),
                                    ut[:, 0:2 * QC].rearrange("p (f t) -> p f t", f=2)),
                                    reads=[b_ssg, but], writes=[b_sh])
                    if debug and e_i == 0 and th == 0:
                        p.dma("sync", dH, sh[:, 0:1024], reads=[b_sh])
                    for j in range(2):
                        p.add("tensor", lambda e, e_i=e_i, th=th, j=j: e.matmul(
                            pU[:, 0:QC], sel8[:, e_i * 128:(e_i + 1) * 128],
                            sdwT[0:8, th * 1024 + j * QC: th * 1024 + (j + 1) * QC],
                            start=True, stop=True),
                            reads=[b_sdwT, b_sel8], writes=[b_pU])
                        p.add("vector", lambda e, j=j: e.tensor_copy(
                            sdwB[:, j * QC:(j + 1) * QC], pU[:, 0:QC]),
                            reads=[b_pU], writes=[b_sdwB])
                    dbanks = [(pG, b_pG, 0), (pG, b_pG, 1), (pG, b_pG, 2), (pG, b_pG, 3),
                              (pU, b_pU, 0), (pU, b_pU, 1), (pBC, b_pBC, 0), (pBC, b_pBC, 1)]
                    for t8 in range(2):
                        t0 = t8 * QC
                        tg = th * 1024 + t0
                        for fc in range(16):
                            st = fc == 0
                            sp_ = fc == 15
                            for dt_ in range(8):
                                yt, b_yt, bi = dbanks[dt_]
                                p.add("tensor", lambda e, fc=fc, dt_=dt_, t0=t0, st=st, sp_=sp_, yt=yt, bi=bi: e.matmul(
                                    yt[:, bi * QC:(bi + 1) * QC],
                                    swd[:, fc * 1024 + dt_ * 128: fc * 1024 + (dt_ + 1) * 128],
                                    shv[:, fc, t0:t0 + QC], start=st, stop=sp_),
                                    reads=[b_swd, b_sh],
                                    writes=([b_yt] if st or sp_ else []))
                        a2w = sdwB[:, t0:t0 + QC].rearrange("p (a t) -> p a t", a=1)
                        a0 = scc[:].rearrange("p (c t) -> p c t", c=2)

                        def scale_add(src2, b_src, cbase, tg=tg, a0=a0, a2w=a2w):
                            _, a2 = _bcast(src2, a2w)
                            p.add("vector", lambda e, a0=a0, src2=src2, a2=a2: e.tensor_tensor(
                                a0, src2, a2, ALU.mult),
                                reads=[b_src, b_sdwB], writes=[b_scc])
                            p.add("vector", lambda e, a0=a0, cbase=cbase, tg=tg: e.tensor_add(
                                sxv[:, cbase:cbase + 2, tg:tg + QC],
                                sxv[:, cbase:cbase + 2, tg:tg + QC], a0),
                                reads=[b_scc], writes=[b_sx])

                        scale_add(pG[:, 0:2 * QC].rearrange("p (c t) -> p c t", c=2), b_pG, 0)
                        if debug and e_i == 0 and th == 0 and t8 == 0:
                            p.dma("gpsimd", dSB, sdwB[:, 0:256], reads=[b_sdwB])
                            p.dma("gpsimd", dSC, scc[:, 0:1024], reads=[b_scc])
                        scale_add(pG[:, 2 * QC:4 * QC].rearrange("p (c t) -> p c t", c=2), b_pG, 2)
                        scale_add(pU[:, 0:2 * QC].rearrange("p (c t) -> p c t", c=2), b_pU, 4)
                        scale_add(pBC[:, 0:2 * QC].rearrange("p (c t) -> p c t", c=2), b_pBC, 6)

            for ic in range(8):
                p.dma("sync", oX4[ic * 128:(ic + 1) * 128, :],
                      sx[:, ic * TOK:(ic + 1) * TOK], reads=[b_sx])

        phase_F()
        p.emit(ctx)
    return nc



def _perm_deinterleave():
    perm = np.empty(D, np.int64)
    for h in range(H):
        for i in range(32):
            perm[h * 64 + i] = h * 64 + 2 * i
            perm[h * 64 + 32 + i] = h * 64 + 2 * i + 1
    return perm


def host_prep(inp):
    d = {}
    x = np.asarray(inp["L0_attn_hidden_states"])
    d["xfm"] = np.ascontiguousarray(x.transpose(0, 2, 1))
    perm = _perm_deinterleave()
    d["wq0"] = np.ascontiguousarray(np.asarray(inp["L0_attn_Wq"])[:, perm])
    d["wk0"] = np.ascontiguousarray(np.asarray(inp["L0_attn_Wk"])[:, perm])
    d["wv0"] = np.ascontiguousarray(np.asarray(inp["L0_attn_Wv"]))
    d["bqkv0"] = np.ascontiguousarray(np.stack([
        np.asarray(inp["L0_attn_bq"])[perm],
        np.asarray(inp["L0_attn_bk"])[perm],
        np.asarray(inp["L0_attn_bv"])]).reshape(1, 3 * D))
    d["wfcT"] = np.ascontiguousarray(np.asarray(inp["L0_ffn_W_fc"]).T)
    d["wprojT"] = np.ascontiguousarray(np.asarray(inp["L0_ffn_W_proj"]).T)
    d["lnw"] = np.asarray(inp["L0_ffn_ln_w"]).reshape(1, -1)
    d["lnb"] = np.asarray(inp["L0_ffn_ln_b"]).reshape(1, -1)
    d["wq1T"] = np.ascontiguousarray(np.asarray(inp["L1_attn_Wq"]).T)
    d["wk1T"] = np.ascontiguousarray(np.asarray(inp["L1_attn_Wk"]).T)
    d["wv1T"] = np.ascontiguousarray(np.asarray(inp["L1_attn_Wv"]).T)
    d["wo1T"] = np.ascontiguousarray(np.asarray(inp["L1_attn_Wo"]).T)
    d["b1"] = np.ascontiguousarray(np.stack([
        np.asarray(inp["L1_attn_bq"]), np.asarray(inp["L1_attn_bk"]),
        np.asarray(inp["L1_attn_bv"]), np.asarray(inp["L1_attn_bo"])]))
    d["rwT"] = np.ascontiguousarray(np.asarray(inp["L1_ffn_router_w"]).T)
    d["wgT"] = np.ascontiguousarray(np.asarray(inp["L1_ffn_w_gate"]).transpose(0, 2, 1))
    d["wuT"] = np.ascontiguousarray(np.asarray(inp["L1_ffn_w_up"]).transpose(0, 2, 1))
    d["wdT"] = np.ascontiguousarray(np.asarray(inp["L1_ffn_w_down"]).transpose(0, 2, 1))
    return d


_NC_CACHE = {}


def kernel(**inputs):
    debug = bool(inputs.pop("_debug", False))
    d = host_prep(inputs)
    key = ("fused", debug)
    if key not in _NC_CACHE:
        _NC_CACHE[key] = build_fused(debug=debug)
    nc = _NC_CACHE[key]
    in_maps = []
    for c in range(NC):
        b, q = c // 4, c % 4
        in_maps.append(dict(
            xin=np.ascontiguousarray(d["xfm"][b][:, q * TOK:(q + 1) * TOK]),
            pos0=np.array([[q * TOK]], dtype=np.float32),
            wq0=d["wq0"], wk0=d["wk0"], wv0=d["wv0"], bqkv=d["bqkv0"],
            wfcT=d["wfcT"], wprojT=d["wprojT"], lnw=d["lnw"], lnb=d["lnb"],
            wq1=d["wq1T"], wk1=d["wk1T"], wv1=d["wv1T"], wo1=d["wo1T"],
            b1=d["b1"].reshape(1, 4 * D),
            rwT=d["rwT"],
            wgT=d["wgT"].reshape(8 * D, 2048),
            wuT=d["wuT"].reshape(8 * D, 2048),
            wdT=d["wdT"].reshape(8 * 2048, D),
        ))
    res = run_bass_kernel_spmd(nc, in_maps, core_ids=list(range(NC)))
    out = np.empty((BN, SEQ, D), np.float32)
    for c in range(NC):
        b, q = c // 4, c % 4
        out[b, q * TOK:(q + 1) * TOK, :] = np.asarray(res.results[c]["oX4"]).T
    if debug:
        dbg = {k: [np.asarray(res.results[c][k]) for c in range(NC)]
               for k in ("dX0", "dX1", "dX2", "dX3", "dSM", "dBC", "dL",
                         "dQ", "dK", "dV", "dDW", "dLN", "dH", "dSB", "dSC")}
        return out, dbg
    return out


# version 19
# speedup vs baseline: 1.2759x; 1.0697x over previous
import math
import numpy as np
import ml_dtypes
import concourse.bass as bass
import concourse.mybir as mybir
from concourse.bass_utils import run_bass_kernel_spmd

F32 = mybir.dt.float32
BF16 = mybir.dt.bfloat16
AF = mybir.ActivationFunctionType
ALU = mybir.AluOpType
AX = mybir.AxisListType

BN, SEQ, D, H, HD = 2, 8192, 1024, 16, 64
TOK = 2048
NC = 8
NHP = 8
QC = 512
LN_EPS = 1e-5
FFN_LN_EPS = 1e-5
ABSMEAN_EPS = 1e-6
BIG = 1e9
RG = [[0, 1, 2, 3], [4, 5, 6, 7]]

ENGINES = ("sync", "gpsimd", "tensor", "vector", "scalar")
DMA_QUEUES = ("sync", "gpsimd")


class Buf:
    __slots__ = ("name", "last_write", "readers")

    def __init__(self, name=""):
        self.name = name
        self.last_write = None
        self.readers = []


class Prog:

    def __init__(self, nc, trust_engines=("tensor",)):
        self.nc = nc
        self.steps = {e: [] for e in ENGINES}
        self._consts = {}
        self._ctx = None
        self.trust_engines = set(trust_engines)
        self._phase_deps = None
        self._phase_seen = set()

    def buf(self, name=""):
        return Buf(name)

    def constf(self, value):
        key = float(value)
        if key not in self._consts:
            t = self._ctx.enter_context(
                self.nc.sbuf_tensor(f"c_f{len(self._consts)}", [1, 1], F32))
            b = Buf(f"constf_{key}")
            self.add("vector", lambda e: e.memset(t[:], key), writes=[b])
            self._consts[key] = (t, b)
        return self._consts[key]

    def begin_phase(self):
        self._phase_deps = [(e, len(self.steps[e]) - 1)
                            for e in ENGINES if self.steps[e]]
        self._phase_seen = set()

    def add(self, eng, fn, reads=(), writes=(), deps=(), dma=False):
        import traceback
        tag = traceback.extract_stack(limit=3)[0].lineno
        tok = (eng, len(self.steps[eng]))
        alldeps = list(deps)
        if self._phase_deps is not None and eng not in self._phase_seen:
            alldeps.extend(t for t in self._phase_deps if t[0] != eng or t[1] >= 0)
            self._phase_seen.add(eng)
            if len(self._phase_seen) == len(ENGINES):
                self._phase_deps = None
        for b in reads:
            if b.last_write is not None:
                alldeps.append(b.last_write)
        for b in writes:
            if b.last_write is not None:
                alldeps.append(b.last_write)
            alldeps.extend(b.readers)
        self.steps[eng].append(dict(fn=fn, deps=alldeps, dma=dma, inc=0, tag=tag))
        for b in reads:
            b.readers.append(tok)
        for b in writes:
            b.last_write = tok
            b.readers = []
        return tok

    def dma(self, q, out, in_, reads=(), writes=(), deps=()):
        return self.add(q, lambda e: e.dma_start(out=out, in_=in_),
                        reads=reads, writes=writes, deps=deps, dma=True)

    def emit(self, ctx):
        nc = self.nc
        sems = {e: ctx.enter_context(nc.semaphore("sem_" + e)) for e in ENGINES}
        for e in ENGINES:
            for s in self.steps[e]:
                if s["dma"]:
                    s["inc"] = 16
        for e in ENGINES:
            for i, s in enumerate(self.steps[e]):
                for (de, di) in s["deps"]:
                    if de == e and de in self.trust_engines:
                        continue
                    if de == e and di >= i:
                        raise RuntimeError(f"bad same-queue dep {de}:{di} >= {i}")
                    tgt = self.steps[de][di]
                    if tgt["inc"] == 0:
                        tgt["inc"] = 1
        cum = {}
        for e in ENGINES:
            c = 0
            cc = []
            for s in self.steps[e]:
                c += s["inc"]
                cc.append(c)
            cum[e] = cc
        blk = ctx.enter_context(nc.Block())

        no_dedup = True

        def emitter_for(ename):
            def emitter(eng):
                waited = {}
                for i, s in enumerate(self.steps[ename]):
                    waitv = {}
                    for (de, di) in s["deps"]:
                        if de == ename and de in self.trust_engines:
                            continue
                        if de == ename and di >= i:
                            continue
                        v = cum[de][di]
                        if v > waitv.get(de, 0):
                            waitv[de] = v
                    for de, v in sorted(waitv.items()):
                        if no_dedup or v > waited.get(de, 0):
                            eng.wait_ge(sems[de], v)
                            waited[de] = max(v, waited.get(de, 0))
                    try:
                        inst = s["fn"](eng)
                    except Exception:
                        import sys
                        print(f"EMIT FAIL {ename} step {i} tag={s.get('tag')}",
                              file=sys.stderr)
                        raise
                    if s["inc"]:
                        inst.then_inc(sems[ename], s["inc"])
                if ename in DMA_QUEUES and cum[ename] and cum[ename][-1] > 0:
                    eng.wait_ge(sems[ename], cum[ename][-1])
            return emitter

        blk.sync(emitter_for("sync"))
        blk.tensor(emitter_for("tensor"))
        blk.vector(emitter_for("vector"))
        blk.scalar(emitter_for("scalar"))
        blk.gpsimd(emitter_for("gpsimd"))



def build_consts(nc, p, ctx):
    C = {}

    def sb(name, shape, dtype):
        h = ctx.enter_context(nc.sbuf_tensor(name, shape, dtype))
        b = Buf(name)
        return h, b

    ones_col, b_ones_col = sb("c_ones_col", [128, 1], BF16)
    ones_row, b_ones_row = sb("c_ones_row", [1, QC], BF16)
    onesf_col, b_onesf_col = sb("c_onesf_col", [1, 128], F32)
    onesf_colb, b_onesf_colb = sb("c_onesf_colb", [1, 128], BF16)
    onesf_c128, b_onesf_c128 = sb("c_onesf_c128", [128, 8], F32)
    p.add("vector", lambda e: e.memset(ones_col[:], 1.0), writes=[b_ones_col])
    p.add("vector", lambda e: e.memset(ones_row[:], 1.0), writes=[b_ones_row])
    p.add("vector", lambda e: e.memset(onesf_col[:], 1.0), writes=[b_onesf_col])
    p.add("vector", lambda e: e.memset(onesf_colb[:], 1.0), writes=[b_onesf_colb])
    p.add("vector", lambda e: e.memset(onesf_c128[:], 1.0), writes=[b_onesf_c128])
    C["onesf_c128"] = (onesf_c128, b_onesf_c128)
    C["ones_col"] = (ones_col, b_ones_col)
    C["ones_row"] = (ones_row, b_ones_row)
    C["onesf_col"] = (onesf_col, b_onesf_col)
    C["onesf_colb"] = (onesf_colb, b_onesf_colb)
    return C


def _scr(nc, p, ctx, n):
    out = []
    for i in range(n):
        h = ctx.enter_context(nc.sbuf_tensor(f"scr{i}_{len(p.steps['vector'])}",
                                             [128, 128], F32))
        out.append((h, Buf(f"scr{i}")))
    return out


def add_rope_consts(nc, p, ctx, C):
    (u1, b_u1), (u2, b_u2), (u3, b_u3) = _scr(nc, p, ctx, 3)
    R = ctx.enter_context(nc.sbuf_tensor("c_R", [128, 128], BF16))
    b_R = Buf("c_R")
    p.add("gpsimd", lambda e: e.iota(u1[:], [[-1, 128]], base=0, channel_multiplier=1,
                                     allow_small_or_imprecise_dtypes=True), writes=[b_u1])
    p.add("gpsimd", lambda e: e.iota(u2[:].rearrange("p (a b) -> p a b", b=64),
                                     [[0, 2], [1, 64]], base=0, channel_multiplier=0,
                                     allow_small_or_imprecise_dtypes=True), writes=[b_u2])
    p.add("vector", lambda e: e.tensor_scalar(u3[:], u1[:], 32.0, None, ALU.is_equal),
          reads=[b_u1], writes=[b_u3])
    p.add("vector", lambda e: e.tensor_scalar(u2[:], u2[:], 32.0, None, ALU.is_lt),
          reads=[b_u2], writes=[b_u2])
    p.add("vector", lambda e: e.tensor_mul(u3[:], u3[:], u2[:]),
          reads=[b_u2, b_u3], writes=[b_u3])
    p.add("vector", lambda e: e.tensor_scalar(u1[:], u1[:], -32.0, None, ALU.is_equal),
          reads=[b_u1], writes=[b_u1])
    p.add("vector", lambda e: e.tensor_scalar(u2[:], u2[:], -1.0, 1.0, ALU.mult, ALU.add),
          reads=[b_u2], writes=[b_u2])
    p.add("vector", lambda e: e.tensor_mul(u1[:], u1[:], u2[:]),
          reads=[b_u1, b_u2], writes=[b_u1])
    p.add("vector", lambda e: e.tensor_sub(R[:], u1[:], u3[:]),
          reads=[b_u1, b_u3], writes=[b_R])
    C["R"] = (R, b_R)
    Dm = ctx.enter_context(nc.sbuf_tensor("c_D", [32, 128], F32))
    b_D = Buf("c_D")
    p.add("gpsimd", lambda e: e.iota(Dm[:].rearrange("p (a b) -> p a b", b=32),
                                     [[0, 4], [1, 32]], base=0, channel_multiplier=-1,
                                     allow_small_or_imprecise_dtypes=True), writes=[b_D])
    p.add("vector", lambda e: e.tensor_scalar(Dm[:], Dm[:], 0.0, None, ALU.is_equal),
          reads=[b_D], writes=[b_D])
    C["D"] = (Dm, b_D)


def add_E1_const(nc, p, ctx, C):
    (v1, b_v1), (v2, b_v2), (v3, b_v3), (v4, b_v4), (v5, b_v5) = _scr(nc, p, ctx, 5)
    E1 = ctx.enter_context(nc.sbuf_tensor("c_E1", [128, 128], F32))
    b_E1 = Buf("c_E1")
    p.add("gpsimd", lambda e: e.iota(v1[:], [[0, 128]], base=0, channel_multiplier=1,
                                     allow_small_or_imprecise_dtypes=True), writes=[b_v1])
    p.add("gpsimd", lambda e: e.iota(v2[:], [[1, 128]], base=0, channel_multiplier=0,
                                     allow_small_or_imprecise_dtypes=True), writes=[b_v2])
    p.add("vector", lambda e: e.tensor_scalar(v3[:], v1[:], 0.0, None, ALU.is_equal),
          reads=[b_v1], writes=[b_v3])
    p.add("vector", lambda e: e.tensor_scalar(v4[:], v2[:], 64.0, None, ALU.is_lt),
          reads=[b_v2], writes=[b_v4])
    p.add("vector", lambda e: e.tensor_mul(v3[:], v3[:], v4[:]),
          reads=[b_v4, b_v3], writes=[b_v3])
    p.add("vector", lambda e: e.tensor_scalar(v5[:], v1[:], 64.0, None, ALU.is_equal),
          reads=[b_v1], writes=[b_v5])
    p.add("vector", lambda e: e.tensor_scalar(v2[:], v4[:], -1.0, 1.0, ALU.mult, ALU.add),
          reads=[b_v4], writes=[b_v2])
    p.add("vector", lambda e: e.tensor_mul(v5[:], v5[:], v2[:]),
          reads=[b_v5, b_v2], writes=[b_v5])
    p.add("vector", lambda e: e.tensor_add(E1[:], v5[:], v3[:]),
          reads=[b_v5, b_v3], writes=[b_E1])
    C["E1"] = (E1, b_E1)


def add_moe_consts(nc, p, ctx, C):
    idf32 = ctx.enter_context(nc.sbuf_tensor("c_idf32", [128, 128], F32))
    b_id = Buf("c_idf32")
    p.add("gpsimd", lambda e: e.iota(idf32[:], [[-1, 128]], base=0, channel_multiplier=1,
                                     allow_small_or_imprecise_dtypes=True), writes=[b_id])
    p.add("vector", lambda e: e.tensor_scalar(idf32[:], idf32[:], 0.0, None, ALU.is_equal),
          reads=[b_id], writes=[b_id])
    C["idf32"] = (idf32, b_id)
    S8 = ctx.enter_context(nc.sbuf_tensor("c_sel8", [8, 8 * 128], BF16))
    b_S8 = Buf("c_sel8")
    p.add("gpsimd", lambda e: e.iota(S8[:].rearrange("p (a b) -> p a b", b=128),
                                     [[1, 8], [0, 128]], base=0, channel_multiplier=-1,
                                     allow_small_or_imprecise_dtypes=True), writes=[b_S8])
    p.add("vector", lambda e: e.tensor_scalar(S8[:], S8[:], 0.0, None, ALU.is_equal),
          reads=[b_S8], writes=[b_S8])
    C["sel8"] = (S8, b_S8)


def ln_stats_normalize(nc, p, ctx, C, sx, b_sx, lnx, b_lnx, tag, pstat, b_pstat,
                       pbc, b_pbc, eps=LN_EPS):
    blist = b_lnx if isinstance(b_lnx, list) else [b_lnx] * (TOK // QC)
    onesf_col = C["onesf_col"][0]
    b_onesf = C["onesf_col"][1]
    ones_colb, b_ones_colb = C["ones_col"]

    xb = ctx.enter_context(nc.sbuf_tensor(f"{tag}_xb", [128, 2 * QC], BF16))
    xs = ctx.enter_context(nc.sbuf_tensor(f"{tag}_xs", [128, 2 * QC], BF16))
    b_xb, b_xs = p.buf(), p.buf()
    row = ctx.enter_context(nc.sbuf_tensor(f"{tag}_row", [1, QC], F32))
    row2 = ctx.enter_context(nc.sbuf_tensor(f"{tag}_row2", [1, QC], F32))
    rowm = ctx.enter_context(nc.sbuf_tensor(f"{tag}_rowm", [1, QC], F32))
    b_row, b_row2, b_rowm = p.buf(), p.buf(), p.buf()
    tmp = ctx.enter_context(nc.sbuf_tensor(f"{tag}_tmp", [128, QC], F32))
    b_tmp = p.buf()

    for qc in range(TOK // QC):
        for ic in range(8):
            ssl = slice(ic * TOK + qc * QC, ic * TOK + (qc + 1) * QC)
            dsl = slice((ic % 2) * QC, (ic % 2) * QC + QC)
            st = ic == 0
            sp = ic == 7
            p.add("vector", lambda e, ssl=ssl, dsl=dsl: e.tensor_copy(xb[:, dsl], sx[:, ssl]),
                  reads=[b_sx], writes=[b_xb])
            p.add("scalar", lambda e, ssl=ssl, dsl=dsl: e.activation(
                xs[:, dsl], sx[:, ssl], AF.Square),
                reads=[b_sx], writes=[b_xs])
            p.add("tensor", lambda e, dsl=dsl, st=st, sp=sp: e.matmul(
                pstat[0:1, 0:QC], ones_colb[:], xb[:, dsl], start=st, stop=sp,
                tile_position=(0, 0)),
                reads=[b_xb, b_ones_colb], writes=([b_pstat] if st or sp else []))
            p.add("tensor", lambda e, dsl=dsl, st=st, sp=sp: e.matmul(
                pstat[32:33, QC:2 * QC], ones_colb[:], xs[:, dsl], start=st, stop=sp,
                tile_position=(0, 32)),
                reads=[b_xs, b_ones_colb], writes=([b_pstat] if st or sp else []))
        p.add("vector", lambda e: e.tensor_scalar(
            rowm[:], pstat[0:1, 0:QC], 1.0 / D, None, ALU.mult),
            reads=[b_pstat], writes=[b_rowm])
        p.add("vector", lambda e: e.tensor_scalar(
            row[:], pstat[32:33, QC:2 * QC], 1.0 / D, None, ALU.mult),
            reads=[b_pstat], writes=[b_row])
        p.add("vector", lambda e: e.scalar_tensor_tensor(
            row2[:], rowm[:], -1.0, rowm[:], ALU.mult, ALU.mult),
            reads=[b_rowm], writes=[b_row2])
        p.add("vector", lambda e: e.tensor_add(
            row[:], row[:], row2[:]),
            reads=[b_row, b_row2], writes=[b_row])
        ceps, b_ceps = p.constf(eps)
        p.add("scalar", lambda e: e.activation(
            row[:], row[:], AF.Ln, bias=ceps[:]),
            reads=[b_row, b_ceps], writes=[b_row])
        p.add("scalar", lambda e: e.activation(
            row[:], row[:], AF.Exp, scale=-0.5),
            reads=[b_row], writes=[b_row])
        p.add("vector", lambda e: e.scalar_tensor_tensor(
            row2[:], rowm[:], -1.0, row[:], ALU.mult, ALU.mult),
            reads=[b_rowm, b_row], writes=[b_row2])
        p.add("tensor", lambda e: e.matmul(
            pbc[:, 0:QC], onesf_col[:], row[:], start=True, stop=True),
            reads=[b_row, b_onesf], writes=[b_pbc])
        p.add("tensor", lambda e: e.matmul(
            pbc[:, QC:2 * QC], onesf_col[:], row2[:], start=True, stop=True),
            reads=[b_row2, b_onesf], writes=[b_pbc])
        for ic in range(8):
            csl = slice(ic * TOK + qc * QC, ic * TOK + (qc + 1) * QC)
            p.add("vector", lambda e, csl=csl: e.tensor_mul(
                tmp[:], sx[:, csl], pbc[:, 0:QC]),
                reads=[b_sx, b_pbc], writes=[b_tmp])
            tok_v = p.add("vector", lambda e, csl=csl: e.tensor_add(
                lnx[:, csl], tmp[:], pbc[:, QC:2 * QC]),
                reads=[b_tmp, b_pbc], writes=[blist[qc]])
        tok_s = p.add("scalar", lambda e: e.activation(row[:], row[:], AF.Identity),
                      reads=[b_row], writes=[b_row])
        tok_t = p.add("tensor", lambda e: e.matmul(
            pbc[:, 0:QC], onesf_col[:], row[:], start=True, stop=True),
            reads=[b_row, b_onesf], writes=[b_pbc])
    return [tok_v, tok_s, tok_t]


QCH = 512


def make_quant_shared(nc, p, ctx, tag):
    sw = [ctx.enter_context(nc.sbuf_tensor(f"{tag}_sw{i}", [128, QCH], F32))
          for i in range(2)]
    swq = [ctx.enter_context(nc.sbuf_tensor(f"{tag}_swq{i}", [128, QCH], BF16))
           for i in range(2)]
    ta = ctx.enter_context(nc.sbuf_tensor(f"{tag}_ta", [128, QCH], F32))
    return dict(sw=sw, b_sw=[p.buf(), p.buf()], swq=swq,
                b_swq=[p.buf(), p.buf()], ta=ta, b_ta=p.buf())


def make_quant(nc, p, ctx, C, sh_, src_dram, n_rows, n_cols, scratch, b_scratch,
               tag, pg, b_pg):
    onesf_col, b_onesf = C["onesf_col"]
    onesf_c128, b_onesf128 = C["onesf_c128"]
    sw, b_sw = sh_["sw"], sh_["b_sw"]
    swq, b_swq = sh_["swq"], sh_["b_swq"]
    ta, b_ta = sh_["ta"], sh_["b_ta"]
    nch = n_rows // 128
    nhc = n_cols // QCH
    chunks = [(ic, hc) for ic in range(nch) for hc in range(nhc)]
    sabs = ctx.enter_context(nc.sbuf_tensor(f"{tag}_sabs", [128, max(8, len(chunks))], F32))
    b_sabs = p.buf()
    srow = ctx.enter_context(nc.sbuf_tensor(f"{tag}_srow", [128, 24], F32))
    sat, sthr, snthr = srow[:, 0:8], srow[:, 8:16], srow[:, 16:24]
    b_sat, b_sthr, b_snthr = p.buf(), p.buf(), p.buf()
    sg8 = ctx.enter_context(nc.sbuf_tensor(f"{tag}_sg8", [1, 8], F32))
    b_sg8 = p.buf()

    def pass1():
        for i, (ic, hc) in enumerate(chunks):
            par = i % 2
            p.dma("gpsimd", sw[par][:],
                  src_dram[ic * 128:(ic + 1) * 128, hc * QCH:(hc + 1) * QCH],
                  writes=[b_sw[par]])
            p.add("vector", lambda e, i=i, par=par: e.reduce_sum(
                sabs[:, i:i + 1], sw[par][:], axis=AX.X, apply_absolute_value=True),
                reads=[b_sw[par]], writes=[b_sabs])

    def mid():
        p.add("vector", lambda e: e.reduce_sum(sat[:, 0:1], sabs[:, 0:len(chunks)],
                                               axis=AX.X),
              reads=[b_sabs], writes=[b_sat])
        p.add("tensor", lambda e: e.matmul(pg[0:1, 0:8], sat[:, 0:1], onesf_c128[:, 0:8],
                                           start=True, stop=True),
              reads=[b_sat, b_onesf128], writes=[b_pg])
        ntot = float(n_rows * n_cols)
        p.add("vector", lambda e: e.tensor_scalar(sg8[:], pg[0:1, 0:8], 0.5 / ntot,
                                                  0.5e-6, ALU.mult, ALU.add),
              reads=[b_pg], writes=[b_sg8])
        p.add("tensor", lambda e: e.matmul(pg[:, 0:8], onesf_col[:], sg8[:],
                                           start=True, stop=True),
              reads=[b_sg8, b_onesf], writes=[b_pg])
        p.add("vector", lambda e: e.tensor_copy(sthr[:], pg[:, 0:8]),
              reads=[b_pg], writes=[b_sthr])
        p.add("vector", lambda e: e.tensor_scalar(snthr[:], sthr[:], -1.0, None, ALU.mult),
              reads=[b_sthr], writes=[b_snthr])

    def pass2():
        for i, (ic, hc) in enumerate(chunks):
            par = i % 2
            p.dma("gpsimd", sw[par][:],
                  src_dram[ic * 128:(ic + 1) * 128, hc * QCH:(hc + 1) * QCH],
                  writes=[b_sw[par]])
            p.add("vector", lambda e, par=par: e.tensor_scalar(
                ta[:], sw[par][:], snthr[:, 0:1], None, ALU.is_le),
                reads=[b_sw[par], b_snthr], writes=[b_ta])
            p.add("vector", lambda e, par=par: e.scalar_tensor_tensor(
                swq[par][:], sw[par][:], sthr[:, 0:1], ta[:], ALU.is_ge, ALU.subtract),
                reads=[b_sw[par], b_sthr, b_ta], writes=[b_swq[par]])
            p.dma("sync", scratch[ic * 128:(ic + 1) * 128, hc * QCH:(hc + 1) * QCH],
                  swq[par][:], reads=[b_swq[par]], writes=[b_scratch])

    return pass1, mid, pass2



def build_fused(debug=False):
    nc = bass.Bass("TRN2", target_bir_lowering=False)
    xin = nc.dram_tensor("xin", [D, TOK], F32, kind="ExternalInput").ap()
    pos0 = nc.dram_tensor("pos0", [1, 1], F32, kind="ExternalInput").ap()
    wq0 = nc.dram_tensor("wq0", [D, D], F32, kind="ExternalInput").ap()
    wk0 = nc.dram_tensor("wk0", [D, D], F32, kind="ExternalInput").ap()
    wv0 = nc.dram_tensor("wv0", [D, D], F32, kind="ExternalInput").ap()
    bqkv = nc.dram_tensor("bqkv", [1, 3 * D], F32, kind="ExternalInput").ap()
    wfcT = nc.dram_tensor("wfcT", [D, 4096], F32, kind="ExternalInput").ap()
    wprojT = nc.dram_tensor("wprojT", [4096, D], F32, kind="ExternalInput").ap()
    lnw = nc.dram_tensor("lnw", [1, 4096], F32, kind="ExternalInput").ap()
    lnb = nc.dram_tensor("lnb", [1, 4096], F32, kind="ExternalInput").ap()
    wq1 = nc.dram_tensor("wq1", [D, D], F32, kind="ExternalInput").ap()
    wk1 = nc.dram_tensor("wk1", [D, D], F32, kind="ExternalInput").ap()
    wv1 = nc.dram_tensor("wv1", [D, D], F32, kind="ExternalInput").ap()
    wo1 = nc.dram_tensor("wo1", [D, D], F32, kind="ExternalInput").ap()
    b1 = nc.dram_tensor("b1", [1, 4 * D], F32, kind="ExternalInput").ap()
    rwT = nc.dram_tensor("rwT", [D, 8], F32, kind="ExternalInput").ap()
    wgT = nc.dram_tensor("wgT", [8 * D, 2048], F32, kind="ExternalInput").ap()
    wuT = nc.dram_tensor("wuT", [8 * D, 2048], F32, kind="ExternalInput").ap()
    wdT = nc.dram_tensor("wdT", [8 * 2048, D], F32, kind="ExternalInput").ap()
    oX4 = nc.dram_tensor("oX4", [D, TOK], F32, kind="ExternalOutput").ap()
    if debug:
        dX1 = nc.dram_tensor("dX1", [D, TOK], F32, kind="ExternalOutput").ap()
        dX2 = nc.dram_tensor("dX2", [D, TOK], F32, kind="ExternalOutput").ap()
        dX3 = nc.dram_tensor("dX3", [D, TOK], F32, kind="ExternalOutput").ap()
        dX0 = nc.dram_tensor("dX0", [D, TOK], F32, kind="ExternalOutput").ap()
        dSM = nc.dram_tensor("dSM", [1, 104], F32, kind="ExternalOutput").ap()
        dBC = nc.dram_tensor("dBC", [128, 1], F32, kind="ExternalOutput").ap()
        dL = nc.dram_tensor("dL", [128, QC], F32, kind="ExternalOutput").ap()
        dQ = nc.dram_tensor("dQ", [128, QC], BF16, kind="ExternalOutput").ap()
        dK = nc.dram_tensor("dK", [128, QC], BF16, kind="ExternalOutput").ap()
        dV = nc.dram_tensor("dV", [128, QC], BF16, kind="ExternalOutput").ap()
        dDW = nc.dram_tensor("dDW", [8, TOK], F32, kind="ExternalOutput").ap()
        dLN = nc.dram_tensor("dLN", [128, TOK], BF16, kind="ExternalOutput").ap()
        dH = nc.dram_tensor("dH", [128, 1024], BF16, kind="ExternalOutput").ap()
        dSB = nc.dram_tensor("dSB", [128, 256], F32, kind="ExternalOutput").ap()
        dSC = nc.dram_tensor("dSC", [128, 1024], F32, kind="ExternalOutput").ap()
        dCH = nc.dram_tensor("dCH", [128, 2048], BF16, kind="ExternalOutput").ap()
    q0 = nc.dram_tensor("q0", [D, TOK], BF16, kind="Internal").ap()
    k0in = nc.dram_tensor("k0in", [D, TOK], BF16, kind="Internal").ap()
    v0in = nc.dram_tensor("v0in", [NHP, TOK, 128], BF16, kind="Internal").ap()
    s0in = nc.dram_tensor("s0in", [1, 16], BF16, kind="Internal").ap()
    k0out = nc.dram_tensor("k0out", [4, D, TOK], BF16, kind="Internal").ap()
    v0out = nc.dram_tensor("v0out", [4, NHP, TOK, 128], BF16, kind="Internal").ap()
    s0out = nc.dram_tensor("s0out", [4, 1, 16], BF16, kind="Internal").ap()
    q1 = nc.dram_tensor("q1", [D, TOK], BF16, kind="Internal").ap()
    k1in = nc.dram_tensor("k1in", [D, TOK], BF16, kind="Internal").ap()
    v1in = nc.dram_tensor("v1in", [NHP, TOK, 128], BF16, kind="Internal").ap()
    k1out = nc.dram_tensor("k1out", [4, D, TOK], BF16, kind="Internal").ap()
    v1out = nc.dram_tensor("v1out", [4, NHP, TOK, 128], BF16, kind="Internal").ap()
    wfcq = nc.dram_tensor("wfcq", [D, 4096], BF16, kind="Internal").ap()
    wprojq = nc.dram_tensor("wprojq", [4096, D], BF16, kind="Internal").ap()

    NKB = SEQ // 128
    from contextlib import ExitStack
    with ExitStack() as ctx:
        p = Prog(nc)
        p._ctx = ctx
        C = build_consts(nc, p, ctx)
        p.constf(LN_EPS)
        p.constf(FFN_LN_EPS)
        ones_row, b_ones_row = C["ones_row"]
        onesf_col, b_onesf = C["onesf_col"]
        onesf_colb, b_onesf_colb = C["onesf_colb"]
        ones_colb, b_ones_colb = C["ones_col"]

        sx = ctx.enter_context(nc.sbuf_tensor("sx", [128, 8 * TOK], F32))
        b_sx = p.buf("sx")
        nacc = ctx.enter_context(nc.sbuf_tensor("nacc", [1, 32], F32))
        b_nacc = p.buf("nacc")
        b_q0, b_k0in, b_v0in, b_s0in = p.buf(), p.buf(), p.buf(), p.buf()
        b_k0out, b_v0out, b_s0out = p.buf(), p.buf(), p.buf()
        b_q1, b_k1in, b_v1in = p.buf(), p.buf(), p.buf()
        b_k1out, b_v1out = p.buf(), p.buf()
        b_wfcq, b_wprojq = p.buf("wfcq"), p.buf("wprojq")

        def phase_A():
          with ExitStack() as actx:
            for ic in range(8):
                p.dma("sync", sx[:, ic * TOK:(ic + 1) * TOK],
                      xin[ic * 128:(ic + 1) * 128, :], writes=[b_sx])
            swq = actx.enter_context(nc.sbuf_tensor("a_swq", [128, 8 * D], BF16))
            swk = actx.enter_context(nc.sbuf_tensor("a_swk", [128, 8 * D], BF16))
            swv = actx.enter_context(nc.sbuf_tensor("a_swv", [128, 8 * D], BF16))
            b_swq, b_swk, b_swv = p.buf(), p.buf(), p.buf()
            sb_b = actx.enter_context(nc.sbuf_tensor("a_sbb", [1, 3 * D], BF16))
            b_sbb = p.buf()
            lnx = actx.enter_context(nc.sbuf_tensor("a_lnx", [128, 8 * TOK], BF16))
            b_lnx4 = [p.buf() for _ in range(TOK // QC)]
            for ic in range(8):
                p.dma("gpsimd", swq[:, ic * D:(ic + 1) * D],
                      wq0[ic * 128:(ic + 1) * 128, :], writes=[b_swq])
                p.dma("gpsimd", swk[:, ic * D:(ic + 1) * D],
                      wk0[ic * 128:(ic + 1) * 128, :], writes=[b_swk])
                p.dma("gpsimd", swv[:, ic * D:(ic + 1) * D],
                      wv0[ic * 128:(ic + 1) * 128, :], writes=[b_swv])
            p.dma("gpsimd", sb_b[:], bqkv, writes=[b_sbb])

            add_rope_consts(nc, p, actx, C)
            C2 = actx.enter_context(nc.sbuf_tensor("a_C2", [128, TOK], F32))
            S2 = actx.enter_context(nc.sbuf_tensor("a_S2", [128, TOK], F32))
            b_C2, b_S2 = p.buf(), p.buf()
            tctx = ExitStack()
            inv = tctx.enter_context(nc.sbuf_tensor("a_inv", [32, 1], F32))
            b_inv = p.buf()
            ramp = tctx.enter_context(nc.sbuf_tensor("a_ramp", [1, QC], F32))
            b_ramp = p.buf()
            spos = tctx.enter_context(nc.sbuf_tensor("a_spos", [1, 1], F32))
            b_spos = p.buf()
            ang = tctx.enter_context(nc.sbuf_tensor("a_ang", [32, QC], F32))
            wk1_ = tctx.enter_context(nc.sbuf_tensor("a_wk1", [32, QC], F32))
            wk2_ = tctx.enter_context(nc.sbuf_tensor("a_wk2", [32, QC], F32))
            b_ang, b_wk1, b_wk2 = p.buf(), p.buf(), p.buf()
            pA = actx.enter_context(nc.psum_tensor("a_pA", [128, 2 * QC], F32))
            pB = actx.enter_context(nc.psum_tensor("a_pB", [128, 2 * QC], F32))
            pC = actx.enter_context(nc.psum_tensor("a_pC", [128, 2 * QC], F32))
            pD = actx.enter_context(nc.psum_tensor("a_pD", [128, 2 * QC], F32))
            b_pA, b_pB, b_pC, b_pD = p.buf("pA"), p.buf("pB"), p.buf("pC"), p.buf("pD")
            R, b_R = C["R"]
            Dm, b_D = C["D"]
            prt, b_prt = pA, b_pA

            p.dma("sync", spos[:], pos0, writes=[b_spos])
            p.add("gpsimd", lambda e: e.iota(inv[:], [[0, 1]], base=0, channel_multiplier=1,
                                             allow_small_or_imprecise_dtypes=True),
                  writes=[b_inv])
            p.add("scalar", lambda e: e.activation(inv[:], inv[:], AF.Exp,
                                                   scale=-math.log(10000.0) / 32.0),
                  reads=[b_inv], writes=[b_inv])

            TWO_PI = 2.0 * math.pi
            CW_HI = 6.28125
            CW_LO = TWO_PI - CW_HI
            MAGIC = 1.5 * 2.0 ** 23

            for qc in range(TOK // QC):
                qsl = slice(qc * QC, (qc + 1) * QC)
                p.add("gpsimd", lambda e, qc=qc: e.iota(ramp[:], [[1, QC]], base=qc * QC,
                                                        channel_multiplier=0,
                                                        allow_small_or_imprecise_dtypes=True),
                      writes=[b_ramp])
                p.add("vector", lambda e: e.tensor_scalar(ramp[:], ramp[:], spos[:], None, ALU.add),
                      reads=[b_ramp, b_spos], writes=[b_ramp])
                p.add("tensor", lambda e: e.matmul(
                    prt[0:32, 0:QC], onesf_col[:, 0:32], ramp[:], start=True, stop=True),
                    reads=[b_ramp, b_onesf], writes=[b_prt])
                p.add("vector", lambda e: e.tensor_scalar(ang[:], prt[0:32, 0:QC], inv[:], None, ALU.mult),
                      reads=[b_prt, b_inv], writes=[b_ang])
                p.add("vector", lambda e: e.tensor_scalar(wk1_[:], ang[:], 1.0 / TWO_PI, MAGIC,
                                                          ALU.mult, ALU.add),
                      reads=[b_ang], writes=[b_wk1])
                p.add("vector", lambda e: e.tensor_scalar(wk1_[:], wk1_[:], MAGIC, None, ALU.subtract),
                      reads=[b_wk1], writes=[b_wk1])
                p.add("vector", lambda e: e.scalar_tensor_tensor(wk2_[:], wk1_[:], -CW_HI, ang[:],
                                                                 ALU.mult, ALU.add),
                      reads=[b_wk1, b_ang], writes=[b_wk2])
                p.add("vector", lambda e: e.scalar_tensor_tensor(wk2_[:], wk1_[:], -CW_LO, wk2_[:],
                                                                 ALU.mult, ALU.add),
                      reads=[b_wk1, b_wk2], writes=[b_wk2])
                p.add("scalar", lambda e: e.activation(ang[:], wk2_[:], AF.Sin),
                      reads=[b_wk2], writes=[b_ang])
                p.add("tensor", lambda e: e.matmul(prt[:, 0:QC], Dm[:], ang[:], start=True, stop=True),
                      reads=[b_ang, b_D], writes=[b_prt])
                p.add("vector", lambda e, qsl=qsl: e.tensor_copy(S2[:, qsl], prt[:, 0:QC]),
                      reads=[b_prt], writes=[b_S2])
                p.add("vector", lambda e: e.tensor_scalar(wk1_[:], wk2_[:], math.pi / 2.0, None, ALU.is_gt),
                      reads=[b_wk2], writes=[b_wk1])
                p.add("vector", lambda e: e.scalar_tensor_tensor(wk2_[:], wk1_[:], -TWO_PI, wk2_[:],
                                                                 ALU.mult, ALU.add),
                      reads=[b_wk1, b_wk2], writes=[b_wk2])
                p.add("vector", lambda e: e.tensor_scalar(wk2_[:], wk2_[:], math.pi / 2.0, None, ALU.add),
                      reads=[b_wk2], writes=[b_wk2])
                p.add("scalar", lambda e: e.activation(ang[:], wk2_[:], AF.Sin),
                      reads=[b_wk2], writes=[b_ang])
                p.add("tensor", lambda e: e.matmul(prt[:, 0:QC], Dm[:], ang[:], start=True, stop=True),
                      reads=[b_ang, b_D], writes=[b_prt])
                p.add("vector", lambda e, qsl=qsl: e.tensor_copy(C2[:, qsl], prt[:, 0:QC]),
                      reads=[b_prt], writes=[b_C2])

            ln_stats_normalize(nc, p, tctx, C, sx, b_sx, lnx, b_lnx4, "a_ln",
                               pB, b_pB, pC, b_pC)
            actx.enter_context(tctx)

            stg = actx.enter_context(nc.sbuf_tensor("a_stg", [128, TOK], BF16))
            b_stg = p.buf()
            qb = actx.enter_context(nc.sbuf_tensor("a_qb", [128, QC], BF16))
            b_qb = p.buf()
            tq = actx.enter_context(nc.sbuf_tensor("a_tq", [128, QC], F32))
            tr = actx.enter_context(nc.sbuf_tensor("a_tr", [128, QC], F32))
            b_tq, b_tr = p.buf(), p.buf()
            sq2 = actx.enter_context(nc.sbuf_tensor("a_sq2", [128, QC], BF16))
            b_sq2 = p.buf()
            nstrip = actx.enter_context(nc.sbuf_tensor("a_nstrip", [1, 128], F32))
            b_nstrip = p.buf()
            sk16 = actx.enter_context(nc.sbuf_tensor("a_sk16", [1, 16], BF16))
            b_sk16 = p.buf()

            pP, b_pP = pA, b_pA
            pR, b_pR = pB, b_pB
            pN, b_pN = pC, b_pC

            k_toks = []

            def emit_proj(kind):
                sw = [swq, swk][kind]
                b_sw = [b_swq, b_swk][kind]
                brow = kind
                out_t = [q0, k0in][kind]
                b_out = [b_q0, b_k0in][kind]
                for hp in range(NHP):
                    for qc in range(TOK // QC):
                        psl = slice((qc % 2) * QC, (qc % 2) * QC + QC)
                        qsl = slice(qc * QC, (qc + 1) * QC)
                        for ic in range(8):
                            st = ic == 0
                            p.add("tensor", lambda e, ic=ic, hp=hp, qsl=qsl, psl=psl, st=st, sw=sw: e.matmul(
                                pP[:, psl], sw[:, ic * D + hp * 128: ic * D + (hp + 1) * 128],
                                lnx[:, ic * TOK:][:, qsl], start=st, stop=False),
                                reads=[b_sw, b_lnx4[qc]], writes=([b_pP] if st else []))
                        p.add("tensor", lambda e, hp=hp, psl=psl, brow=brow: e.matmul(
                            pP[:, psl], sb_b[0:1, brow * D + hp * 128: brow * D + (hp + 1) * 128],
                            ones_row[:], start=False, stop=True),
                            reads=[b_sbb, b_ones_row], writes=[b_pP])
                        p.add("scalar", lambda e, psl=psl: e.activation(qb[:], pP[:, psl], AF.Copy),
                              reads=[b_pP], writes=[b_qb])
                        p.add("tensor", lambda e, psl=psl: e.matmul(
                            pR[:, psl], R[:], qb[:], start=True, stop=True),
                            reads=[b_qb, b_R], writes=[b_pR])
                        p.add("vector", lambda e, qsl=qsl: e.tensor_mul(tq[:], qb[:], C2[:, qsl]),
                              reads=[b_qb, b_C2], writes=[b_tq])
                        p.add("vector", lambda e, qsl=qsl, psl=psl: e.tensor_mul(
                            tr[:], pR[:, psl], S2[:, qsl]),
                              reads=[b_pR, b_S2], writes=[b_tr])
                        p.add("vector", lambda e, qsl=qsl: e.tensor_add(stg[:, qsl], tq[:], tr[:]),
                              reads=[b_tq, b_tr], writes=[b_stg])
                        p.add("scalar", lambda e, qsl=qsl: e.activation(sq2[:], stg[:, qsl], AF.Square),
                              reads=[b_stg], writes=[b_sq2])
                        p.add("tensor", lambda e: e.matmul(
                            pN[0:1, 0:QC], ones_colb[0:64, :], sq2[0:64, :],
                            start=True, stop=True, tile_position=(0, 0)),
                            reads=[b_sq2, b_ones_colb], writes=[b_pN])
                        p.add("tensor", lambda e: e.matmul(
                            pN[0:1, QC:2 * QC], ones_colb[64:128, :], sq2[64:128, :],
                            start=True, stop=True, tile_position=(64, 0)),
                            reads=[b_sq2, b_ones_colb], writes=[b_pN])
                        colA = ((kind * 8 + hp) * 2 + 0) * 4 + qc
                        colB = ((kind * 8 + hp) * 2 + 1) * 4 + qc
                        p.add("vector", lambda e, colA=colA: e.reduce_max(
                            nstrip[:, colA:colA + 1], pN[0:1, 0:QC], axis=AX.X),
                            reads=[b_pN], writes=[b_nstrip])
                        p.add("vector", lambda e, colB=colB: e.reduce_max(
                            nstrip[:, colB:colB + 1], pN[0:1, QC:2 * QC], axis=AX.X),
                            reads=[b_pN], writes=[b_nstrip])
                    tk = p.dma("sync", out_t[hp * 128:(hp + 1) * 128, :], stg[:],
                               reads=[b_stg], writes=[b_out])
                    if kind == 1:
                        k_toks.append(tk)

            emit_proj(1)
            p.add("vector", lambda e: e.reduce_max(
                nacc[:, 16:32], nstrip[:, 64:128].rearrange("p (h q) -> p h q", q=4),
                axis=AX.X),
                reads=[b_nstrip], writes=[b_nacc])
            p.add("vector", lambda e: e.tensor_scalar(
                sk16[:], nacc[:, 16:32], 1.01, None, ALU.mult),
                reads=[b_nacc], writes=[b_sk16])
            s_tok = p.dma("sync", s0in, sk16[:], reads=[b_sk16], writes=[b_s0in])
            p.add("gpsimd", lambda e: e.collective_compute(
                "AllGather", ALU.bypass, ins=[s0in], outs=[s0out], replica_groups=RG),
                reads=[b_s0in], writes=[b_s0out], deps=[s_tok])
            p.add("gpsimd", lambda e: e.collective_compute(
                "AllGather", ALU.bypass, ins=[k0in], outs=[k0out], replica_groups=RG),
                reads=[b_k0in], writes=[b_k0out], deps=k_toks)

            pV, b_pV = pD, b_pD
            sV0 = actx.enter_context(nc.sbuf_tensor("a_sV", [128, D], BF16))
            b_sV0 = p.buf()
            v_toks = []
            for tt in range(TOK // 128):
                tsl = slice(tt * 128, (tt + 1) * 128)
                for half in range(2):
                    hsl = slice(half * QC, (half + 1) * QC)
                    for ic in range(8):
                        st = ic == 0
                        sp = ic == 7
                        p.add("tensor", lambda e, ic=ic, tsl=tsl, hsl=hsl, st=st, sp=sp: e.matmul(
                            pV[:, hsl], lnx[:, ic * TOK:][:, tsl],
                            swv[:, ic * D:][:, hsl], start=st, stop=sp),
                            reads=[b_lnx4[tt // 4], b_swv], writes=([b_pV] if st else []))
                    p.add("tensor", lambda e, hsl=hsl: e.matmul(
                        pV[:, hsl], onesf_colb[:],
                        sb_b[0:1, 2 * D + hsl.start: 2 * D + hsl.stop],
                        start=False, stop=True),
                        reads=[b_sbb, b_onesf_colb], writes=[b_pV])
                p.add("scalar", lambda e: e.activation(sV0[:], pV[:, 0:D], AF.Copy),
                      reads=[b_pV], writes=[b_sV0])
                for hp in range(NHP):
                    v_toks.append(p.dma(
                        "sync", v0in[hp, tsl, :], sV0[:, hp * 128:(hp + 1) * 128],
                        reads=[b_sV0], writes=[b_v0in]))
            p.add("gpsimd", lambda e: e.collective_compute(
                "AllGather", ALU.bypass, ins=[v0in], outs=[v0out], replica_groups=RG),
                reads=[b_v0in], writes=[b_v0out], deps=v_toks)

            emit_proj(0)
            p.add("vector", lambda e: e.reduce_max(
                nacc[:, 0:16], nstrip[:, 0:64].rearrange("p (h q) -> p h q", q=4),
                axis=AX.X),
                reads=[b_nstrip], writes=[b_nacc])

        phase_A()

        p.begin_phase()
        def phase_B():
          with ExitStack() as bctx:
            add_E1_const(nc, p, bctx, C)
            E1, b_E1 = C["E1"]
            sQ = bctx.enter_context(nc.sbuf_tensor("b_sQ", [128, NHP * TOK], BF16))
            b_sQ = p.buf()
            sK = [bctx.enter_context(nc.sbuf_tensor(f"b_sK{i}", [128, SEQ], BF16)) for i in range(2)]
            sVt = [bctx.enter_context(nc.sbuf_tensor(f"b_sV{i}", [128, SEQ], BF16)) for i in range(2)]
            b_sK = [p.buf(), p.buf()]
            b_sV = [p.buf(), p.buf()]
            sP = bctx.enter_context(nc.sbuf_tensor("b_sP", [128, 3072], BF16))
            b_sP = [p.buf(), p.buf(), p.buf()]
            sl = bctx.enter_context(nc.sbuf_tensor("b_sl", [128, QC], F32))
            srl = bctx.enter_context(nc.sbuf_tensor("b_srl", [128, QC], F32))
            sO = bctx.enter_context(nc.sbuf_tensor("b_sO", [128, QC], F32))
            sC_ = bctx.enter_context(nc.sbuf_tensor("b_sC", [128, QC], F32))
            b_sl, b_srl, b_sO, b_sC = p.buf(), p.buf(), p.buf(), p.buf()
            smq = bctx.enter_context(nc.sbuf_tensor("b_smq", [1, 16], F32))
            smkb = bctx.enter_context(nc.sbuf_tensor("b_smkb", [1, 64], BF16))
            smk = bctx.enter_context(nc.sbuf_tensor("b_smk", [1, 64], F32))
            sm1 = bctx.enter_context(nc.sbuf_tensor("b_sm1", [1, 8], F32))
            sm2 = bctx.enter_context(nc.sbuf_tensor("b_sm2", [1, 8], F32))
            sm3 = bctx.enter_context(nc.sbuf_tensor("b_sm3", [1, 8], F32))
            sbias_c = bctx.enter_context(nc.sbuf_tensor("b_sbias", [128, 1], F32))
            b_smq, b_smkb, b_smk, b_sm1, b_sbias = (p.buf(), p.buf(), p.buf(),
                                                    p.buf(), p.buf())

            pS = bctx.enter_context(nc.psum_tensor("b_pS", [128, 3072], F32))
            b_pSslot = [p.buf(), p.buf(), p.buf()]
            pO = bctx.enter_context(nc.psum_tensor("b_pO", [128, QC], F32))
            b_pO = p.buf()
            pL = bctx.enter_context(nc.psum_tensor("b_pL", [128, QC], F32))
            b_pL = p.buf()
            sacc = bctx.enter_context(nc.sbuf_tensor("b_sacc", [128, 1024], F32))
            b_sacc = p.buf()
            onesf1 = bctx.enter_context(nc.sbuf_tensor("b_onesf1", [128, 1], F32))
            b_onesf1 = p.buf()
            p.add("vector", lambda e: e.memset(onesf1[:], 1.0), writes=[b_onesf1])

            p.dma("sync", smkb[:], s0out.rearrange("r a h -> a (r h)"),
                  reads=[b_s0out], writes=[b_smkb])
            p.add("vector", lambda e: e.tensor_copy(smq[:], nacc[:, 0:16]),
                  reads=[b_nacc], writes=[b_smq])
            p.add("vector", lambda e: e.tensor_copy(smk[:], smkb[:]),
                  reads=[b_smkb], writes=[b_smk])
            p.add("vector", lambda e: e.memset(sm1[:], 0.0), writes=[b_sm1])
            p.add("vector", lambda e: e.memset(sm2[:], 0.0), writes=[b_sm1])
            p.add("vector", lambda e: e.reduce_max(sm1[:, 0:1], smq[:], axis=AX.X),
                  reads=[b_smq], writes=[b_sm1])
            p.add("vector", lambda e: e.reduce_max(sm2[:, 0:1], smk[:], axis=AX.X),
                  reads=[b_smk], writes=[b_sm1])
            p.add("vector", lambda e: e.tensor_mul(sm3[:], sm1[:], sm2[:]),
                  reads=[b_sm1], writes=[b_sm1])
            p.add("scalar", lambda e: e.activation(sm1[:], sm3[:], AF.Sqrt),
                  reads=[b_sm1], writes=[b_sm1])
            p.add("vector", lambda e: e.tensor_scalar(sm2[:], sm1[:],
                                                      -1.02 / 8.0, -0.5, ALU.mult, ALU.add),
                  reads=[b_sm1], writes=[b_sm1])
            p.add("tensor", lambda e: e.matmul(pO[:, 0:1], onesf_col[:], sm2[:, 0:1],
                                               start=True, stop=True),
                  reads=[b_sm1, b_onesf], writes=[b_pO])
            p.add("vector", lambda e: e.tensor_copy(sbias_c[:], pO[:, 0:1]),
                  reads=[b_pO], writes=[b_sbias])
            if debug:
                for ic in range(8):
                    p.dma("sync", dX0[ic * 128:(ic + 1) * 128, :],
                          sx[:, ic * TOK:(ic + 1) * TOK], reads=[b_sx])
                p.dma("sync", dSM[:, 0:64], smk[:], reads=[b_smk])
                p.dma("sync", dSM[:, 64:80], smq[:], reads=[b_smq])
                p.dma("sync", dSM[:, 80:88], sm1[:], reads=[b_sm1])
                p.dma("sync", dSM[:, 88:96], sm2[:], reads=[b_sm1])
                p.dma("sync", dSM[:, 96:104], sm3[:], reads=[b_sm1])
                p.dma("sync", dBC, sbias_c[:], reads=[b_sbias])

            p.dma("sync", sQ[:].rearrange("p (h t) -> p h t", h=NHP),
                  q0.rearrange("(h p) t -> p h t", p=128),
                  reads=[b_q0], writes=[b_sQ])

            def load_hp0(hp, par):
                for r in range(4):
                    p.dma("sync", sK[par][:, r * TOK:(r + 1) * TOK],
                          k0out[r, hp * 128:(hp + 1) * 128, :],
                          reads=[b_k0out], writes=[b_sK[par]])
                    p.dma("gpsimd",
                          sVt[par][:, r * TOK:(r + 1) * TOK]
                          .rearrange("p (kb d) -> p kb d", d=128),
                          v0out[r, hp].rearrange("(kb p) d -> p kb d", p=128),
                          reads=[b_v0out], writes=[b_sV[par]])

            load_hp0(0, 0)
            for hp in range(NHP):
                par = hp % 2
                if hp + 1 < NHP:
                    load_hp0(hp + 1, (hp + 1) % 2)
                for qc in range(TOK // QC):
                    qoff = hp * TOK + qc * QC

                    def emit_S(g, qoff=qoff, par=par):
                        slot = g % 3
                        scol = slot * 1024
                        p.add("tensor", lambda e, g=g, scol=scol, qoff=qoff, par=par: e.matmul(
                            pS[:, scol:scol + QC], sK[par][0:64, g * 128:(g + 1) * 128],
                            sQ[0:64, qoff:qoff + QC], start=True, stop=True,
                            tile_position=(0, 0)),
                            reads=[b_sK[par], b_sQ], writes=[b_pSslot[slot]])
                        p.add("tensor", lambda e, g=g, scol=scol, qoff=qoff, par=par: e.matmul(
                            pS[:, scol + QC:scol + 1024], sK[par][64:128, g * 128:(g + 1) * 128],
                            sQ[64:128, qoff:qoff + QC], start=True, stop=True,
                            tile_position=(64, 0)),
                            reads=[b_sK[par], b_sQ], writes=[b_pSslot[slot]])
                        p.add("scalar", lambda e, scol=scol: e.activation(
                            sP[:, scol:scol + 1024], pS[:, scol:scol + 1024], AF.Exp,
                            bias=sbias_c[:], scale=0.125),
                            reads=[b_pSslot[slot], b_sbias], writes=[b_sP[slot]])

                    def emit_PV(g, par=par):
                        slot = g % 3
                        scol = slot * 1024
                        st, sp_ = g == 0, g == NKB - 1
                        p.add("tensor", lambda e, g=g, scol=scol, par=par, st=st, sp_=sp_: e.matmul(
                            pO[0:64, :], sVt[par][:, g * 128:g * 128 + 64],
                            sP[:, scol:scol + QC], start=st, stop=sp_,
                            tile_position=(0, 0)),
                            reads=[b_sV[par], b_sP[slot]],
                            writes=([b_pO] if st or sp_ else []))
                        p.add("tensor", lambda e, g=g, scol=scol, par=par, st=st, sp_=sp_: e.matmul(
                            pO[64:128, :], sVt[par][:, g * 128 + 64:g * 128 + 128],
                            sP[:, scol + QC:scol + 1024], start=st, stop=sp_,
                            tile_position=(0, 64)),
                            reads=[b_sV[par], b_sP[slot]],
                            writes=([b_pO] if st or sp_ else []))
                        p.add("vector", lambda e, scol=scol, st=(g == 0): e.tensor_tensor(
                            sacc[:], sacc[:], sP[:, scol:scol + 1024], ALU.add)
                            if not st else e.tensor_copy(sacc[:], sP[:, scol:scol + 1024]),
                            reads=[b_sP[slot]], writes=[b_sacc])

                    for g in range(NKB + 2):
                        if g < NKB:
                            emit_S(g)
                        if g >= 2:
                            emit_PV(g - 2)
                    p.add("vector", lambda e: e.memset(pL[:], 1.0), writes=[b_pL])
                    p.add("tensor", lambda e: e.matmul(
                        pL[0:1, :], onesf1[:], sacc[:, 0:QC], start=True, stop=True),
                        reads=[b_sacc, b_onesf1], writes=[b_pL])
                    p.add("tensor", lambda e: e.matmul(
                        pL[64:65, :], onesf1[:], sacc[:, QC:2 * QC], start=True, stop=True),
                        reads=[b_sacc, b_onesf1], writes=[b_pL])
                    p.add("vector", lambda e: e.tensor_copy(sl[:], pL[:]),
                          reads=[b_pL], writes=[b_sl])
                    p.add("vector", lambda e: e.reciprocal(srl[:], sl[:]),
                          reads=[b_sl], writes=[b_srl])
                    p.add("tensor", lambda e: e.matmul(pS[:, 0:QC], E1[:], srl[:],
                                                       start=True, stop=True),
                          reads=[b_srl, b_E1], writes=[b_pSslot[0]])
                    p.add("vector", lambda e: e.tensor_copy(sO[:], pO[:]),
                          reads=[b_pO], writes=[b_sO])
                    p.add("vector", lambda e: e.tensor_mul(sC_[:], sO[:], pS[:, 0:QC]),
                          reads=[b_sO, b_pSslot[0]], writes=[b_sC])
                    p.add("vector", lambda e, qc=qc, hp=hp: e.tensor_add(
                        sx[:, hp * TOK + qc * QC: hp * TOK + (qc + 1) * QC],
                        sx[:, hp * TOK + qc * QC: hp * TOK + (qc + 1) * QC], sC_[:]),
                        reads=[b_sC], writes=[b_sx])
                    if debug and hp == 0 and qc == 0:
                        p.dma("sync", dL, sl[:], reads=[b_sl])
                        p.dma("sync", dQ, sQ[:, 0:QC], reads=[b_sQ])
                        p.dma("sync", dK, sK[0][:, 0:QC], reads=[b_sK[0]])
                        p.dma("sync", dV, sVt[0][:, 0:QC], reads=[b_sV[0]])
            if debug:
                for ic in range(8):
                    p.dma("sync", dX1[ic * 128:(ic + 1) * 128, :],
                          sx[:, ic * TOK:(ic + 1) * TOK], reads=[b_sx])

        phase_B()

        p.begin_phase()
        def phase_C():
          with ExitStack() as cctx:
            pH = cctx.enter_context(nc.psum_tensor("c_pH", [128, 4 * QC], F32))
            b_pH = p.buf()
            pstat = cctx.enter_context(nc.psum_tensor("c_pstat", [128, 2 * QC], F32))
            b_pstat = p.buf()
            pbc = cctx.enter_context(nc.psum_tensor("c_pbc", [128, 2 * QC], F32))
            b_pbc = p.buf()

            lnx = cctx.enter_context(nc.sbuf_tensor("c_lnx", [128, 8 * TOK], BF16))
            b_lnx4 = [p.buf() for _ in range(TOK // QC)]
            qsh = make_quant_shared(nc, p, cctx, "c_q")
            qf1, qfm, qf2 = make_quant(nc, p, cctx, C, qsh, wfcT, D, 4096,
                                       wfcq, b_wfcq, "c_qf", pH, b_pH)
            qp1, qpm, qp2 = make_quant(nc, p, cctx, C, qsh, wprojT, 4096, D,
                                       wprojq, b_wprojq, "c_qp", pH, b_pH)
            qf1(); qfm(); qf2()
            ln_stats_normalize(nc, p, cctx, C, sx, b_sx, lnx, b_lnx4, "c_ln",
                               pstat, b_pstat, pbc, b_pbc)
            qp1(); qpm(); qp2()

            slnw = cctx.enter_context(nc.sbuf_tensor("c_slnw", [128, 32], F32))
            slnb = cctx.enter_context(nc.sbuf_tensor("c_slnb", [128, 32], F32))
            b_slnw, b_slnb = p.buf(), p.buf()
            p.add("sync", lambda e: e.dma_start(
                out=slnw[:], in_=lnw.rearrange("a (c p) -> (a p) c", p=128),
                allow_slow_non_contiguous=True), writes=[b_slnw], dma=True)
            p.add("sync", lambda e: e.dma_start(
                out=slnb[:], in_=lnb.rearrange("a (c p) -> (a p) c", p=128),
                allow_slow_non_contiguous=True), writes=[b_slnb], dma=True)

            swf = [cctx.enter_context(nc.sbuf_tensor(f"c_swf{i}", [128, 8 * QC], BF16))
                   for i in range(2)]
            b_swf = [p.buf(), p.buf()]
            swp = [cctx.enter_context(nc.sbuf_tensor(f"c_swp{i}", [128, D], BF16))
                   for i in range(2)]
            b_swp = [p.buf(), p.buf()]
            sh = cctx.enter_context(nc.sbuf_tensor("c_sh", [128, 32 * QC], BF16))
            b_sh = p.buf()
            shsq = cctx.enter_context(nc.sbuf_tensor("c_shsq", [128, QC], BF16))
            b_shsq = p.buf()
            hrow = cctx.enter_context(nc.sbuf_tensor("c_hrow", [1, QC], F32))
            hrow2 = cctx.enter_context(nc.sbuf_tensor("c_hrow2", [1, QC], F32))
            hrowm = cctx.enter_context(nc.sbuf_tensor("c_hrowm", [1, QC], F32))
            b_hrow, b_hrow2, b_hrowm = p.buf(), p.buf(), p.buf()
            t1 = cctx.enter_context(nc.sbuf_tensor("c_t1", [128, QC], F32))
            t2 = cctx.enter_context(nc.sbuf_tensor("c_t2", [128, QC], F32))
            b_t1, b_t2 = p.buf(), p.buf()
            ceps, b_ceps = p.constf(FFN_LN_EPS)

            for tc in range(TOK // QC):
                tsl = slice(tc * QC, (tc + 1) * QC)
                for fb in range(8):
                    par = fb % 2
                    for ic in range(8):
                        p.dma("gpsimd", swf[par][:, ic * QC:(ic + 1) * QC],
                              wfcq[ic * 128:(ic + 1) * 128, fb * QC:(fb + 1) * QC],
                              reads=[b_wfcq], writes=[b_swf[par]])
                    for ft in range(4):
                        for ic in range(8):
                            st = ic == 0
                            sp_ = ic == 7
                            p.add("tensor", lambda e, ft=ft, ic=ic, par=par, tsl=tsl, st=st, sp_=sp_: e.matmul(
                                pH[:, ft * QC:(ft + 1) * QC],
                                swf[par][:, ic * QC + ft * 128: ic * QC + (ft + 1) * 128],
                                lnx[:, ic * TOK:][:, tsl], start=st, stop=sp_),
                                reads=[b_swf[par], b_lnx4[tc]],
                                writes=([b_pH] if st or sp_ else []))
                    p.add("scalar", lambda e, fb=fb: e.activation(
                        sh[:, fb * 4 * QC:(fb + 1) * 4 * QC], pH[:], AF.Gelu),
                        reads=[b_pH], writes=[b_sh])
                    for ft in range(4):
                        fc = fb * 4 + ft
                        st = fc == 0
                        sp_ = fc == 31
                        p.add("scalar", lambda e, fc=fc: e.activation(
                            shsq[:], sh[:, fc * QC:(fc + 1) * QC], AF.Square),
                            reads=[b_sh], writes=[b_shsq])
                        p.add("tensor", lambda e, fc=fc, st=st, sp_=sp_: e.matmul(
                            pstat[0:1, 0:QC], ones_colb[:], sh[:, fc * QC:(fc + 1) * QC],
                            start=st, stop=sp_, tile_position=(0, 0)),
                            reads=[b_sh, b_ones_colb], writes=([b_pstat] if st or sp_ else []))
                        p.add("tensor", lambda e, st=st, sp_=sp_: e.matmul(
                            pstat[32:33, QC:2 * QC], ones_colb[:], shsq[:],
                            start=st, stop=sp_, tile_position=(0, 32)),
                            reads=[b_shsq, b_ones_colb], writes=([b_pstat] if st or sp_ else []))
                p.add("vector", lambda e: e.tensor_scalar(
                    hrowm[:], pstat[0:1, 0:QC], 1.0 / 4096.0, None, ALU.mult),
                    reads=[b_pstat], writes=[b_hrowm])
                p.add("vector", lambda e: e.tensor_scalar(
                    hrow[:], pstat[32:33, QC:2 * QC], 1.0 / 4096.0, None, ALU.mult),
                    reads=[b_pstat], writes=[b_hrow])
                p.add("vector", lambda e: e.scalar_tensor_tensor(
                    hrow2[:], hrowm[:], -1.0, hrowm[:], ALU.mult, ALU.mult),
                    reads=[b_hrowm], writes=[b_hrow2])
                p.add("vector", lambda e: e.tensor_add(hrow[:], hrow[:], hrow2[:]),
                      reads=[b_hrow, b_hrow2], writes=[b_hrow])
                p.add("scalar", lambda e: e.activation(hrow[:], hrow[:], AF.Ln, bias=ceps[:]),
                      reads=[b_hrow, b_ceps], writes=[b_hrow])
                p.add("scalar", lambda e: e.activation(hrow[:], hrow[:], AF.Exp, scale=-0.5),
                      reads=[b_hrow], writes=[b_hrow])
                p.add("vector", lambda e: e.scalar_tensor_tensor(
                    hrow2[:], hrowm[:], -1.0, hrow[:], ALU.mult, ALU.mult),
                    reads=[b_hrowm, b_hrow], writes=[b_hrow2])
                p.add("tensor", lambda e: e.matmul(
                    pbc[:, 0:QC], onesf_col[:], hrow[:], start=True, stop=True),
                    reads=[b_hrow, b_onesf], writes=[b_pbc])
                p.add("tensor", lambda e: e.matmul(
                    pbc[:, QC:2 * QC], onesf_col[:], hrow2[:], start=True, stop=True),
                    reads=[b_hrow2, b_onesf], writes=[b_pbc])
                for fc in range(32):
                    fsl = slice(fc * QC, (fc + 1) * QC)
                    p.add("vector", lambda e, fsl=fsl: e.tensor_mul(
                        t1[:], sh[:, fsl], pbc[:, 0:QC]),
                        reads=[b_sh, b_pbc], writes=[b_t1])
                    p.add("vector", lambda e, fsl=fsl: e.tensor_add(
                        t2[:], t1[:], pbc[:, QC:2 * QC]),
                        reads=[b_t1, b_pbc], writes=[b_t2])
                    p.add("vector", lambda e, fc=fc, fsl=fsl: e.tensor_scalar(
                        sh[:, fsl], t2[:], slnw[:, fc:fc + 1], slnb[:, fc:fc + 1],
                        ALU.mult, ALU.add),
                        reads=[b_t2, b_slnw, b_slnb], writes=[b_sh])
                ybanks = [(pH, b_pH, 0), (pH, b_pH, 1), (pH, b_pH, 2), (pH, b_pH, 3),
                          (pstat, b_pstat, 0), (pstat, b_pstat, 1),
                          (pbc, b_pbc, 0), (pbc, b_pbc, 1)]
                for fc in range(32):
                    par = fc % 2
                    p.dma("gpsimd", swp[par][:],
                          wprojq[fc * 128:(fc + 1) * 128, :],
                          reads=[b_wprojq], writes=[b_swp[par]])
                    for dt_ in range(8):
                        yt, b_yt, bi = ybanks[dt_]
                        st = fc == 0
                        sp_ = fc == 31
                        p.add("tensor", lambda e, fc=fc, dt_=dt_, par=par, yt=yt, bi=bi, st=st, sp_=sp_: e.matmul(
                            yt[:, bi * QC:(bi + 1) * QC],
                            swp[par][:, dt_ * 128:(dt_ + 1) * 128],
                            sh[:, fc * QC:(fc + 1) * QC], start=st, stop=sp_),
                            reads=[b_swp[par], b_sh],
                            writes=([b_yt] if st or sp_ else []))
                for dt_ in range(8):
                    yt, b_yt, bi = ybanks[dt_]
                    p.add("vector", lambda e, dt_=dt_, yt=yt, bi=bi, tsl=tsl: e.tensor_add(
                        sx[:, dt_ * TOK:][:, tsl], sx[:, dt_ * TOK:][:, tsl],
                        yt[:, bi * QC:(bi + 1) * QC]),
                        reads=[b_yt], writes=[b_sx])
            if debug:
                for ic in range(8):
                    p.dma("sync", dX2[ic * 128:(ic + 1) * 128, :],
                          sx[:, ic * TOK:(ic + 1) * TOK], reads=[b_sx])

        phase_C()

        p.begin_phase()
        def phase_D():
          with ExitStack() as dctx:
            swq = dctx.enter_context(nc.sbuf_tensor("d_swq", [128, 8 * D], BF16))
            swk = dctx.enter_context(nc.sbuf_tensor("d_swk", [128, 8 * D], BF16))
            swv = dctx.enter_context(nc.sbuf_tensor("d_swv", [128, 8 * D], BF16))
            b_swq, b_swk, b_swv = p.buf(), p.buf(), p.buf()
            sb_b = dctx.enter_context(nc.sbuf_tensor("d_sbb", [1, 4 * D], BF16))
            b_sbb = p.buf()
            lnx = dctx.enter_context(nc.sbuf_tensor("d_lnx", [128, 8 * TOK], BF16))
            b_lnx4 = [p.buf() for _ in range(TOK // QC)]
            stg = dctx.enter_context(nc.sbuf_tensor("d_stg", [128, TOK], BF16))
            b_stg = p.buf()
            sV0 = dctx.enter_context(nc.sbuf_tensor("d_sV", [128, D], BF16))
            b_sV0 = p.buf()

            pP = dctx.enter_context(nc.psum_tensor("d_pP", [128, 2 * QC], F32))
            b_pP = p.buf()
            pstat = dctx.enter_context(nc.psum_tensor("d_pstat", [128, 2 * QC], F32))
            b_pstat = p.buf()
            pbc = dctx.enter_context(nc.psum_tensor("d_pbc", [128, 2 * QC], F32))
            b_pbc = p.buf()
            pV = dctx.enter_context(nc.psum_tensor("d_pV", [128, 2 * QC], F32))
            b_pV = p.buf()

            for ic in range(8):
                p.dma("gpsimd", swq[:, ic * D:(ic + 1) * D],
                      wq1[ic * 128:(ic + 1) * 128, :], writes=[b_swq])
                p.dma("gpsimd", swk[:, ic * D:(ic + 1) * D],
                      wk1[ic * 128:(ic + 1) * 128, :], writes=[b_swk])
                p.dma("gpsimd", swv[:, ic * D:(ic + 1) * D],
                      wv1[ic * 128:(ic + 1) * 128, :], writes=[b_swv])
            p.dma("gpsimd", sb_b[:], b1, writes=[b_sbb])

            ln_stats_normalize(nc, p, dctx, C, sx, b_sx, lnx, b_lnx4, "d_ln",
                               pstat, b_pstat, pbc, b_pbc)

            k_toks1 = []
            q_toks1 = []

            def emit_projD(kind):
                sw = [swq, swk][kind]
                b_sw = [b_swq, b_swk][kind]
                out_t = [q1, k1in][kind]
                b_out = [b_q1, b_k1in][kind]
                for hp in range(NHP):
                    for qc in range(TOK // QC):
                        psl = slice((qc % 2) * QC, (qc % 2) * QC + QC)
                        qsl = slice(qc * QC, (qc + 1) * QC)
                        for ic in range(8):
                            st = ic == 0
                            p.add("tensor", lambda e, ic=ic, hp=hp, qsl=qsl, psl=psl, st=st, sw=sw: e.matmul(
                                pP[:, psl], sw[:, ic * D + hp * 128: ic * D + (hp + 1) * 128],
                                lnx[:, ic * TOK:][:, qsl], start=st, stop=False),
                                reads=[b_sw, b_lnx4[qc]], writes=([b_pP] if st else []))
                        p.add("tensor", lambda e, hp=hp, psl=psl, kind=kind: e.matmul(
                            pP[:, psl], sb_b[0:1, kind * D + hp * 128: kind * D + (hp + 1) * 128],
                            ones_row[:], start=False, stop=True),
                            reads=[b_sbb, b_ones_row], writes=[b_pP])
                        p.add("vector", lambda e, qsl=qsl, psl=psl: e.tensor_scalar(
                            stg[:, qsl], pP[:, psl], 1.0, None, ALU.is_ge),
                            reads=[b_pP], writes=[b_stg])
                    tk = p.dma("sync", out_t[hp * 128:(hp + 1) * 128, :], stg[:],
                               reads=[b_stg], writes=[b_out])
                    (q_toks1 if kind == 0 else k_toks1).append(tk)

            emit_projD(1)
            p.add("gpsimd", lambda e: e.collective_compute(
                "AllGather", ALU.bypass, ins=[k1in], outs=[k1out], replica_groups=RG),
                reads=[b_k1in], writes=[b_k1out], deps=k_toks1)

            v_toks1 = []
            for tt in range(TOK // 128):
                tsl = slice(tt * 128, (tt + 1) * 128)
                for half in range(2):
                    hsl = slice(half * QC, (half + 1) * QC)
                    for ic in range(8):
                        st = ic == 0
                        p.add("tensor", lambda e, ic=ic, tsl=tsl, hsl=hsl, st=st: e.matmul(
                            pV[:, hsl], lnx[:, ic * TOK:][:, tsl],
                            swv[:, ic * D:][:, hsl], start=st, stop=False),
                            reads=[b_lnx4[tt // 4], b_swv], writes=([b_pV] if st else []))
                    p.add("tensor", lambda e, hsl=hsl: e.matmul(
                        pV[:, hsl], onesf_colb[:],
                        sb_b[0:1, 2 * D + hsl.start: 2 * D + hsl.stop],
                        start=False, stop=True),
                        reads=[b_sbb, b_onesf_colb], writes=[b_pV])
                p.add("scalar", lambda e: e.activation(sV0[:], pV[:, 0:D], AF.Copy),
                      reads=[b_pV], writes=[b_sV0])
                for hp in range(NHP):
                    v_toks1.append(p.dma(
                        "sync", v1in[hp, tsl, :], sV0[:, hp * 128:(hp + 1) * 128],
                        reads=[b_sV0], writes=[b_v1in]))
            p.add("gpsimd", lambda e: e.collective_compute(
                "AllGather", ALU.bypass, ins=[v1in], outs=[v1out], replica_groups=RG),
                reads=[b_v1in], writes=[b_v1out], deps=v_toks1)

            emit_projD(0)

        phase_D()

        p.begin_phase()
        def phase_E():
          with ExitStack() as ectx:
            sctx_t = ectx.enter_context(nc.sbuf_tensor("e_sctx", [128, NHP * TOK], BF16))
            b_sctx = p.buf()
            pS = ectx.enter_context(nc.psum_tensor("e_pS", [128, 3072], F32))
            b_pSslot = [p.buf(), p.buf(), p.buf()]
            pO = ectx.enter_context(nc.psum_tensor("e_pO", [128, QC], F32))
            b_pO = p.buf()
            with ExitStack() as e1ctx:
                sQ = e1ctx.enter_context(nc.sbuf_tensor("e_sQ", [128, NHP * TOK], BF16))
                b_sQ = p.buf()
                sK = [e1ctx.enter_context(nc.sbuf_tensor(f"e_sK{i}", [128, SEQ], BF16))
                      for i in range(2)]
                sVt = [e1ctx.enter_context(nc.sbuf_tensor(f"e_sV{i}", [128, SEQ], BF16))
                       for i in range(2)]
                b_sK = [p.buf(), p.buf()]
                b_sV = [p.buf(), p.buf()]
                sP = e1ctx.enter_context(nc.sbuf_tensor("e_sP", [128, 3072], BF16))
                b_sP = [p.buf(), p.buf(), p.buf()]

                p.dma("sync", sQ[:].rearrange("p (h t) -> p h t", h=NHP),
                      q1.rearrange("(h p) t -> p h t", p=128),
                      reads=[b_q1], writes=[b_sQ])

                def load_hp1(hp, par):
                    for r in range(4):
                        p.dma("sync", sK[par][:, r * TOK:(r + 1) * TOK],
                              k1out[r, hp * 128:(hp + 1) * 128, :],
                              reads=[b_k1out], writes=[b_sK[par]])
                        p.dma("gpsimd",
                              sVt[par][:, r * TOK:(r + 1) * TOK]
                              .rearrange("p (kb d) -> p kb d", d=128),
                              v1out[r, hp].rearrange("(kb p) d -> p kb d", p=128),
                              reads=[b_v1out], writes=[b_sV[par]])

                load_hp1(0, 0)
                for hp in range(NHP):
                    par = hp % 2
                    if hp + 1 < NHP:
                        load_hp1(hp + 1, (hp + 1) % 2)
                    for qc in range(TOK // QC):
                        qoff = hp * TOK + qc * QC

                        def emit_S1(g, qoff=qoff, par=par):
                            slot = g % 3
                            scol = slot * 1024
                            p.add("tensor", lambda e, g=g, scol=scol, qoff=qoff, par=par: e.matmul(
                                pS[:, scol:scol + QC], sK[par][0:64, g * 128:(g + 1) * 128],
                                sQ[0:64, qoff:qoff + QC], start=True, stop=True,
                                tile_position=(0, 0)),
                                reads=[b_sK[par], b_sQ], writes=[b_pSslot[slot]])
                            p.add("tensor", lambda e, g=g, scol=scol, qoff=qoff, par=par: e.matmul(
                                pS[:, scol + QC:scol + 1024], sK[par][64:128, g * 128:(g + 1) * 128],
                                sQ[64:128, qoff:qoff + QC], start=True, stop=True,
                                tile_position=(64, 0)),
                                reads=[b_sK[par], b_sQ], writes=[b_pSslot[slot]])
                            p.add("scalar", lambda e, scol=scol: e.activation(
                                sP[:, scol:scol + 1024], pS[:, scol:scol + 1024], AF.Sigmoid,
                                scale=0.125),
                                reads=[b_pSslot[slot]], writes=[b_sP[slot]])

                        def emit_PV1(g, par=par):
                            slot = g % 3
                            scol = slot * 1024
                            st, sp_ = g == 0, g == NKB - 1
                            p.add("tensor", lambda e, g=g, scol=scol, par=par, st=st, sp_=sp_: e.matmul(
                                pO[0:64, :], sVt[par][:, g * 128:g * 128 + 64],
                                sP[:, scol:scol + QC], start=st, stop=sp_,
                                tile_position=(0, 0)),
                                reads=[b_sV[par], b_sP[slot]],
                                writes=([b_pO] if st or sp_ else []))
                            p.add("tensor", lambda e, g=g, scol=scol, par=par, st=st, sp_=sp_: e.matmul(
                                pO[64:128, :], sVt[par][:, g * 128 + 64:g * 128 + 128],
                                sP[:, scol + QC:scol + 1024], start=st, stop=sp_,
                                tile_position=(0, 64)),
                                reads=[b_sV[par], b_sP[slot]],
                                writes=([b_pO] if st or sp_ else []))

                        for g in range(NKB + 2):
                            if g < NKB:
                                emit_S1(g)
                            if g >= 2:
                                emit_PV1(g - 2)
                        p.add("vector", lambda e, hp=hp, qc=qc: e.tensor_copy(
                            sctx_t[:, hp * TOK + qc * QC: hp * TOK + (qc + 1) * QC],
                            pO[:]),
                            reads=[b_pO], writes=[b_sctx])

            p.begin_phase()
            swo = ectx.enter_context(nc.sbuf_tensor("e_swo", [128, 8 * D], BF16))
            b_swo = p.buf()
            sb_b = ectx.enter_context(nc.sbuf_tensor("e_sbb", [1, 4 * D], BF16))
            b_sbb = p.buf()
            ssp = ectx.enter_context(nc.sbuf_tensor("e_ssp", [128, QC], F32))
            b_ssp = p.buf()
            for ic in range(8):
                p.dma("gpsimd", swo[:, ic * D:(ic + 1) * D],
                      wo1[ic * 128:(ic + 1) * 128, :], writes=[b_swo])
            p.dma("gpsimd", sb_b[:], b1, writes=[b_sbb])

            for hp in range(NHP):
                for qc in range(TOK // QC):
                    psl = slice((qc % 2) * QC, (qc % 2) * QC + QC)
                    for ic in range(8):
                        st = ic == 0
                        p.add("tensor", lambda e, ic=ic, hp=hp, qc=qc, psl=psl, st=st: e.matmul(
                            pS[:, psl], swo[:, ic * D + hp * 128: ic * D + (hp + 1) * 128],
                            sctx_t[:, ic * TOK + qc * QC: ic * TOK + (qc + 1) * QC],
                            start=st, stop=False),
                            reads=[b_swo, b_sctx], writes=([b_pSslot[0]] if st else []))
                    p.add("tensor", lambda e, hp=hp, psl=psl: e.matmul(
                        pS[:, psl], sb_b[0:1, 3 * D + hp * 128: 3 * D + (hp + 1) * 128],
                        ones_row[:], start=False, stop=True),
                        reads=[b_sbb, b_ones_row], writes=[b_pSslot[0]])
                    p.add("vector", lambda e, psl=psl: e.tensor_scalar(
                        ssp[:], pS[:, psl], 1.0, None, ALU.is_ge),
                        reads=[b_pSslot[0]], writes=[b_ssp])
                    p.add("vector", lambda e, qc=qc, hp=hp: e.tensor_add(
                        sx[:, hp * TOK + qc * QC: hp * TOK + (qc + 1) * QC],
                        sx[:, hp * TOK + qc * QC: hp * TOK + (qc + 1) * QC], ssp[:]),
                        reads=[b_ssp], writes=[b_sx])
            if debug:
                for ic in range(8):
                    p.dma("sync", dX3[ic * 128:(ic + 1) * 128, :],
                          sx[:, ic * TOK:(ic + 1) * TOK], reads=[b_sx])

        phase_E()

        p.begin_phase()
        def phase_F():
          with ExitStack() as fctx:
            add_moe_consts(nc, p, fctx, C)
            sel8, b_sel8 = C["sel8"]
            idf32, b_idf32 = C["idf32"]

            lnx = fctx.enter_context(nc.sbuf_tensor("f_lnx", [128, 8 * TOK], BF16))
            b_lnx4 = [p.buf() for _ in range(TOK // QC)]
            sh = fctx.enter_context(nc.sbuf_tensor("f_sh", [128, 16 * 1024], BF16))
            b_sh = p.buf()
            srw = fctx.enter_context(nc.sbuf_tensor("f_srw", [128, 8 * 8], BF16))
            b_srw = p.buf()
            sdwT = fctx.enter_context(nc.sbuf_tensor("f_sdwT", [8, TOK], BF16))
            b_sdwT = p.buf()

            pG = fctx.enter_context(nc.psum_tensor("f_pG", [128, 4 * QC], F32))
            b_pG = p.buf()
            b_pG2 = p.buf()
            pU = fctx.enter_context(nc.psum_tensor("f_pU", [128, 2 * QC], F32))
            b_pU = p.buf()
            pBC = fctx.enter_context(nc.psum_tensor("f_pBC", [128, 2 * QC], F32))
            b_pBC = p.buf()

            for ic in range(8):
                p.dma("gpsimd", srw[:, ic * 8:(ic + 1) * 8],
                      rwT[ic * 128:(ic + 1) * 128, :], writes=[b_srw])

            p.constf(LN_EPS)
            with ExitStack() as lctx:
                lndone = ln_stats_normalize(nc, p, lctx, C, sx, b_sx, lnx, b_lnx4, "f_ln",
                                            pU, b_pU, pBC, b_pBC)

            wall = fctx.enter_context(nc.sbuf_tensor("f_wall", [128, 88], F32))
            w8 = [wall[:, i * 8:(i + 1) * 8] for i in range(8)]
            b_w8 = [p.buf() for _ in range(8)]
            wm = [wall[:, 64 + i:65 + i] for i in range(2)]
            b_wm = [p.buf(), p.buf()]
            ssum = wall[:, 66:74]
            b_ssum = p.buf()
            srs = wall[:, 74:82]
            b_srs = p.buf()

            for tt in range(TOK // 128):
                tsl = slice(tt * 128, (tt + 1) * 128)
                for ic in range(8):
                    st = ic == 0
                    sp_ = ic == 7
                    p.add("tensor", lambda e, ic=ic, tsl=tsl, st=st, sp_=sp_: e.matmul(
                        pG[:, 0:8], lnx[:, ic * TOK:][:, tsl], srw[:, ic * 8:(ic + 1) * 8],
                        start=st, stop=sp_),
                        reads=[b_lnx4[tt // 4], b_srw], writes=([b_pG] if st or sp_ else []))
                slg, as_, f1, d1, k1, g1, msk, oh = w8
                bslg, bas, bf1, bd1, bk1, bg1, bmsk, boh = b_w8
                m1, m2 = wm
                bm1, bm2 = b_wm
                A = lambda eng, fn, rd, wr: p.add(eng, fn, reads=rd, writes=wr)
                p.add("vector", lambda e: e.tensor_copy(slg[:], pG[:, 0:8]),
                      reads=[b_pG], writes=[bslg],
                      deps=(lndone if tt == 0 else ()))
                A("vector", lambda e: e.memset(ssum[:], 1.0), [], [b_ssum])
                from concourse.bass import broadcast_tensor_aps as _bc2
                def _maxb(out, a, mcol):
                    aa, mm = _bc2(a, mcol)
                    return lambda e: e.tensor_tensor(out, aa, mm, ALU.max)
                A("vector", lambda e: e.reduce_max(m1[:], slg[:], axis=AX.X), [bslg], [bm1])
                A("vector", lambda e: e.tensor_scalar(as_[:], slg[:], -1.0, None, ALU.mult), [bslg], [bas])
                A("vector", lambda e: e.tensor_max(as_[:], as_[:], slg[:]), [bslg, bas], [bas])
                A("vector", _maxb(f1[:], as_[:], m1[:]), [bas, bm1], [bf1])
                A("vector", lambda e: e.tensor_scalar(d1[:], slg[:], m1[:], None, ALU.subtract), [bslg, bm1], [bd1])
                A("vector", lambda e: e.scalar_tensor_tensor(k1[:], f1[:], 0.02, d1[:], ALU.mult, ALU.add), [bf1, bd1], [bk1])
                A("vector", lambda e: e.tensor_scalar(k1[:], k1[:], 0.0, None, ALU.is_ge), [bk1], [bk1])
                A("scalar", lambda e: e.activation(g1[:], d1[:], AF.Exp), [bd1], [bg1])
                A("vector", lambda e: e.tensor_mul(g1[:], g1[:], k1[:]), [bg1, bk1], [bg1])
                A("vector", lambda e: e.reduce_sum(ssum[:, 0:1], g1[:], axis=AX.X), [bg1, b_ssum], [b_ssum])
                A("vector", lambda e: e.tensor_scalar(oh[:], slg[:], m1[:], None, ALU.is_ge), [bslg, bm1], [boh])
                A("vector", lambda e: e.scalar_tensor_tensor(msk[:], oh[:], -BIG, slg[:], ALU.mult, ALU.add), [boh, bslg], [bmsk])
                A("vector", lambda e: e.reduce_max(m2[:], msk[:], axis=AX.X), [bmsk], [bm2])
                A("vector", _maxb(f1[:], as_[:], m2[:]), [bas, bm2], [bf1])
                A("vector", lambda e: e.tensor_scalar(d1[:], slg[:], m2[:], None, ALU.subtract), [bslg, bm2], [bd1])
                A("vector", lambda e: e.scalar_tensor_tensor(k1[:], f1[:], 0.02, d1[:], ALU.mult, ALU.add), [bf1, bd1], [bk1])
                A("vector", lambda e: e.tensor_scalar(k1[:], k1[:], 0.0, None, ALU.is_ge), [bk1], [bk1])
                A("vector", lambda e: e.tensor_scalar(d1[:], msk[:], m2[:], None, ALU.subtract), [bmsk, bm2], [bd1])
                A("scalar", lambda e: e.activation(g1[:], d1[:], AF.Exp), [bd1], [bg1])
                A("vector", lambda e: e.tensor_mul(g1[:], g1[:], k1[:]), [bg1, bk1], [bg1])
                A("vector", lambda e: e.reduce_sum(ssum[:, 1:2], g1[:], axis=AX.X), [bg1, b_ssum], [b_ssum])
                A("vector", lambda e: e.tensor_scalar(msk[:], msk[:], m2[:], None, ALU.is_ge), [bmsk, bm2], [bmsk])
                A("vector", lambda e: e.reciprocal(srs[:], ssum[:]), [b_ssum], [b_srs])
                A("vector", lambda e: e.tensor_scalar(oh[:], oh[:], srs[:, 0:1], None, ALU.mult), [boh, b_srs], [boh])
                A("vector", lambda e: e.scalar_tensor_tensor(slg[:], msk[:], srs[:, 1:2], oh[:], ALU.mult, ALU.add), [bmsk, b_srs, boh], [bslg])
                p.add("tensor", lambda e: e.transpose(pU[0:8, 0:128], slg[:], idf32[:]),
                      reads=[bslg, b_idf32], writes=[b_pU])
                p.add("vector", lambda e, tsl=tsl: e.tensor_copy(sdwT[:, tsl], pU[0:8, 0:128]),
                      reads=[b_pU], writes=[b_sdwT])
            if debug:
                p.dma("gpsimd", dDW, sdwT[:], reads=[b_sdwT])
                p.dma("sync", dLN, lnx[:, 0:TOK], reads=list(b_lnx4))

            from concourse.bass import broadcast_tensor_aps as _bcast
            swg = [fctx.enter_context(nc.sbuf_tensor(f"f_swg{i}", [128, 8 * QC], BF16))
                   for i in range(2)]
            swu = [fctx.enter_context(nc.sbuf_tensor(f"f_swu{i}", [128, 8 * QC], BF16))
                   for i in range(2)]
            b_swg = [p.buf(), p.buf()]
            b_swu = [p.buf(), p.buf()]
            swd = fctx.enter_context(nc.sbuf_tensor("f_swd", [128, 16 * 1024], BF16))
            b_swd = p.buf()
            ssg = fctx.enter_context(nc.sbuf_tensor("f_ssg", [128, 2 * QC], BF16))
            b_ssg = p.buf()
            sdwB = fctx.enter_context(nc.sbuf_tensor("f_sdwB", [128, 1024], BF16))
            b_sdwB = p.buf()
            scc = fctx.enter_context(nc.sbuf_tensor("f_scc", [128, 2 * QC], BF16))
            b_scc = p.buf()
            sxv = sx[:].rearrange("p (c t) -> p c t", c=8)
            shv = sh[:].rearrange("p (f t) -> p f t", f=16)

            for e_i in range(8):
                for th in range(2):
                    for fb in range(4):
                        par = (e_i * 8 + th * 4 + fb) % 2
                        for ic in range(8):
                            p.dma("gpsimd", swg[par][:, ic * QC:(ic + 1) * QC],
                                  wgT[e_i * D + ic * 128: e_i * D + (ic + 1) * 128,
                                      fb * QC:(fb + 1) * QC], writes=[b_swg[par]],
                                  deps=(lndone if e_i == 0 and th == 0 and fb == 0
                                        and ic == 0 else ()))
                            p.dma("gpsimd", swu[par][:, ic * QC:(ic + 1) * QC],
                                  wuT[e_i * D + ic * 128: e_i * D + (ic + 1) * 128,
                                      fb * QC:(fb + 1) * QC], writes=[b_swu[par]])
                        if th == 0 and fb == 3:
                            for fc in range(16):
                                p.dma("gpsimd", swd[:, fc * 1024:(fc + 1) * 1024],
                                      wdT[e_i * 2048 + fc * 128: e_i * 2048 + (fc + 1) * 128, :],
                                      writes=[b_swd])
                        for tc in range(2):
                            tsl = slice(th * 1024 + tc * QC, th * 1024 + (tc + 1) * QC)
                            for half in range(2):
                                gof = half * 2
                                ut, but = (pU, b_pU) if half == 0 else (pBC, b_pBC)
                                b_pgh = [b_pG, b_pG2][half]
                                for ft in range(2):
                                    fa = half * 2 + ft
                                    for ic in range(8):
                                        st = ic == 0
                                        sp_ = ic == 7
                                        p.add("tensor", lambda e, fa=fa, gof=gof, ft=ft, ic=ic, tsl=tsl, st=st, sp_=sp_, par=par: e.matmul(
                                            pG[:, (gof + ft) * QC:(gof + ft + 1) * QC],
                                            swg[par][:, ic * QC + fa * 128: ic * QC + (fa + 1) * 128],
                                            lnx[:, ic * TOK:][:, tsl], start=st, stop=sp_),
                                            reads=[b_swg[par], b_lnx4[th * 2 + tc]],
                                            writes=([b_pgh] if st or sp_ else []))
                                        p.add("tensor", lambda e, fa=fa, ic=ic, ft=ft, tsl=tsl, st=st, sp_=sp_, ut=ut, but=but, par=par: e.matmul(
                                            ut[:, ft * QC:(ft + 1) * QC],
                                            swu[par][:, ic * QC + fa * 128: ic * QC + (fa + 1) * 128],
                                            lnx[:, ic * TOK:][:, tsl], start=st, stop=sp_),
                                            reads=[b_swu[par], b_lnx4[th * 2 + tc]],
                                            writes=([but] if st or sp_ else []))
                                p.add("scalar", lambda e, gof=gof, half=half: e.activation(
                                    ssg[:], pG[:, gof * QC:(gof + 2) * QC], AF.Silu),
                                    reads=[b_pgh], writes=[b_ssg])
                                p.add("vector", lambda e, fb=fb, tc=tc, half=half, ut=ut: e.tensor_mul(
                                    shv[:, fb * 4 + half * 2: fb * 4 + half * 2 + 2,
                                        tc * QC:(tc + 1) * QC],
                                    ssg[:].rearrange("p (f t) -> p f t", f=2),
                                    ut[:, 0:2 * QC].rearrange("p (f t) -> p f t", f=2)),
                                    reads=[b_ssg, but], writes=[b_sh])
                    if debug and e_i == 0 and th == 0:
                        p.dma("sync", dH, sh[:, 0:1024], reads=[b_sh])
                    for j in range(2):
                        p.add("tensor", lambda e, e_i=e_i, th=th, j=j: e.matmul(
                            pU[:, 0:QC], sel8[:, e_i * 128:(e_i + 1) * 128],
                            sdwT[0:8, th * 1024 + j * QC: th * 1024 + (j + 1) * QC],
                            start=True, stop=True),
                            reads=[b_sdwT, b_sel8], writes=[b_pU])
                        p.add("vector", lambda e, j=j: e.tensor_copy(
                            sdwB[:, j * QC:(j + 1) * QC], pU[:, 0:QC]),
                            reads=[b_pU], writes=[b_sdwB])
                    dbanks = [(pG, b_pG, 0), (pG, b_pG, 1), (pG, b_pG2, 2), (pG, b_pG2, 3),
                              (pU, b_pU, 0), (pU, b_pU, 1), (pBC, b_pBC, 0), (pBC, b_pBC, 1)]
                    for t8 in range(2):
                        t0 = t8 * QC
                        tg = th * 1024 + t0
                        a2w = sdwB[:, t0:t0 + QC].rearrange("p (a t) -> p a t", a=1)
                        a0 = scc[:].rearrange("p (c t) -> p c t", c=2)

                        def scale_add(src2, b_src, cbase, tg=tg, a0=a0, a2w=a2w):
                            _, a2 = _bcast(src2, a2w)
                            p.add("vector", lambda e, a0=a0, src2=src2, a2=a2: e.tensor_tensor(
                                a0, src2, a2, ALU.mult),
                                reads=[b_src, b_sdwB], writes=[b_scc])
                            p.add("vector", lambda e, a0=a0, cbase=cbase, tg=tg: e.tensor_add(
                                sxv[:, cbase:cbase + 2, tg:tg + QC],
                                sxv[:, cbase:cbase + 2, tg:tg + QC], a0),
                                reads=[b_scc], writes=[b_sx])

                        for dpass in range(2):
                            for fc in range(16):
                                st = fc == 0
                                sp_ = fc == 15
                                for dt_ in range(dpass * 4, dpass * 4 + 4):
                                    yt, b_yt, bi = dbanks[dt_]
                                    p.add("tensor", lambda e, fc=fc, dt_=dt_, t0=t0, st=st, sp_=sp_, yt=yt, bi=bi: e.matmul(
                                        yt[:, bi * QC:(bi + 1) * QC],
                                        swd[:, fc * 1024 + dt_ * 128: fc * 1024 + (dt_ + 1) * 128],
                                        shv[:, fc, t0:t0 + QC], start=st, stop=sp_),
                                        reads=[b_swd, b_sh],
                                        writes=([b_yt] if st or sp_ else []))
                            if dpass == 0:
                                scale_add(pG[:, 0:2 * QC].rearrange("p (c t) -> p c t", c=2), b_pG, 0)
                                if debug and e_i == 0 and th == 0 and t8 == 0:
                                    p.dma("gpsimd", dSB, sdwB[:, 0:256], reads=[b_sdwB])
                                    p.dma("gpsimd", dSC, scc[:, 0:1024], reads=[b_scc])
                                scale_add(pG[:, 2 * QC:4 * QC].rearrange("p (c t) -> p c t", c=2), b_pG2, 2)
                            else:
                                scale_add(pU[:, 0:2 * QC].rearrange("p (c t) -> p c t", c=2), b_pU, 4)
                                scale_add(pBC[:, 0:2 * QC].rearrange("p (c t) -> p c t", c=2), b_pBC, 6)

            for ic in range(8):
                p.dma("sync", oX4[ic * 128:(ic + 1) * 128, :],
                      sx[:, ic * TOK:(ic + 1) * TOK], reads=[b_sx])

        phase_F()
        p.emit(ctx)
    return nc



def _perm_deinterleave():
    perm = np.empty(D, np.int64)
    for h in range(H):
        for i in range(32):
            perm[h * 64 + i] = h * 64 + 2 * i
            perm[h * 64 + 32 + i] = h * 64 + 2 * i + 1
    return perm


def host_prep(inp):
    d = {}
    x = np.asarray(inp["L0_attn_hidden_states"])
    d["xfm"] = np.ascontiguousarray(x.transpose(0, 2, 1))
    perm = _perm_deinterleave()
    d["wq0"] = np.ascontiguousarray(np.asarray(inp["L0_attn_Wq"])[:, perm])
    d["wk0"] = np.ascontiguousarray(np.asarray(inp["L0_attn_Wk"])[:, perm])
    d["wv0"] = np.ascontiguousarray(np.asarray(inp["L0_attn_Wv"]))
    d["bqkv0"] = np.ascontiguousarray(np.stack([
        np.asarray(inp["L0_attn_bq"])[perm],
        np.asarray(inp["L0_attn_bk"])[perm],
        np.asarray(inp["L0_attn_bv"])]).reshape(1, 3 * D))
    d["wfcT"] = np.ascontiguousarray(np.asarray(inp["L0_ffn_W_fc"]).T)
    d["wprojT"] = np.ascontiguousarray(np.asarray(inp["L0_ffn_W_proj"]).T)
    d["lnw"] = np.asarray(inp["L0_ffn_ln_w"]).reshape(1, -1)
    d["lnb"] = np.asarray(inp["L0_ffn_ln_b"]).reshape(1, -1)
    d["wq1T"] = np.ascontiguousarray(np.asarray(inp["L1_attn_Wq"]).T)
    d["wk1T"] = np.ascontiguousarray(np.asarray(inp["L1_attn_Wk"]).T)
    d["wv1T"] = np.ascontiguousarray(np.asarray(inp["L1_attn_Wv"]).T)
    d["wo1T"] = np.ascontiguousarray(np.asarray(inp["L1_attn_Wo"]).T)
    d["b1"] = np.ascontiguousarray(np.stack([
        np.asarray(inp["L1_attn_bq"]), np.asarray(inp["L1_attn_bk"]),
        np.asarray(inp["L1_attn_bv"]), np.asarray(inp["L1_attn_bo"])]))
    d["rwT"] = np.ascontiguousarray(np.asarray(inp["L1_ffn_router_w"]).T)
    d["wgT"] = np.ascontiguousarray(np.asarray(inp["L1_ffn_w_gate"]).transpose(0, 2, 1))
    d["wuT"] = np.ascontiguousarray(np.asarray(inp["L1_ffn_w_up"]).transpose(0, 2, 1))
    d["wdT"] = np.ascontiguousarray(np.asarray(inp["L1_ffn_w_down"]).transpose(0, 2, 1))
    return d


_NC_CACHE = {}


def kernel(**inputs):
    debug = bool(inputs.pop("_debug", False))
    d = host_prep(inputs)
    key = ("fused", debug)
    if key not in _NC_CACHE:
        _NC_CACHE[key] = build_fused(debug=debug)
    nc = _NC_CACHE[key]
    in_maps = []
    for c in range(NC):
        b, q = c // 4, c % 4
        in_maps.append(dict(
            xin=np.ascontiguousarray(d["xfm"][b][:, q * TOK:(q + 1) * TOK]),
            pos0=np.array([[q * TOK]], dtype=np.float32),
            wq0=d["wq0"], wk0=d["wk0"], wv0=d["wv0"], bqkv=d["bqkv0"],
            wfcT=d["wfcT"], wprojT=d["wprojT"], lnw=d["lnw"], lnb=d["lnb"],
            wq1=d["wq1T"], wk1=d["wk1T"], wv1=d["wv1T"], wo1=d["wo1T"],
            b1=d["b1"].reshape(1, 4 * D),
            rwT=d["rwT"],
            wgT=d["wgT"].reshape(8 * D, 2048),
            wuT=d["wuT"].reshape(8 * D, 2048),
            wdT=d["wdT"].reshape(8 * 2048, D),
        ))
    res = run_bass_kernel_spmd(nc, in_maps, core_ids=list(range(NC)))
    out = np.empty((BN, SEQ, D), np.float32)
    for c in range(NC):
        b, q = c // 4, c % 4
        out[b, q * TOK:(q + 1) * TOK, :] = np.asarray(res.results[c]["oX4"]).T
    if debug:
        dbg = {k: [np.asarray(res.results[c][k]) for c in range(NC)]
               for k in ("dX0", "dX1", "dX2", "dX3", "dSM", "dBC", "dL",
                         "dQ", "dK", "dV", "dDW", "dLN", "dH", "dSB", "dSC", "dCH")}
        return out, dbg
    return out
